# version 14
# baseline (speedup 1.0000x reference)
import math
import numpy as np
import ml_dtypes
from contextlib import ExitStack

import concourse.bass as bass
import concourse.mybir as mybir
from concourse.bass_utils import run_bass_kernel_spmd

F32 = mybir.dt.float32
BF16 = mybir.dt.bfloat16
I32 = mybir.dt.int32
AF = mybir.ActivationFunctionType
ALU = mybir.AluOpType
AX = mybir.AxisListType

NCORE = 8
D = 2048
S = 8192
H = 16
DH = 128
HID = 8192
TPC = S // NCORE
NTT = TPC // 128
NDC = D // 128
NHC = HID // 128
NG = 8
HPC = H // NCORE
RK = 16
NA = 4
INVSQ = 1.0 / math.sqrt(DH)
BF = ml_dtypes.bfloat16


class Eng:

    def __init__(self, nc, ctx, name, eng):
        self.nc = nc
        self.e = eng
        self.sem = ctx.enter_context(nc.semaphore(f"sem_{name}"))
        self.n = 0
        self.last = None

    def put(self, instr):
        self.last = instr
        return instr

    def sig(self, k=1):
        self.last.then_inc(self.sem, k)
        self.n += k
        return self.n

    def wait(self, other, count):
        if count > 0:
            self.e.wait_ge(other.sem, count)


class Stream:

    def __init__(self, nc, ctx, name):
        self.sem = ctx.enter_context(nc.semaphore(f"st_{name}"))
        self.n = 0

    def dma(self, qeng, out, in_, **kw):
        qeng.e.dma_start(out, in_, **kw).then_inc(self.sem, 16)
        self.n += 16
        return self.n

    def waited_by(self, eng, count=None):
        eng.e.wait_ge(self.sem, self.n if count is None else count)


def mk_engs(nc, ctx):
    return (Eng(nc, ctx, "pe", nc.tensor), Eng(nc, ctx, "act", nc.scalar),
            Eng(nc, ctx, "dve", nc.vector), Eng(nc, ctx, "sp", nc.sync),
            Eng(nc, ctx, "gp", nc.gpsimd))


def emit_layernorm(nc, E, resid, xt, scr, stats, eps_t, ident, banks, lnout,
                   resid_dve_ready):
    if not isinstance(resid_dve_ready, (list, tuple)):
        resid_dve_ready = [resid_dve_ready] * NTT
    """LayerNorm of the NTT resident token tiles into transposed bf16 xt.

    resid [128, NTT*D] f32 -> xt [128, NDC*TPC] bf16 (X^T, feature-major).
    banks: two f32 [128,512] psum banks (bitcast bf16) for PE transposes.
    Returns act count at which xt is fully written.
    """
    pe, act, dve, sp, gp = E
    sums = stats[:, 0:NTT]
    sumsq = stats[:, NTT:2 * NTT]
    s2 = stats[:, 2 * NTT:3 * NTT]
    vpe = stats[:, 3 * NTT:4 * NTT]
    sq = stats[:, 4 * NTT:5 * NTT]
    rstd = stats[:, 5 * NTT:6 * NTT]
    nmr = stats[:, 6 * NTT:7 * NTT]

    for tt in range(NTT):
        act.wait(dve, resid_dve_ready[tt])
        act.put(nc.scalar.activation(scr[:, :], resid[:, tt * D:(tt + 1) * D],
                                     AF.Square, accum_out=sumsq[:, tt:tt + 1]))
        nc.scalar.drain()
    for tt in range(NTT):
        dve.wait(dve, resid_dve_ready[tt])
        dve.put(nc.vector.reduce_sum(sums[:, tt:tt + 1],
                                     resid[:, tt * D:(tt + 1) * D], axis=AX.X))
        nc.vector.drain()
    d_sums = dve.sig()
    act.wait(dve, d_sums)
    act.put(nc.scalar.activation(s2, sums, AF.Square, scale=1.0 / D))
    a_stats = act.sig()
    dve.wait(act, a_stats)
    dve.put(nc.vector.scalar_tensor_tensor(vpe, sumsq, 1.0 / D, s2, ALU.mult,
                                           ALU.subtract))
    d_vpe = dve.sig()
    act.wait(dve, d_vpe)
    act.put(nc.scalar.activation(sq, vpe, AF.Sqrt, bias=eps_t[:, 0:1]))
    a_sq = act.sig()
    dve.wait(act, a_sq)
    dve.put(nc.vector.reciprocal(rstd, sq))
    nc.vector.drain()
    dve.put(nc.vector.scalar_tensor_tensor(nmr, sums, -1.0 / D, rstd, ALU.mult,
                                           ALU.mult))
    nc.vector.drain()
    dve.sig()

    slot_evac = [0, 0]
    xt_tt_counts = [0] * NTT
    for tt in range(NTT):
        lo = lnout[tt % 2]
        dve.wait(pe, pe.n)
        dve.put(nc.vector.tensor_scalar(lo[:, :], resid[:, tt * D:(tt + 1) * D],
                                        rstd[:, tt:tt + 1], nmr[:, tt:tt + 1],
                                        ALU.mult, ALU.add))
        d_ln = dve.sig()
        for dc in range(NDC):
            sl = dc % 2
            ps = banks[sl].bitcast(BF16)[:, 0:128]
            pe.wait(act, slot_evac[sl])
            pe.wait(dve, d_ln)
            pe.put(nc.tensor.transpose(ps, lo[:, dc * 128:(dc + 1) * 128], ident))
            p_tp = pe.sig()
            act.wait(pe, p_tp)
            act.put(nc.scalar.copy(
                xt[:, dc * TPC + tt * 128: dc * TPC + (tt + 1) * 128], ps))
            slot_evac[sl] = act.sig()
        xt_tt_counts[tt] = act.n
    return xt_tt_counts


def emit_ffn(nc, E, ST, xt, resid, gated, wg_sb, wu_sb, wd_sb, gps, ups, dps,
             wst, xt_act_ready, store_cb=None):
    pe, act, dve, sp, gp = E
    NGRP = NHC // NG
    gate_done = [0, 0]
    down_done = [0] * 4
    wbuf_free = [0, 0]
    wd_free = [0] * NG
    swg = [ST["wg0"], ST["wg1"]]
    swu = [ST["wu0"], ST["wu1"]]
    swd = [ST[f"wd{i}"] for i in range(NG)]
    if isinstance(xt_act_ready, (list, tuple)):
        xt_tc = [xt_act_ready[3], xt_act_ready[7]]
    else:
        xt_tc = [xt_act_ready, xt_act_ready]
    waited_tc = [False, False]
    for g in range(NGRP):
        for hc in range(NG):
            hcg = g * NG + hc
            b = hc % 2
            gp.e.wait_ge(pe.sem, wbuf_free[b])
            swg[b].dma(gp, wg_sb[b][:, :], wst["wg"][hcg], max_dma_last_dim=8192)
            swu[b].dma(gp, wu_sb[b][:, :], wst["wu"][hcg], max_dma_last_dim=8192)
            gp.e.wait_ge(pe.sem, wd_free[hc])
            swd[hc].dma(gp, wd_sb[hc][:, :], wst["wd"][hcg],
                        max_dma_last_dim=8192)
            for tc in range(2):
                sl = tc
                if tc == 0:
                    swg[b].waited_by(pe)
                    swu[b].waited_by(pe)
                if not waited_tc[tc]:
                    pe.wait(act, xt_tc[tc])
                    waited_tc[tc] = True
                pe.wait(dve, gate_done[sl])
                for dc in range(NDC):
                    nc.tensor.matmul(
                        gps[sl][:, :], wg_sb[b][:, dc * 128:(dc + 1) * 128],
                        xt[:, dc * TPC + tc * 512: dc * TPC + tc * 512 + 512],
                        start=(dc == 0), stop=(dc == NDC - 1))
                last = None
                for dc in range(NDC):
                    last = nc.tensor.matmul(
                        ups[sl][:, :], wu_sb[b][:, dc * 128:(dc + 1) * 128],
                        xt[:, dc * TPC + tc * 512: dc * TPC + tc * 512 + 512],
                        start=(dc == 0), stop=(dc == NDC - 1))
                pe.put(last)
                p_mm = pe.sig()
                act.wait(pe, p_mm)
                gsl = gated[:, hc * TPC + tc * 512: hc * TPC + tc * 512 + 512]
                act.put(nc.scalar.activation(gsl, gps[sl][:, :], AF.Silu))
                a_silu = act.sig()
                dve.wait(act, a_silu)
                dve.put(nc.vector.scalar_tensor_tensor(gsl, gsl, 0.5, ups[sl][:, :],
                                                       ALU.mult, ALU.mult))
                gate_done[sl] = dve.sig()
            wbuf_free[b] = pe.n
        d_gate = dve.n
        pe.wait(dve, d_gate)
        if not waited_tc[1]:
            pe.wait(act, xt_tc[1])
            waited_tc[1] = True
        for hc in range(NG):
            swd[hc].waited_by(pe)
        for ts in range(NTT):
            for oc in range(4):
                pe.wait(dve, down_done[oc])
                last = None
                for hc in range(NG):
                    last = nc.tensor.matmul(
                        dps[oc][:, :],
                        gated[:, hc * TPC + ts * 128: hc * TPC + (ts + 1) * 128],
                        wd_sb[hc][:, oc * 512:(oc + 1) * 512],
                        start=(hc == 0), stop=(hc == NG - 1))
                pe.put(last)
                p_dn = pe.sig()
                dve.wait(pe, p_dn)
                rsl = resid[:, ts * D + oc * 512: ts * D + oc * 512 + 512]
                dve.put(nc.vector.tensor_tensor(rsl, rsl, dps[oc][:, :], ALU.add))
                down_done[oc] = dve.sig()
            if store_cb is not None and g == NGRP - 1:
                store_cb(ts, dve.n)
        for hc in range(NG):
            wd_free[hc] = pe.n
    return dve.n


def emit_onehot(nc, E, starts_sb, iota_sb, ohb, ones1, oh_ps, s_ld, s_ss):
    pe, act, dve, sp, gp = E
    s_ld.waited_by(dve)
    s_ss.waited_by(dve)
    cnt = starts_sb[0:1, 4:5]
    le = starts_sb[0:1, 12:16]
    ohr = starts_sb[0:1, 8:12]
    dve.put(nc.vector.tensor_single_scalar(le, starts_sb[0:1, 0:4], 0.0, ALU.is_le))
    nc.vector.drain()
    dve.put(nc.vector.reduce_sum(cnt, le, axis=AX.X))
    nc.vector.drain()
    dve.put(nc.vector.tensor_scalar_add(cnt, cnt, -1.0))
    nc.vector.drain()
    dve.put(nc.vector.tensor_relu(cnt, cnt))
    nc.vector.drain()
    dve.put(nc.vector.tensor_scalar(ohr, iota_sb[0:1, 0:4], cnt[0:1, 0:1], None,
                                    ALU.subtract, ALU.bypass))
    nc.vector.drain()
    dve.put(nc.vector.tensor_tensor(ohr, ohr, ohr, ALU.mult))
    nc.vector.drain()
    dve.put(nc.vector.tensor_scalar(ohr, ohr, -1.0, 1.0, ALU.mult, ALU.add))
    nc.vector.drain()
    dve.put(nc.vector.tensor_relu(ohr, ohr))
    d_oh = dve.sig()
    pe.wait(dve, d_oh)
    pe.put(nc.tensor.matmul(oh_ps[:, 0:4], ones1[0:1, :], ohr, start=True,
                            stop=True))
    p_oh = pe.sig()
    dve.wait(pe, p_oh)
    dve.put(nc.vector.tensor_copy(ohb[:, 0:4], oh_ps[:, 0:4]))
    nc.vector.drain()
    return dve.sig()


def emit_ab_select(nc, E, ohb, a_pairs, b_pairs):
    pe, act, dve, sp, gp = E
    for a_sb, asl in a_pairs:
        for a in range(NA):
            src = a_sb.rearrange("p (c a r) -> p c a r", a=NA, r=RK)[:, :, a, :]
            if a == 0:
                dve.put(nc.vector.tensor_scalar_mul(asl, src, ohb[:, 0:1]))
            else:
                dve.put(nc.vector.scalar_tensor_tensor(asl, src, ohb[:, a:a + 1],
                                                       asl, ALU.mult, ALU.add))
            nc.vector.drain()
    a_done = dve.sig()
    for b_all, bsel in b_pairs:
        for a in range(NA):
            src = b_all[:, a * D:(a + 1) * D]
            if a == 0:
                dve.put(nc.vector.tensor_scalar_mul(bsel[:, :], src,
                                                    ohb[0:16, 0:1]))
            else:
                dve.put(nc.vector.scalar_tensor_tensor(bsel[:, :], src,
                                                       ohb[0:16, a:a + 1],
                                                       bsel[:, :], ALU.mult,
                                                       ALU.add))
            nc.vector.drain()
    return a_done, dve.sig()


def build_l1():
    nc = bass.Bass("TRN2", target_bir_lowering=False)
    x_in = nc.dram_tensor("x", [NTT, 128, D], F32, kind="ExternalInput").ap()
    wg_d = nc.dram_tensor("wg", [NHC, 128, D], F32, kind="ExternalInput").ap()
    wu_d = nc.dram_tensor("wu", [NHC, 128, D], F32, kind="ExternalInput").ap()
    wd_d = nc.dram_tensor("wd", [NHC, 128, D], F32, kind="ExternalInput").ap()
    wq_d = nc.dram_tensor("wq", [NDC, 4, 128, 512], F32, kind="ExternalInput").ap()
    wk_d = nc.dram_tensor("wk", [NDC, 4, 128, 512], F32, kind="ExternalInput").ap()
    wv_d = nc.dram_tensor("wv", [NDC, 4, 128, 512], F32, kind="ExternalInput").ap()
    aq_d = nc.dram_tensor("aq", [NDC, 128, NA, RK], F32, kind="ExternalInput").ap()
    ak_d = nc.dram_tensor("ak", [NDC, 128, NA, RK], F32, kind="ExternalInput").ap()
    av_d = nc.dram_tensor("av", [NDC, 128, NA, RK], F32, kind="ExternalInput").ap()
    bq_d = nc.dram_tensor("bq", [NA, RK, D], F32, kind="ExternalInput").ap()
    bk_d = nc.dram_tensor("bk", [NA, RK, D], F32, kind="ExternalInput").ap()
    bv_d = nc.dram_tensor("bv", [NA, RK, D], F32, kind="ExternalInput").ap()
    cos_d = nc.dram_tensor("cosr", [NTT, 128, 256], F32, kind="ExternalInput").ap()
    sin_d = nc.dram_tensor("sinr", [NTT, 128, 256], F32, kind="ExternalInput").ap()
    st_d = nc.dram_tensor("starts", [1, 4], I32, kind="ExternalInput").ap()
    io_d = nc.dram_tensor("iota", [1, 4], F32, kind="ExternalInput").ap()
    id_d = nc.dram_tensor("ident", [128, 128], BF16, kind="ExternalInput").ap()
    idf_d = nc.dram_tensor("identf", [128, 128], F32, kind="ExternalInput").ap()

    x1_o = nc.dram_tensor("x1", [NTT, 128, D], F32, kind="ExternalOutput").ap()
    q_o = nc.dram_tensor("qo", [H, TPC, DH], BF16, kind="ExternalOutput").ap()
    k_o = nc.dram_tensor("ko", [H, TPC, DH], BF16, kind="ExternalOutput").ap()
    v_o = nc.dram_tensor("vo", [H, TPC, DH], BF16, kind="ExternalOutput").ap()
    qn_o = nc.dram_tensor("qn", [H, 1], F32, kind="ExternalOutput").ap()
    kn_o = nc.dram_tensor("kn", [H, 1], F32, kind="ExternalOutput").ap()

    with ExitStack() as ctx:
        SB = lambda nm, sh, dt: ctx.enter_context(nc.sbuf_tensor("sb_" + nm, sh, dt)).ap()
        resid = SB("resid", [128, NTT * D], F32)
        xt = SB("xt", [128, NDC * TPC], BF16)
        scr = SB("scr", [128, D], BF16)
        lnout = [SB(f"lnout{b}", [128, D], BF16) for b in range(2)]
        stats = SB("stats", [128, 8 * NTT], F32)
        eps_t = SB("eps", [128, 1], F32)
        ident = SB("ident", [128, 128], BF16)
        identf = SB("identf", [128, 128], F32)
        starts_sb = SB("starts", [1, 16], F32)
        iota_sb = SB("iotas", [1, 4], F32)
        ohb = SB("ohb", [128, 8], F32)
        ones1 = SB("ones1", [1, 128], F32)
        asel = SB("asel", [128, NDC * 3 * RK], BF16)
        bsel = [SB(f"bsel{i}", [16, D], BF16) for i in range(3)]
        low = SB("low", [16, 3 * TPC], BF16)
        nrm = SB("nrm", [128, 48], F32)
        nrmf = SB("nrmf", [128, 128], F32)
        banks = [ctx.enter_context(nc.psum_tensor(f"bank{i}", [128, 512], F32)).ap()
                 for i in range(8)]

        E = mk_engs(nc, ctx)
        pe, act, dve, sp, gp = E

        MS = lambda nm: Stream(nc, ctx, nm)
        s_ld, s_ss, s_sa, s_bt, s_so = MS("ld"), MS("ss"), MS("sa"), MS("bt"), MS("so")
        s_q = [MS("q0"), MS("q1")]
        s_wc = MS("wc")
        ST = {k: MS(k) for k in
              ["wg0", "wg1", "wu0", "wu1"] + [f"wd{i}" for i in range(NG)]}
        for tt in range(NTT):
            s_ld.dma(sp, resid[:, tt * D:(tt + 1) * D], x_in[tt])
        s_ld.dma(sp, ident, id_d)
        s_ld.dma(sp, identf, idf_d)
        s_ld.dma(sp, iota_sb, io_d)
        s_ss.dma(gp, starts_sb[0:1, 0:4], st_d)
        dve.put(nc.vector.memset(eps_t, 1e-5))
        dve.put(nc.vector.memset(ones1, 1.0))
        dve.put(nc.vector.memset(nrm, 0.0))
        dve.put(nc.vector.memset(nrmf, 0.0))
        dve.sig()
        s_ld.waited_by(act)
        s_ld.waited_by(dve)

        xt_tts = emit_layernorm(nc, E, resid, xt, scr, stats, eps_t, ident,
                                banks[0:2], lnout, 0)
        xt_ready = xt_tts[-1]

        with ExitStack() as c3:
            gated = c3.enter_context(nc.sbuf_tensor("gated", [128, NG * TPC],
                                                    BF16)).ap()
            wg_sb = [c3.enter_context(nc.sbuf_tensor(f"wg{b}", [128, D], BF16)).ap()
                     for b in range(2)]
            wu_sb = [c3.enter_context(nc.sbuf_tensor(f"wu{b}", [128, D], BF16)).ap()
                     for b in range(2)]
            wd_sb = [c3.enter_context(nc.sbuf_tensor(f"wd{b}", [128, D], BF16)).ap()
                     for b in range(NG)]
            resid_ts_counts = [0] * NTT

            def _store_x1(ts, dcount):
                resid_ts_counts[ts] = dcount
                sp.wait(dve, dcount)
                s_so.dma(sp, x1_o[ts], resid[:, ts * D:(ts + 1) * D])
            ffn_done = emit_ffn(nc, E, ST, xt, resid, gated, wg_sb, wu_sb, wd_sb,
                                [banks[4], banks[5]], [banks[6], banks[7]],
                                banks[0:4], {"wg": wg_d, "wu": wu_d, "wd": wd_d},
                                xt_tts, store_cb=_store_x1)
            gp.e.wait_ge(dve.sem, ffn_done)

        xt2_tts = emit_layernorm(nc, E, resid, xt, scr, stats, eps_t, ident,
                                 banks[0:2], lnout, resid_ts_counts)
        xt2_ready = xt2_tts[-1]

        with ExitStack() as c2:
            a_sb = [c2.enter_context(nc.sbuf_tensor(f"asb{i}", [128, NDC * NA * RK],
                                                    BF16)).ap() for i in range(3)]
            b_all = [c2.enter_context(nc.sbuf_tensor(f"ball{i}", [16, NA * D],
                                                     BF16)).ap() for i in range(3)]
            for i, ad in enumerate([aq_d, ak_d, av_d]):
                s_sa.dma(gp, a_sb[i].rearrange("p (f) -> p f"),
                         ad.transpose([1, 0, 2, 3]))
            for i, bd in enumerate([bq_d, bk_d, bv_d]):
                s_sa.dma(gp, b_all[i].rearrange("p (a d) -> p a d", d=D),
                         bd.transpose([1, 0, 2]))
            emit_onehot(nc, E, starts_sb, iota_sb, ohb, ones1, banks[7], s_ld,
                        s_ss)
            s_sa.waited_by(dve)
            a_pairs = []
            for i in range(3):
                sl = asel.rearrange("p (c t r) -> p c t r", t=3, r=RK)[:, :, i, :]
                a_pairs.append((a_sb[i], sl))
            b_pairs = [(b_all[i], bsel[i]) for i in range(3)]
            a_done, sel_done = emit_ab_select(nc, E, ohb, a_pairs, b_pairs)
            gp.e.wait_ge(dve.sem, sel_done)

        pe.wait(dve, a_done)
        b_waited = False
        lo_ps = banks[2][0:16, :]
        for i in range(3):
            for tc in range(2):
                pe.wait(act, act.n)
                pe.wait(act, xt2_tts[4 * tc + 3])
                last = None
                for dc in range(NDC):
                    lh = asel.rearrange("p (c t r) -> p c t r", t=3, r=RK)[:, dc, i, :]
                    last = nc.tensor.matmul(
                        lo_ps, lh,
                        xt[:, dc * TPC + tc * 512: dc * TPC + tc * 512 + 512],
                        start=(dc == 0), stop=(dc == NDC - 1))
                pe.put(last)
                p_lo = pe.sig()
                act.wait(pe, p_lo)
                act.put(nc.scalar.copy(
                    low[0:16, i * TPC + tc * 512: i * TPC + tc * 512 + 512], lo_ps))
                act.sig()
        low_ready = act.n
        pe.wait(act, low_ready)

        with ExitStack() as c4:
            SB4 = lambda nm, sh, dt: c4.enter_context(nc.sbuf_tensor("sb_" + nm, sh, dt)).ap()
            wcol2 = [SB4(f"wcol{b}", [128, NDC * 512], BF16) for b in range(2)]
            cos_sb = SB4("cos", [128, NTT * 256], F32)
            sin_sb = SB4("sin", [128, NTT * 256], F32)
            ro_t = [[SB4(f"rot{s}_{i}", [128, 256], F32) for i in range(4)]
                    for s in range(2)]
            qkst = [SB4(f"qkst{b}", [128, 512], BF16) for b in range(2)]
            sqt = SB4("sqt", [128, 512], F32)
            sp.wait(dve, sel_done)
            for tt in range(NTT):
                s_ld.dma(sp, cos_sb[:, tt * 256:(tt + 1) * 256], cos_d[tt])
                s_ld.dma(sp, sin_sb[:, tt * 256:(tt + 1) * 256], sin_d[tt])
            s_ld.waited_by(dve)
            qkv_plan = [(wq_d, q_o, True, 0, 0), (wk_d, k_o, True, 1, 16),
                        (wv_d, v_o, False, 2, -1)]
            wcol_free = [0, 0]
            xt2_waited = [False] * NTT
            s_wc2 = [s_wc, Stream(nc, ctx, "wc2")]
            ps_free = [0, 0]
            st_free = [0, 0]
            nrm_done = 0
            blocks = [(w_d, out_d, rope, li, ncol, ow)
                      for w_d, out_d, rope, li, ncol in qkv_plan
                      for ow in range(4)]
            for bi, (w_d, out_d, rope, li, ncol, ow) in enumerate(blocks):
                    wb = bi % 2
                    if bi == 0:
                        gp.e.wait_ge(pe.sem, wcol_free[wb])
                        for dc in range(NDC):
                            s_wc2[wb].dma(gp, wcol2[wb][:, dc * 512:(dc + 1) * 512],
                                          w_d[dc, ow], max_dma_last_dim=8192)
                    if bi + 1 < len(blocks):
                        nw_d, _, _, _, _, now = blocks[bi + 1][0], None, None, None, None, blocks[bi + 1][5]
                        nwb = (bi + 1) % 2
                        gp.e.wait_ge(pe.sem, wcol_free[nwb])
                        for dc in range(NDC):
                            s_wc2[nwb].dma(gp, wcol2[nwb][:, dc * 512:(dc + 1) * 512],
                                           blocks[bi + 1][0][dc, now],
                                           max_dma_last_dim=8192)
                    wcol = wcol2[wb]
                    s_wc2[wb].waited_by(pe)
                    for ts in range(NTT):
                        sl = ts % 2
                        ps = banks[4 + sl]
                        pe.wait(dve, ps_free[sl])
                        if xt2_waited[ts] is False:
                            pe.wait(act, xt2_tts[ts])
                            xt2_waited[ts] = True
                        for dc in range(NDC):
                            nc.tensor.matmul(
                                ps[:, :],
                                xt[:, dc * TPC + ts * 128: dc * TPC + (ts + 1) * 128],
                                wcol[:, dc * 512:(dc + 1) * 512],
                                start=(dc == 0), stop=False)
                        if not b_waited:
                            pe.wait(dve, sel_done)
                            b_waited = True
                        last = nc.tensor.matmul(
                            ps[:, :],
                            low[0:16, li * TPC + ts * 128: li * TPC + (ts + 1) * 128],
                            bsel[li][:, ow * 512:(ow + 1) * 512],
                            start=False, stop=True)
                        pe.put(last)
                        p_mm = pe.sig()
                        if ncol >= 0:
                            act.wait(pe, p_mm)
                            act.wait(dve, nrm_done)
                            act.put(nc.scalar.activation(sqt, ps[:, :], AF.Square))
                            a_sq = act.sig()
                            dve.wait(act, a_sq)
                            red = nrm[:, 40:44]
                            njcol = nrm[:, ncol + ow * 4: ncol + ow * 4 + 4]
                            dve.put(nc.vector.reduce_sum(
                                red, sqt.rearrange("p (h d) -> p h d", d=DH),
                                axis=AX.X))
                            nc.vector.drain()
                            dve.put(nc.vector.tensor_tensor(njcol, njcol, red,
                                                            ALU.max))
                            nc.vector.drain()
                            nrm_done = dve.sig()
                        stg = qkst[sl]
                        dve.wait(pe, p_mm)
                        s_q[sl].waited_by(dve)
                        if rope:
                            cs = cos_sb[:, ts * 256:(ts + 1) * 256]
                            sn = sin_sb[:, ts * 256:(ts + 1) * 256]
                            psv = ps.rearrange("p (r two) -> p r two", two=2)
                            stv = stg.rearrange("p (r two) -> p r two", two=2)
                            x1a, x2a = psv[:, :, 0], psv[:, :, 1]
                            o1a, o2a = stv[:, :, 0], stv[:, :, 1]
                            rt = ro_t[sl]
                            dve.put(nc.vector.tensor_tensor(rt[0], x1a, cs, ALU.mult))
                            dve.put(nc.vector.tensor_tensor(rt[1], x2a, sn, ALU.mult))
                            dve.put(nc.vector.tensor_tensor(rt[2], x1a, sn, ALU.mult))
                            dve.put(nc.vector.tensor_tensor(rt[3], x2a, cs, ALU.mult))
                            nc.vector.drain()
                            dve.put(nc.vector.tensor_sub(o1a, rt[0], rt[1]))
                            dve.put(nc.vector.tensor_add(o2a, rt[2], rt[3]))
                        else:
                            dve.put(nc.vector.tensor_copy(stg, ps[:, :]))
                        ps_free[sl] = d_st = dve.sig()
                        sp.wait(dve, d_st)
                        s_q[sl].dma(sp, out_d[ow * 4:(ow + 1) * 4,
                                              ts * 128:(ts + 1) * 128,
                                              :].transpose([1, 0, 2]),
                                    stg.rearrange("p (h d) -> p h d", d=DH))
                    wcol_free[wb] = pe.n

        dve.put(nc.vector.tensor_copy(nrmf[:, 0:48], nrm))
        d_nf = dve.sig()
        pe.wait(dve, d_nf)
        pe.wait(act, act.n)
        tpf = banks[0]
        pe.put(nc.tensor.transpose(tpf[:, 0:128], nrmf, identf))
        p_tn = pe.sig()
        dve.wait(pe, p_tn)
        dve.put(nc.vector.reduce_max(nrmf[0:32, 64:65], tpf[0:32, 0:128], axis=AX.X))
        d_mx = dve.sig()
        sp.wait(dve, d_mx)
        s_so.dma(sp, qn_o, nrmf[0:16, 64:65])
        s_so.dma(sp, kn_o, nrmf[16:32, 64:65])

        s_so.waited_by(sp)
        s_q[0].waited_by(sp)
        s_q[1].waited_by(sp)
        s_so.waited_by(gp)
        s_q[0].waited_by(gp)
        s_q[1].waited_by(gp)
    return nc


def build_l2():
    nc = bass.Bass("TRN2", target_bir_lowering=False)
    q_d = nc.dram_tensor("qh", [HPC, DH, S], BF16, kind="ExternalInput").ap()
    k_d = nc.dram_tensor("kh", [HPC, DH, S], BF16, kind="ExternalInput").ap()
    v_d = nc.dram_tensor("vh", [128, HPC, S // 128, DH], BF16,
                         kind="ExternalInput").ap()
    qn_d = nc.dram_tensor("qn", [1, HPC, NCORE], F32, kind="ExternalInput").ap()
    kn_d = nc.dram_tensor("kn", [1, HPC, NCORE], F32, kind="ExternalInput").ap()
    tri_d = nc.dram_tensor("tri", [128, 128], F32, kind="ExternalInput").ap()
    on_d = nc.dram_tensor("onesf", [1, 128], F32, kind="ExternalInput").ap()
    ob_d = nc.dram_tensor("onesb", [128, 1], BF16, kind="ExternalInput").ap()
    ao_o = nc.dram_tensor("ao", [HPC, DH, S], BF16, kind="ExternalOutput").ap()

    NQT = S // 512

    with ExitStack() as ctx:
        SB = lambda nm, sh, dt: ctx.enter_context(nc.sbuf_tensor("sb_" + nm, sh, dt)).ap()
        qs = SB("qs", [128, HPC * S], BF16)
        ks = SB("ks", [128, HPC * S], BF16)
        vs = SB("vs", [128, HPC * (S // 128) * DH], BF16)
        tri = SB("tri", [128, 128], F32)
        onesf = SB("onesf", [1, 128], F32)
        onesb = SB("onesb", [128, 1], BF16)
        nstat = SB("nstat", [1, 2 * NCORE + 8], F32)
        kn_sb = SB("knsb", [1, 2 * NCORE], F32)
        bias_sb = SB("bias", [128, 2], F32)
        pt = [SB(f"pt{i}", [128, 512], BF16) for i in range(4)]
        rsa = [SB(f"rsa{i}", [128, 512], F32) for i in range(2)]
        rsc = SB("rsc", [128, 512], BF16)
        rcp = [SB(f"rcp{i}", [1, 512], F32) for i in range(HPC)]
        rbs = SB("rbs", [128, 512], F32)
        aost = SB("aost", [128, 512], BF16)
        banks = [ctx.enter_context(nc.psum_tensor(f"bank{i}", [128, 512], F32)).ap()
                 for i in range(8)]
        st_ps = [banks[0], banks[1], banks[7]]
        pv_ps = banks[2:4]
        rs_ps = [banks[4][0:1, :], banks[5][0:1, :]]
        rb_ps = banks[6]

        E = mk_engs(nc, ctx)
        pe, act, dve, sp, gp = E

        s_ld = Stream(nc, ctx, "ld")
        s_ao = Stream(nc, ctx, "ao")
        s_h = [Stream(nc, ctx, f"h{i}") for i in range(2)]
        s_h2 = [Stream(nc, ctx, f"hb{i}") for i in range(2)]
        HS = S // 2
        for hl in range(2):
            s_h[hl].dma(sp, ks[:, hl * S: hl * S + HS], k_d[hl][:, 0:HS])
            s_h[hl].dma(act, qs[:, hl * S: hl * S + HS], q_d[hl][:, 0:HS])
            s_h[hl].dma(sp, vs[:, hl * (S // 128) * DH:
                               hl * (S // 128) * DH + (HS // 128) * DH],
                        v_d[:, hl, 0:HS // 128].rearrange("p k d -> p (k d)"))
        for hl in range(2):
            s_h2[hl].dma(sp, ks[:, hl * S + HS:(hl + 1) * S], k_d[hl][:, HS:S])
            s_h2[hl].dma(act, qs[:, hl * S + HS:(hl + 1) * S], q_d[hl][:, HS:S])
            s_h2[hl].dma(sp, vs[:, hl * (S // 128) * DH + (HS // 128) * DH:
                                (hl + 1) * (S // 128) * DH],
                         v_d[:, hl, HS // 128:].rearrange("p k d -> p (k d)"))
        s_ld.dma(sp, tri, tri_d)
        s_ld.dma(sp, onesf, on_d)
        s_ld.dma(sp, onesb, ob_d)
        s_ld.dma(sp, nstat[0:1, 0:2 * NCORE], qn_d.rearrange("a h c -> a (h c)"))
        s_ld.dma(sp, kn_sb, kn_d.rearrange("a h c -> a (h c)"))
        s_ld.waited_by(dve)
        s_ld.waited_by(pe)
        s_ld.waited_by(act)
        h_waited = [False, False]
        h2_waited = [False, False]
        mq = nstat[0:1, 2 * NCORE:2 * NCORE + 2]
        mk = nstat[0:1, 2 * NCORE + 2:2 * NCORE + 4]
        pr = nstat[0:1, 2 * NCORE + 4:2 * NCORE + 6]
        dve.put(nc.vector.reduce_max(
            mq, nstat[0:1, 0:2 * NCORE].rearrange("a (h c) -> a h c", c=NCORE),
            axis=AX.X))
        nc.vector.drain()
        dve.put(nc.vector.reduce_max(
            mk, kn_sb.rearrange("a (h c) -> a h c", c=NCORE), axis=AX.X))
        nc.vector.drain()
        dve.put(nc.vector.tensor_tensor(pr, mq, mk, ALU.mult))
        d_pr = dve.sig()
        act.wait(dve, d_pr)
        act.put(nc.scalar.activation(pr, pr, AF.Sqrt,
                                     scale=(1.02 * INVSQ) * (1.02 * INVSQ)))
        a_pr = act.sig()
        dve.wait(act, a_pr)
        dve.put(nc.vector.tensor_scalar_mul(pr, pr, -1.0))
        d_ng = dve.sig()
        pe.wait(dve, d_ng)
        pe.put(nc.tensor.matmul(rb_ps[:, 0:2], onesf[0:1, :], pr, start=True,
                                stop=True))
        p_b = pe.sig()
        dve.wait(pe, p_b)
        dve.put(nc.vector.tensor_copy(bias_sb, rb_ps[:, 0:2]))
        d_bias = dve.sig()
        pe.wait(dve, d_bias)
        act.wait(dve, d_bias)

        iters = []
        for qt in range(NQT):
            for hl in range(HPC):
                for kc in range(qt + 1):
                    diag = (kc == qt)
                    for ksub in range(4):
                        iters.append(dict(
                            qt=qt, hl=hl, kt=kc * 4 + ksub,
                            qoff=ksub * 128 if diag else 0,
                            diag=diag,
                            first=(kc == 0 and ksub == 0),
                            first3=(kc == 0 and ksub == 3),
                            lastmm=(diag and ksub == 3)))

        pt_free = [0, 0, 0, 0]
        pt_dve_free = [0, 0, 0, 0]
        rsc_free = [0]
        st_free = [0, 0, 0]
        pv_free = [0, 0]
        rs_free = [0, 0]
        rcp_free = [0, 0]
        rb_free = 0
        exp_of = {}

        def emit_st(i):
            it = iters[i]
            sl = i % 3
            psl = i % 4
            stp = st_ps[sl]
            qb = it["hl"] * S + it["qt"] * 512
            if not h_waited[it["hl"]]:
                s_h[it["hl"]].waited_by(pe)
                s_h[it["hl"]].waited_by(act)
                h_waited[it["hl"]] = True
            if not h2_waited[it["hl"]] and (it["qt"] * 512 + 512 > HS
                                            or (it["kt"] + 1) * 128 > HS):
                s_h2[it["hl"]].waited_by(pe)
                s_h2[it["hl"]].waited_by(act)
                h2_waited[it["hl"]] = True
            pe.wait(act, st_free[sl])
            if it["first"]:
                pe.wait(dve, pv_free[it["hl"]])
                pe.wait(dve, rs_free[it["hl"]])
            pe.put(nc.tensor.matmul(
                stp[:, it["qoff"]:512],
                ks[:, it["hl"] * S + it["kt"] * 128:
                   it["hl"] * S + (it["kt"] + 1) * 128],
                qs[:, qb + it["qoff"]: qb + 512], start=True, stop=True))
            p_st = pe.sig()
            if it["diag"]:
                dve.wait(pe, p_st)
                dsl = stp[:, it["qoff"]:it["qoff"] + 128]
                dve.put(nc.vector.tensor_tensor(dsl, dsl, tri, ALU.add))
                d_msk = dve.sig()
                act.wait(dve, d_msk)
            act.wait(pe, p_st)
            act.wait(pe, pt_free[psl])
            act.wait(dve, pt_dve_free[psl])
            act.put(nc.scalar.activation(
                pt[psl][:, it["qoff"]:512], stp[:, it["qoff"]:512], AF.Exp,
                bias=bias_sb[:, it["hl"]:it["hl"] + 1], scale=INVSQ))
            st_free[sl] = exp_of[i] = act.sig()

        def emit_pv(i):
            it = iters[i]
            psl = i % 4
            hl = it["hl"]
            pe.wait(act, exp_of[i])
            nc.tensor.matmul(
                pv_ps[hl][:, it["qoff"]:512],
                vs[:, (hl * (S // 128) + it["kt"]) * DH:
                   (hl * (S // 128) + it["kt"] + 1) * DH],
                pt[psl][:, it["qoff"]:512],
                start=it["first"], stop=it["lastmm"])
            last = nc.tensor.matmul(
                rs_ps[hl][:, it["qoff"]:512], onesb[:, 0:1],
                pt[psl][:, it["qoff"]:512],
                start=it["first"], stop=it["lastmm"])
            pe.put(last)
            pt_free[psl] = pe.sig()

        def emit_fin_a(i):
            it = iters[i]
            hl = it["hl"]
            p_pv = pe.n
            dve.wait(pe, p_pv)
            dve.wait(pe, rcp_free[hl])
            dve.put(nc.vector.reciprocal(rcp[hl], rs_ps[hl]))
            return dve.sig()

        def emit_fin_b(i, d_rcp):
            nonlocal rb_free
            it = iters[i]
            hl = it["hl"]
            qt = it["qt"]
            pe.wait(dve, d_rcp)
            pe.wait(dve, rb_free)
            pe.put(nc.tensor.matmul(rb_ps[:, :], onesf[0:1, :], rcp[hl],
                                    start=True, stop=True))
            rcp_free[hl] = p_rb = pe.sig()
            dve.wait(pe, p_rb)
            dve.put(nc.vector.tensor_copy(rbs, rb_ps[:, :]))
            rb_free = dve.sig()
            nc.vector.drain()
            s_ao.waited_by(dve)
            dve.put(nc.vector.tensor_tensor(aost, pv_ps[hl][:, :], rbs,
                                            ALU.mult))
            pv_free[hl] = rs_free[hl] = d_ao = dve.sig()
            sp.wait(dve, d_ao)
            s_ao.dma(sp, ao_o[hl][:, qt * 512:(qt + 1) * 512], aost)

        pending = []
        emit_st(0)
        emit_st(1)
        for i in range(len(iters)):
            while pending and pending[0][0] <= i:
                _, bi, d_rcp = pending.pop(0)
                emit_fin_b(bi, d_rcp)
            if i + 2 < len(iters):
                emit_st(i + 2)
            emit_pv(i)
            if iters[i]["lastmm"]:
                d_rcp = emit_fin_a(i)
                pending.append((i + 3, i, d_rcp))
        while pending:
            _, bi, d_rcp = pending.pop(0)
            emit_fin_b(bi, d_rcp)

        s_ao.waited_by(sp)
        s_ao.waited_by(gp)
    return nc


def build_l3():
    nc = bass.Bass("TRN2", target_bir_lowering=False)
    x1_in = nc.dram_tensor("x1", [NTT, 128, D], F32, kind="ExternalInput").ap()
    ao_in = nc.dram_tensor("aof", [NDC, 128, TPC], BF16, kind="ExternalInput").ap()
    wo_d = nc.dram_tensor("wo", [NDC, 128, D], F32, kind="ExternalInput").ap()
    ax_d = nc.dram_tensor("ao_a", [NDC, 128, NA, RK], F32, kind="ExternalInput").ap()
    bx_d = nc.dram_tensor("ao_b", [NA, RK, D], F32, kind="ExternalInput").ap()
    wg_d = nc.dram_tensor("wg", [NHC, 128, D], F32, kind="ExternalInput").ap()
    wu_d = nc.dram_tensor("wu", [NHC, 128, D], F32, kind="ExternalInput").ap()
    wd_d = nc.dram_tensor("wd", [NHC, 128, D], F32, kind="ExternalInput").ap()
    st_d = nc.dram_tensor("starts", [1, 4], I32, kind="ExternalInput").ap()
    io_d = nc.dram_tensor("iota", [1, 4], F32, kind="ExternalInput").ap()
    id_d = nc.dram_tensor("ident", [128, 128], BF16, kind="ExternalInput").ap()
    y_o = nc.dram_tensor("y", [NTT, 128, D], F32, kind="ExternalOutput").ap()

    with ExitStack() as ctx:
        SB = lambda nm, sh, dt: ctx.enter_context(nc.sbuf_tensor("sb_" + nm, sh, dt)).ap()
        resid = SB("resid", [128, NTT * D], F32)
        scr = SB("scr", [128, D], BF16)
        lnout = [SB(f"lnout{b}", [128, D], BF16) for b in range(2)]
        stats = SB("stats", [128, 8 * NTT], F32)
        eps_t = SB("eps", [128, 1], F32)
        ident = SB("ident", [128, 128], BF16)
        starts_sb = SB("starts", [1, 16], F32)
        iota_sb = SB("iotas", [1, 4], F32)
        ohb = SB("ohb", [128, 8], F32)
        ones1 = SB("ones1", [1, 128], F32)
        asel = SB("asel", [128, NDC * 3 * RK], BF16)
        bsel = SB("bsel", [16, D], BF16)
        low = SB("low", [16, TPC], BF16)
        banks = [ctx.enter_context(nc.psum_tensor(f"bank{i}", [128, 512], F32)).ap()
                 for i in range(8)]
        E = mk_engs(nc, ctx)
        pe, act, dve, sp, gp = E

        MS = lambda nm: Stream(nc, ctx, nm)
        s_ld, s_ss, s_sa, s_bt, s_so = MS("ld"), MS("ss"), MS("sa"), MS("bt"), MS("so")
        s_wa = MS("wa")
        s_aos = MS("aos")
        ST = {k: MS(k) for k in
              ["wg0", "wg1", "wu0", "wu1"] + [f"wd{i}" for i in range(NG)]}
        c2 = ctx.enter_context(ExitStack())
        SB2 = lambda nm, sh, dt: c2.enter_context(nc.sbuf_tensor("sb_" + nm, sh, dt)).ap()
        aos = SB2("aos", [128, NDC * TPC], BF16)
        s_aos.dma(sp, aos.rearrange("p (c t) -> p c t", t=TPC),
                  ao_in.transpose([1, 0, 2]))
        for tt in range(NTT):
            s_ld.dma(sp, resid[:, tt * D:(tt + 1) * D], x1_in[tt])
        s_ld.dma(sp, ident, id_d)
        s_ld.dma(sp, iota_sb, io_d)
        s_ss.dma(gp, starts_sb[0:1, 0:4], st_d)
        dve.put(nc.vector.memset(eps_t, 1e-5))
        dve.put(nc.vector.memset(ones1, 1.0))
        dve.sig()
        s_ld.waited_by(act)
        s_ld.waited_by(dve)

        if True:

            wall = [SB2(f"wall{i}", [128, D], BF16) for i in range(NDC)]
            for dc in range(NDC):
                s_wa.dma(gp, wall[dc], wo_d[dc], max_dma_last_dim=8192)
            a_sb = SB2("asb", [128, NDC * NA * RK], BF16)
            b_all = SB2("ball", [16, NA * D], BF16)
            s_sa.dma(gp, a_sb.rearrange("p (f) -> p f"), ax_d.transpose([1, 0, 2, 3]))
            s_sa.dma(gp, b_all.rearrange("p (a d) -> p a d", d=D),
                     bx_d.transpose([1, 0, 2]))
            emit_onehot(nc, E, starts_sb, iota_sb, ohb, ones1, banks[7], s_ld,
                        s_ss)
            s_sa.waited_by(dve)
            slA = asel.rearrange("p (c t r) -> p c t r", t=3, r=RK)[:, :, 0, :]
            emit_ab_select(nc, E, ohb, [(a_sb, slA)], [(b_all, bsel)])
            gp.e.wait_ge(dve.sem, dve.n)

            s_aos.waited_by(pe)
            pe.wait(dve, dve.n)
            lo_ps = banks[2][0:16, :]
            for tc in range(2):
                pe.wait(act, act.n)
                last = None
                for dc in range(NDC):
                    lh = asel.rearrange("p (c t r) -> p c t r", t=3, r=RK)[:, dc, 0, :]
                    last = nc.tensor.matmul(
                        lo_ps, lh,
                        aos[:, dc * TPC + tc * 512: dc * TPC + tc * 512 + 512],
                        start=(dc == 0), stop=(dc == NDC - 1))
                pe.put(last)
                p_lo = pe.sig()
                act.wait(pe, p_lo)
                act.put(nc.scalar.copy(low[0:16, tc * 512: tc * 512 + 512], lo_ps))
                act.sig()
            low_ready = act.n
            pe.wait(act, low_ready)

            o_done = [0] * 4
            resid_ts_counts = [0] * NTT
            wa_waited = 0
            low_waited = False
            for ts in range(NTT):
                for oc in range(4):
                    pe.wait(dve, o_done[oc])
                    for dc in range(NDC):
                        if wa_waited < (dc + 1) * 16:
                            wa_waited = (dc + 1) * 16
                            s_wa.waited_by(pe, wa_waited)
                        nc.tensor.matmul(
                            banks[oc][:, :],
                            aos[:, dc * TPC + ts * 128: dc * TPC + (ts + 1) * 128],
                            wall[dc][:, oc * 512:(oc + 1) * 512],
                            start=(dc == 0), stop=False)
                    if not low_waited:
                        pe.wait(act, low_ready)
                        low_waited = True
                    last = nc.tensor.matmul(
                        banks[oc][:, :], low[0:16, ts * 128:(ts + 1) * 128],
                        bsel[:, oc * 512:(oc + 1) * 512], start=False, stop=True)
                    pe.put(last)
                    p_o = pe.sig()
                    dve.wait(pe, p_o)
                    rsl = resid[:, ts * D + oc * 512: ts * D + oc * 512 + 512]
                    dve.put(nc.vector.tensor_tensor(rsl, rsl, banks[oc][:, :],
                                                    ALU.add))
                    o_done[oc] = dve.sig()
                resid_ts_counts[ts] = dve.n
            oproj_done = dve.n
            gp.e.wait_ge(dve.sem, oproj_done)
            sp.e.wait_ge(dve.sem, oproj_done)
        ctx.pop_all().close() if False else None
        c2.close()

        with ExitStack() as c3:
            SB3 = lambda nm, sh, dt: c3.enter_context(nc.sbuf_tensor("sb_" + nm, sh, dt)).ap()
            xt = SB3("xt", [128, NDC * TPC], BF16)
            xt_tts = emit_layernorm(nc, E, resid, xt, scr, stats, eps_t, ident,
                                    banks[0:2], lnout, resid_ts_counts)
            gated = SB3("gated", [128, NG * TPC], BF16)
            wg_sb = [SB3(f"wg{b}", [128, D], BF16) for b in range(2)]
            wu_sb = [SB3(f"wu{b}", [128, D], BF16) for b in range(2)]
            wd_sb = [SB3(f"wd{b}", [128, D], BF16) for b in range(NG)]
            def _store_y(ts, dcount):
                sp.wait(dve, dcount)
                s_so.dma(sp, y_o[ts], resid[:, ts * D:(ts + 1) * D])
            ffn_done = emit_ffn(nc, E, ST, xt, resid, gated, wg_sb, wu_sb, wd_sb,
                                [banks[4], banks[5]], [banks[6], banks[7]],
                                banks[0:4], {"wg": wg_d, "wu": wu_d, "wd": wd_d},
                                xt_tts, store_cb=_store_y)
            gp.e.wait_ge(dve.sem, ffn_done)
            s_so.waited_by(sp)
            s_so.waited_by(gp)
    return nc


def _rope_tables():
    pos = np.arange(S, dtype=np.float64)
    j = np.arange(64, dtype=np.float64)
    inv = 10000.0 ** (-2.0 * j / DH)
    ang = pos[:, None] * inv[None, :]
    cos4 = np.tile(np.cos(ang).astype(np.float32), (1, 4))
    sin4 = np.tile(np.sin(ang).astype(np.float32), (1, 4))
    return cos4, sin4


def _tri_mask():
    m = np.zeros((128, 128), np.float32)
    kk, qq = np.meshgrid(np.arange(128), np.arange(128), indexing="ij")
    m[qq < kk] = -1e9
    return m


def _stage_a(A):
    return np.ascontiguousarray(A.transpose(2, 0, 1).reshape(NDC, 128, NA, RK))


def _stage_b(B):
    return np.ascontiguousarray(B.transpose(0, 2, 1))


def _stage_wT(W):
    return np.ascontiguousarray(W.T.reshape(NDC, 128, W.shape[0]))


def _stage_w_colblocks(W):
    return np.ascontiguousarray(W.T.reshape(NDC, 128, 4, 512).transpose(0, 2, 1, 3))


def _stage_wgu(W):
    return np.ascontiguousarray(
        W.reshape(NHC, 128, NDC, 128).transpose(0, 3, 2, 1).reshape(NHC, 128, D))


def _stage_wd(W):
    return np.ascontiguousarray(W.T.reshape(NHC, 128, D))


_IDENT_BF = np.eye(128, dtype=BF)
_IDENT_F32 = np.eye(128, dtype=np.float32)
_IOTA = np.arange(4, dtype=np.float32).reshape(1, 4)
_CACHE = {}


def _get_ncs():
    if "l1" not in _CACHE:
        _CACHE["l1"] = build_l1()
        _CACHE["l2"] = build_l2()
        _CACHE["l3"] = build_l3()
    return _CACHE["l1"], _CACHE["l2"], _CACHE["l3"]


def kernel(L0_attn_x, L0_attn_lora_start_indices, L0_attn_Wq, L0_attn_Aq,
           L0_attn_Bq, L0_attn_Wk, L0_attn_Ak, L0_attn_Bk, L0_attn_Wv,
           L0_attn_Av, L0_attn_Bv, L0_attn_Wo, L0_attn_Ao, L0_attn_Bo,
           L0_ffn_w_gate, L0_ffn_w_up, L0_ffn_w_down):
    nc1, nc2, nc3 = _get_ncs()
    f32 = lambda a: np.ascontiguousarray(np.asarray(a, dtype=np.float32))
    x = f32(L0_attn_x).reshape(S, D)
    starts = np.ascontiguousarray(np.asarray(L0_attn_lora_start_indices,
                                             dtype=np.int32)).reshape(1, 4)

    wg = _stage_wgu(f32(L0_ffn_w_gate))
    wu = _stage_wgu(f32(L0_ffn_w_up))
    wd = _stage_wd(f32(L0_ffn_w_down))
    wq = _stage_w_colblocks(f32(L0_attn_Wq))
    wk = _stage_w_colblocks(f32(L0_attn_Wk))
    wv = _stage_w_colblocks(f32(L0_attn_Wv))
    wo = _stage_wT(f32(L0_attn_Wo))
    aq, ak, av, ao_a = (_stage_a(f32(a)) for a in
                        (L0_attn_Aq, L0_attn_Ak, L0_attn_Av, L0_attn_Ao))
    bq, bk, bv, bo_b = (_stage_b(f32(b)) for b in
                        (L0_attn_Bq, L0_attn_Bk, L0_attn_Bv, L0_attn_Bo))
    cos4, sin4 = _rope_tables()

    in1 = []
    for c in range(NCORE):
        tok = slice(c * TPC, (c + 1) * TPC)
        in1.append(dict(
            x=np.ascontiguousarray(x[tok].reshape(NTT, 128, D)),
            wg=wg, wu=wu, wd=wd, wq=wq, wk=wk, wv=wv,
            aq=aq, ak=ak, av=av, bq=bq, bk=bk, bv=bv,
            cosr=np.ascontiguousarray(cos4[tok].reshape(NTT, 128, 256)),
            sinr=np.ascontiguousarray(sin4[tok].reshape(NTT, 128, 256)),
            starts=starts, iota=_IOTA, ident=_IDENT_BF, identf=_IDENT_F32))
    r1 = run_bass_kernel_spmd(nc1, in1, core_ids=list(range(NCORE)))
    x1 = np.stack([r1.results[c]["x1"].reshape(TPC, D) for c in range(NCORE)])
    qo = np.stack([r1.results[c]["qo"] for c in range(NCORE)])
    ko = np.stack([r1.results[c]["ko"] for c in range(NCORE)])
    vo = np.stack([r1.results[c]["vo"] for c in range(NCORE)])
    qn = np.stack([r1.results[c]["qn"].reshape(H) for c in range(NCORE)])
    kn = np.stack([r1.results[c]["kn"].reshape(H) for c in range(NCORE)])

    q_full = np.ascontiguousarray(qo.transpose(1, 0, 2, 3).reshape(H, S, DH))
    k_full = np.ascontiguousarray(ko.transpose(1, 0, 2, 3).reshape(H, S, DH))
    v_full = np.ascontiguousarray(vo.transpose(1, 0, 2, 3).reshape(H, S, DH))
    tri = _tri_mask()
    onesf = np.ones((1, 128), np.float32)
    onesb = np.ones((128, 1), BF)
    in2 = []
    for c in range(NCORE):
        hs = slice(HPC * c, HPC * (c + 1))
        in2.append(dict(
            qh=np.ascontiguousarray(q_full[hs].transpose(0, 2, 1)),
            kh=np.ascontiguousarray(k_full[hs].transpose(0, 2, 1)),
            vh=np.ascontiguousarray(v_full[hs].reshape(HPC, S // 128, 128, DH)
                                    .transpose(2, 0, 1, 3)),
            qn=np.ascontiguousarray(qn.T[hs].reshape(1, HPC, NCORE)),
            kn=np.ascontiguousarray(kn.T[hs].reshape(1, HPC, NCORE)),
            tri=tri, onesf=onesf, onesb=onesb))
    r2 = run_bass_kernel_spmd(nc2, in2, core_ids=list(range(NCORE)))
    ao = np.stack([r2.results[c]["ao"] for c in range(NCORE)])
    ao_full = ao.reshape(H, DH, S)

    in3 = []
    for c in range(NCORE):
        tok = slice(c * TPC, (c + 1) * TPC)
        in3.append(dict(
            x1=np.ascontiguousarray(x1[c].reshape(NTT, 128, D)),
            aof=np.ascontiguousarray(ao_full[:, :, tok]),
            wo=wo, ao_a=ao_a, ao_b=bo_b, wg=wg, wu=wu, wd=wd,
            starts=starts, iota=_IOTA, ident=_IDENT_BF))
    r3 = run_bass_kernel_spmd(nc3, in3, core_ids=list(range(NCORE)))
    y = np.concatenate([r3.results[c]["y"].reshape(TPC, D) for c in range(NCORE)])
    return y.reshape(1, S, D)


# version 15
# speedup vs baseline: 1.0116x; 1.0116x over previous
import math
import numpy as np
import ml_dtypes
from contextlib import ExitStack

import concourse.bass as bass
import concourse.mybir as mybir
from concourse.bass_utils import run_bass_kernel_spmd

F32 = mybir.dt.float32
BF16 = mybir.dt.bfloat16
I32 = mybir.dt.int32
AF = mybir.ActivationFunctionType
ALU = mybir.AluOpType
AX = mybir.AxisListType

NCORE = 8
D = 2048
S = 8192
H = 16
DH = 128
HID = 8192
TPC = S // NCORE
NTT = TPC // 128
NDC = D // 128
NHC = HID // 128
NG = 8
HPC = H // NCORE
RK = 16
NA = 4
INVSQ = 1.0 / math.sqrt(DH)
BF = ml_dtypes.bfloat16


class Eng:

    def __init__(self, nc, ctx, name, eng):
        self.nc = nc
        self.e = eng
        self.sem = ctx.enter_context(nc.semaphore(f"sem_{name}"))
        self.n = 0
        self.last = None

    def put(self, instr):
        self.last = instr
        return instr

    def sig(self, k=1):
        self.last.then_inc(self.sem, k)
        self.n += k
        return self.n

    def wait(self, other, count):
        if count > 0:
            self.e.wait_ge(other.sem, count)


class Stream:

    def __init__(self, nc, ctx, name):
        self.sem = ctx.enter_context(nc.semaphore(f"st_{name}"))
        self.n = 0

    def dma(self, qeng, out, in_, **kw):
        qeng.e.dma_start(out, in_, **kw).then_inc(self.sem, 16)
        self.n += 16
        return self.n

    def waited_by(self, eng, count=None):
        eng.e.wait_ge(self.sem, self.n if count is None else count)


def mk_engs(nc, ctx):
    return (Eng(nc, ctx, "pe", nc.tensor), Eng(nc, ctx, "act", nc.scalar),
            Eng(nc, ctx, "dve", nc.vector), Eng(nc, ctx, "sp", nc.sync),
            Eng(nc, ctx, "gp", nc.gpsimd))


def emit_layernorm(nc, E, resid, xt, scr, stats, eps_t, ident, banks, lnout,
                   resid_dve_ready):
    if not isinstance(resid_dve_ready, (list, tuple)):
        resid_dve_ready = [resid_dve_ready] * NTT
    """LayerNorm of the NTT resident token tiles into transposed bf16 xt.

    resid [128, NTT*D] f32 -> xt [128, NDC*TPC] bf16 (X^T, feature-major).
    banks: two f32 [128,512] psum banks (bitcast bf16) for PE transposes.
    Returns act count at which xt is fully written.
    """
    pe, act, dve, sp, gp = E
    sums = stats[:, 0:NTT]
    sumsq = stats[:, NTT:2 * NTT]
    s2 = stats[:, 2 * NTT:3 * NTT]
    vpe = stats[:, 3 * NTT:4 * NTT]
    sq = stats[:, 4 * NTT:5 * NTT]
    rstd = stats[:, 5 * NTT:6 * NTT]
    nmr = stats[:, 6 * NTT:7 * NTT]

    for tt in range(NTT):
        act.wait(dve, resid_dve_ready[tt])
        act.put(nc.scalar.activation(scr[:, :], resid[:, tt * D:(tt + 1) * D],
                                     AF.Square, accum_out=sumsq[:, tt:tt + 1]))
        nc.scalar.drain()
    for tt in range(NTT):
        dve.wait(dve, resid_dve_ready[tt])
        dve.put(nc.vector.reduce_sum(sums[:, tt:tt + 1],
                                     resid[:, tt * D:(tt + 1) * D], axis=AX.X))
        nc.vector.drain()
    d_sums = dve.sig()
    act.wait(dve, d_sums)
    act.put(nc.scalar.activation(s2, sums, AF.Square, scale=1.0 / D))
    a_stats = act.sig()
    dve.wait(act, a_stats)
    dve.put(nc.vector.scalar_tensor_tensor(vpe, sumsq, 1.0 / D, s2, ALU.mult,
                                           ALU.subtract))
    d_vpe = dve.sig()
    act.wait(dve, d_vpe)
    act.put(nc.scalar.activation(sq, vpe, AF.Sqrt, bias=eps_t[:, 0:1]))
    a_sq = act.sig()
    dve.wait(act, a_sq)
    dve.put(nc.vector.reciprocal(rstd, sq))
    nc.vector.drain()
    dve.put(nc.vector.scalar_tensor_tensor(nmr, sums, -1.0 / D, rstd, ALU.mult,
                                           ALU.mult))
    nc.vector.drain()
    dve.sig()

    slot_evac = [0, 0]
    xt_tt_counts = [0] * NTT
    for tt in range(NTT):
        lo = lnout[tt % 2]
        dve.wait(pe, pe.n)
        dve.put(nc.vector.tensor_scalar(lo[:, :], resid[:, tt * D:(tt + 1) * D],
                                        rstd[:, tt:tt + 1], nmr[:, tt:tt + 1],
                                        ALU.mult, ALU.add))
        d_ln = dve.sig()
        for dc in range(NDC):
            sl = dc % 2
            ps = banks[sl].bitcast(BF16)[:, 0:128]
            pe.wait(act, slot_evac[sl])
            pe.wait(dve, d_ln)
            pe.put(nc.tensor.transpose(ps, lo[:, dc * 128:(dc + 1) * 128], ident))
            p_tp = pe.sig()
            act.wait(pe, p_tp)
            act.put(nc.scalar.copy(
                xt[:, dc * TPC + tt * 128: dc * TPC + (tt + 1) * 128], ps))
            slot_evac[sl] = act.sig()
        xt_tt_counts[tt] = act.n
    return xt_tt_counts


def emit_ffn(nc, E, ST, xt, resid, gated, wg_sb, wu_sb, wd_sb, gps, ups, dps,
             wst, xt_act_ready, store_cb=None):
    pe, act, dve, sp, gp = E
    NGRP = NHC // NG
    gate_done = [0, 0]
    down_done = [0] * 4
    wbuf_free = [0, 0]
    wd_free = [0] * NG
    swg = [ST["wg0"], ST["wg1"]]
    swu = [ST["wu0"], ST["wu1"]]
    swd = [ST[f"wd{i}"] for i in range(NG)]
    if isinstance(xt_act_ready, (list, tuple)):
        xt_tc = [xt_act_ready[3], xt_act_ready[7]]
    else:
        xt_tc = [xt_act_ready, xt_act_ready]
    waited_tc = [False, False]
    for g in range(NGRP):
        for hc in range(NG):
            hcg = g * NG + hc
            b = hc % 2
            gp.e.wait_ge(pe.sem, wbuf_free[b])
            swg[b].dma(gp, wg_sb[b][:, :], wst["wg"][hcg], max_dma_last_dim=8192)
            swu[b].dma(gp, wu_sb[b][:, :], wst["wu"][hcg], max_dma_last_dim=8192)
            gp.e.wait_ge(pe.sem, wd_free[hc])
            swd[hc].dma(gp, wd_sb[hc][:, :], wst["wd"][hcg],
                        max_dma_last_dim=8192)
            for tc in range(2):
                sl = tc
                if tc == 0:
                    swg[b].waited_by(pe)
                    swu[b].waited_by(pe)
                if not waited_tc[tc]:
                    pe.wait(act, xt_tc[tc])
                    waited_tc[tc] = True
                pe.wait(dve, gate_done[sl])
                for dc in range(NDC):
                    nc.tensor.matmul(
                        gps[sl][:, :], wg_sb[b][:, dc * 128:(dc + 1) * 128],
                        xt[:, dc * TPC + tc * 512: dc * TPC + tc * 512 + 512],
                        start=(dc == 0), stop=(dc == NDC - 1))
                last = None
                for dc in range(NDC):
                    last = nc.tensor.matmul(
                        ups[sl][:, :], wu_sb[b][:, dc * 128:(dc + 1) * 128],
                        xt[:, dc * TPC + tc * 512: dc * TPC + tc * 512 + 512],
                        start=(dc == 0), stop=(dc == NDC - 1))
                pe.put(last)
                p_mm = pe.sig()
                act.wait(pe, p_mm)
                gsl = gated[:, hc * TPC + tc * 512: hc * TPC + tc * 512 + 512]
                act.put(nc.scalar.activation(gsl, gps[sl][:, :], AF.Silu))
                a_silu = act.sig()
                dve.wait(act, a_silu)
                dve.put(nc.vector.scalar_tensor_tensor(gsl, gsl, 0.5, ups[sl][:, :],
                                                       ALU.mult, ALU.mult))
                gate_done[sl] = dve.sig()
            wbuf_free[b] = pe.n
        d_gate = dve.n
        pe.wait(dve, d_gate)
        if not waited_tc[1]:
            pe.wait(act, xt_tc[1])
            waited_tc[1] = True
        for hc in range(NG):
            swd[hc].waited_by(pe)
        for ts in range(NTT):
            for oc in range(4):
                pe.wait(dve, down_done[oc])
                last = None
                for hc in range(NG):
                    last = nc.tensor.matmul(
                        dps[oc][:, :],
                        gated[:, hc * TPC + ts * 128: hc * TPC + (ts + 1) * 128],
                        wd_sb[hc][:, oc * 512:(oc + 1) * 512],
                        start=(hc == 0), stop=(hc == NG - 1))
                pe.put(last)
                p_dn = pe.sig()
                dve.wait(pe, p_dn)
                rsl = resid[:, ts * D + oc * 512: ts * D + oc * 512 + 512]
                dve.put(nc.vector.tensor_tensor(rsl, rsl, dps[oc][:, :], ALU.add))
                down_done[oc] = dve.sig()
            if store_cb is not None and g == NGRP - 1:
                store_cb(ts, dve.n)
        for hc in range(NG):
            wd_free[hc] = pe.n
    return dve.n


def emit_onehot(nc, E, starts_sb, iota_sb, ohb, ones1, oh_ps, s_ld, s_ss):
    pe, act, dve, sp, gp = E
    s_ld.waited_by(dve)
    s_ss.waited_by(dve)
    cnt = starts_sb[0:1, 4:5]
    le = starts_sb[0:1, 12:16]
    ohr = starts_sb[0:1, 8:12]
    dve.put(nc.vector.tensor_single_scalar(le, starts_sb[0:1, 0:4], 0.0, ALU.is_le))
    nc.vector.drain()
    dve.put(nc.vector.reduce_sum(cnt, le, axis=AX.X))
    nc.vector.drain()
    dve.put(nc.vector.tensor_scalar_add(cnt, cnt, -1.0))
    nc.vector.drain()
    dve.put(nc.vector.tensor_relu(cnt, cnt))
    nc.vector.drain()
    dve.put(nc.vector.tensor_scalar(ohr, iota_sb[0:1, 0:4], cnt[0:1, 0:1], None,
                                    ALU.subtract, ALU.bypass))
    nc.vector.drain()
    dve.put(nc.vector.tensor_tensor(ohr, ohr, ohr, ALU.mult))
    nc.vector.drain()
    dve.put(nc.vector.tensor_scalar(ohr, ohr, -1.0, 1.0, ALU.mult, ALU.add))
    nc.vector.drain()
    dve.put(nc.vector.tensor_relu(ohr, ohr))
    d_oh = dve.sig()
    pe.wait(dve, d_oh)
    pe.put(nc.tensor.matmul(oh_ps[:, 0:4], ones1[0:1, :], ohr, start=True,
                            stop=True))
    p_oh = pe.sig()
    dve.wait(pe, p_oh)
    dve.put(nc.vector.tensor_copy(ohb[:, 0:4], oh_ps[:, 0:4]))
    nc.vector.drain()
    return dve.sig()


def emit_ab_select(nc, E, ohb, a_pairs, b_pairs):
    pe, act, dve, sp, gp = E
    for a_sb, asl in a_pairs:
        for a in range(NA):
            src = a_sb.rearrange("p (c a r) -> p c a r", a=NA, r=RK)[:, :, a, :]
            if a == 0:
                dve.put(nc.vector.tensor_scalar_mul(asl, src, ohb[:, 0:1]))
            else:
                dve.put(nc.vector.scalar_tensor_tensor(asl, src, ohb[:, a:a + 1],
                                                       asl, ALU.mult, ALU.add))
            nc.vector.drain()
    a_done = dve.sig()
    for b_all, bsel in b_pairs:
        for a in range(NA):
            src = b_all[:, a * D:(a + 1) * D]
            if a == 0:
                dve.put(nc.vector.tensor_scalar_mul(bsel[:, :], src,
                                                    ohb[0:16, 0:1]))
            else:
                dve.put(nc.vector.scalar_tensor_tensor(bsel[:, :], src,
                                                       ohb[0:16, a:a + 1],
                                                       bsel[:, :], ALU.mult,
                                                       ALU.add))
            nc.vector.drain()
    return a_done, dve.sig()


def build_l1():
    nc = bass.Bass("TRN2", target_bir_lowering=False)
    x_in = nc.dram_tensor("x", [NTT, 128, D], F32, kind="ExternalInput").ap()
    wg_d = nc.dram_tensor("wg", [NHC, 128, D], F32, kind="ExternalInput").ap()
    wu_d = nc.dram_tensor("wu", [NHC, 128, D], F32, kind="ExternalInput").ap()
    wd_d = nc.dram_tensor("wd", [NHC, 128, D], F32, kind="ExternalInput").ap()
    wq_d = nc.dram_tensor("wq", [NDC, 4, 128, 512], F32, kind="ExternalInput").ap()
    wk_d = nc.dram_tensor("wk", [NDC, 4, 128, 512], F32, kind="ExternalInput").ap()
    wv_d = nc.dram_tensor("wv", [NDC, 4, 128, 512], F32, kind="ExternalInput").ap()
    aq_d = nc.dram_tensor("aq", [NDC, 128, NA, RK], F32, kind="ExternalInput").ap()
    ak_d = nc.dram_tensor("ak", [NDC, 128, NA, RK], F32, kind="ExternalInput").ap()
    av_d = nc.dram_tensor("av", [NDC, 128, NA, RK], F32, kind="ExternalInput").ap()
    bq_d = nc.dram_tensor("bq", [NA, RK, D], F32, kind="ExternalInput").ap()
    bk_d = nc.dram_tensor("bk", [NA, RK, D], F32, kind="ExternalInput").ap()
    bv_d = nc.dram_tensor("bv", [NA, RK, D], F32, kind="ExternalInput").ap()
    cos_d = nc.dram_tensor("cosr", [NTT, 128, 256], F32, kind="ExternalInput").ap()
    sin_d = nc.dram_tensor("sinr", [NTT, 128, 256], F32, kind="ExternalInput").ap()
    st_d = nc.dram_tensor("starts", [1, 4], I32, kind="ExternalInput").ap()
    io_d = nc.dram_tensor("iota", [1, 4], F32, kind="ExternalInput").ap()
    id_d = nc.dram_tensor("ident", [128, 128], BF16, kind="ExternalInput").ap()
    idf_d = nc.dram_tensor("identf", [128, 128], F32, kind="ExternalInput").ap()

    x1_o = nc.dram_tensor("x1", [NTT, 128, D], F32, kind="ExternalOutput").ap()
    q_o = nc.dram_tensor("qo", [H, TPC, DH], BF16, kind="ExternalOutput").ap()
    k_o = nc.dram_tensor("ko", [H, TPC, DH], BF16, kind="ExternalOutput").ap()
    v_o = nc.dram_tensor("vo", [H, TPC, DH], BF16, kind="ExternalOutput").ap()
    qn_o = nc.dram_tensor("qn", [H, 1], F32, kind="ExternalOutput").ap()
    kn_o = nc.dram_tensor("kn", [H, 1], F32, kind="ExternalOutput").ap()

    with ExitStack() as ctx:
        SB = lambda nm, sh, dt: ctx.enter_context(nc.sbuf_tensor("sb_" + nm, sh, dt)).ap()
        resid = SB("resid", [128, NTT * D], F32)
        xt = SB("xt", [128, NDC * TPC], BF16)
        scr = SB("scr", [128, D], BF16)
        lnout = [SB(f"lnout{b}", [128, D], BF16) for b in range(2)]
        stats = SB("stats", [128, 8 * NTT], F32)
        eps_t = SB("eps", [128, 1], F32)
        ident = SB("ident", [128, 128], BF16)
        identf = SB("identf", [128, 128], F32)
        starts_sb = SB("starts", [1, 16], F32)
        iota_sb = SB("iotas", [1, 4], F32)
        ohb = SB("ohb", [128, 8], F32)
        ones1 = SB("ones1", [1, 128], F32)
        asel = SB("asel", [128, NDC * 3 * RK], BF16)
        bsel = [SB(f"bsel{i}", [16, D], BF16) for i in range(3)]
        low = SB("low", [16, 3 * TPC], BF16)
        nrm = SB("nrm", [128, 48], F32)
        nrmf = SB("nrmf", [128, 128], F32)
        banks = [ctx.enter_context(nc.psum_tensor(f"bank{i}", [128, 512], F32)).ap()
                 for i in range(8)]

        E = mk_engs(nc, ctx)
        pe, act, dve, sp, gp = E

        MS = lambda nm: Stream(nc, ctx, nm)
        s_ld, s_ss, s_sa, s_bt, s_so = MS("ld"), MS("ss"), MS("sa"), MS("bt"), MS("so")
        s_q = [MS("q0"), MS("q1")]
        s_wc = MS("wc")
        ST = {k: MS(k) for k in
              ["wg0", "wg1", "wu0", "wu1"] + [f"wd{i}" for i in range(NG)]}
        for tt in range(NTT):
            s_ld.dma(sp, resid[:, tt * D:(tt + 1) * D], x_in[tt])
        s_ld.dma(sp, ident, id_d)
        s_ld.dma(sp, identf, idf_d)
        s_ld.dma(sp, iota_sb, io_d)
        s_ss.dma(gp, starts_sb[0:1, 0:4], st_d)
        dve.put(nc.vector.memset(eps_t, 1e-5))
        dve.put(nc.vector.memset(ones1, 1.0))
        dve.put(nc.vector.memset(nrm, 0.0))
        dve.put(nc.vector.memset(nrmf, 0.0))
        dve.sig()
        s_ld.waited_by(act)
        s_ld.waited_by(dve)

        xt_tts = emit_layernorm(nc, E, resid, xt, scr, stats, eps_t, ident,
                                banks[0:2], lnout, 0)
        xt_ready = xt_tts[-1]

        with ExitStack() as c3:
            gated = c3.enter_context(nc.sbuf_tensor("gated", [128, NG * TPC],
                                                    BF16)).ap()
            wg_sb = [c3.enter_context(nc.sbuf_tensor(f"wg{b}", [128, D], BF16)).ap()
                     for b in range(2)]
            wu_sb = [c3.enter_context(nc.sbuf_tensor(f"wu{b}", [128, D], BF16)).ap()
                     for b in range(2)]
            wd_sb = [c3.enter_context(nc.sbuf_tensor(f"wd{b}", [128, D], BF16)).ap()
                     for b in range(NG)]
            resid_ts_counts = [0] * NTT

            def _store_x1(ts, dcount):
                resid_ts_counts[ts] = dcount
                sp.wait(dve, dcount)
                s_so.dma(sp, x1_o[ts], resid[:, ts * D:(ts + 1) * D])
            ffn_done = emit_ffn(nc, E, ST, xt, resid, gated, wg_sb, wu_sb, wd_sb,
                                [banks[4], banks[5]], [banks[6], banks[7]],
                                banks[0:4], {"wg": wg_d, "wu": wu_d, "wd": wd_d},
                                xt_tts, store_cb=_store_x1)
            gp.e.wait_ge(dve.sem, ffn_done)

        xt2_tts = emit_layernorm(nc, E, resid, xt, scr, stats, eps_t, ident,
                                 banks[0:2], lnout, resid_ts_counts)
        xt2_ready = xt2_tts[-1]

        with ExitStack() as c2:
            a_sb = [c2.enter_context(nc.sbuf_tensor(f"asb{i}", [128, NDC * NA * RK],
                                                    BF16)).ap() for i in range(3)]
            b_all = [c2.enter_context(nc.sbuf_tensor(f"ball{i}", [16, NA * D],
                                                     BF16)).ap() for i in range(3)]
            for i, ad in enumerate([aq_d, ak_d, av_d]):
                s_sa.dma(gp, a_sb[i].rearrange("p (f) -> p f"),
                         ad.transpose([1, 0, 2, 3]))
            for i, bd in enumerate([bq_d, bk_d, bv_d]):
                s_sa.dma(gp, b_all[i].rearrange("p (a d) -> p a d", d=D),
                         bd.transpose([1, 0, 2]))
            emit_onehot(nc, E, starts_sb, iota_sb, ohb, ones1, banks[7], s_ld,
                        s_ss)
            s_sa.waited_by(dve)
            a_pairs = []
            for i in range(3):
                sl = asel.rearrange("p (c t r) -> p c t r", t=3, r=RK)[:, :, i, :]
                a_pairs.append((a_sb[i], sl))
            b_pairs = [(b_all[i], bsel[i]) for i in range(3)]
            a_done, sel_done = emit_ab_select(nc, E, ohb, a_pairs, b_pairs)
            gp.e.wait_ge(dve.sem, sel_done)

        pe.wait(dve, a_done)
        b_waited = False
        lo_ps = banks[2][0:16, :]
        for i in range(3):
            for tc in range(2):
                pe.wait(act, act.n)
                pe.wait(act, xt2_tts[4 * tc + 3])
                last = None
                for dc in range(NDC):
                    lh = asel.rearrange("p (c t r) -> p c t r", t=3, r=RK)[:, dc, i, :]
                    last = nc.tensor.matmul(
                        lo_ps, lh,
                        xt[:, dc * TPC + tc * 512: dc * TPC + tc * 512 + 512],
                        start=(dc == 0), stop=(dc == NDC - 1))
                pe.put(last)
                p_lo = pe.sig()
                act.wait(pe, p_lo)
                act.put(nc.scalar.copy(
                    low[0:16, i * TPC + tc * 512: i * TPC + tc * 512 + 512], lo_ps))
                act.sig()
        low_ready = act.n
        pe.wait(act, low_ready)

        with ExitStack() as c4:
            SB4 = lambda nm, sh, dt: c4.enter_context(nc.sbuf_tensor("sb_" + nm, sh, dt)).ap()
            wcol2 = [SB4(f"wcol{b}", [128, NDC * 512], BF16) for b in range(2)]
            cos_sb = SB4("cos", [128, NTT * 256], F32)
            sin_sb = SB4("sin", [128, NTT * 256], F32)
            ro_t = [[SB4(f"rot{s}_{i}", [128, 256], F32) for i in range(4)]
                    for s in range(3)]
            qkst = [SB4(f"qkst{b}", [128, 512], BF16) for b in range(3)]
            sqt = SB4("sqt", [128, 512], F32)
            sp.wait(dve, sel_done)
            for tt in range(NTT):
                s_ld.dma(sp, cos_sb[:, tt * 256:(tt + 1) * 256], cos_d[tt])
                s_ld.dma(sp, sin_sb[:, tt * 256:(tt + 1) * 256], sin_d[tt])
            s_ld.waited_by(dve)
            qkv_plan = [(wq_d, q_o, True, 0, 0), (wk_d, k_o, True, 1, 16),
                        (wv_d, v_o, False, 2, -1)]
            wcol_free = [0, 0]
            xt2_waited = [False] * NTT
            s_wc2 = [s_wc, Stream(nc, ctx, "wc2")]
            qk_banks = [banks[4], banks[5], banks[3]]
            ps_free = [0, 0, 0]
            st_free = [0, 0, 0]
            s_q = s_q + [Stream(nc, ctx, "q2")]
            nrm_done = 0
            blocks = [(w_d, out_d, rope, li, ncol, ow)
                      for w_d, out_d, rope, li, ncol in qkv_plan
                      for ow in range(4)]
            for bi, (w_d, out_d, rope, li, ncol, ow) in enumerate(blocks):
                    wb = bi % 2
                    if bi == 0:
                        gp.e.wait_ge(pe.sem, wcol_free[wb])
                        for dc in range(NDC):
                            s_wc2[wb].dma(gp, wcol2[wb][:, dc * 512:(dc + 1) * 512],
                                          w_d[dc, ow], max_dma_last_dim=8192)
                    if bi + 1 < len(blocks):
                        nw_d, _, _, _, _, now = blocks[bi + 1][0], None, None, None, None, blocks[bi + 1][5]
                        nwb = (bi + 1) % 2
                        gp.e.wait_ge(pe.sem, wcol_free[nwb])
                        for dc in range(NDC):
                            s_wc2[nwb].dma(gp, wcol2[nwb][:, dc * 512:(dc + 1) * 512],
                                           blocks[bi + 1][0][dc, now],
                                           max_dma_last_dim=8192)
                    wcol = wcol2[wb]
                    s_wc2[wb].waited_by(pe)
                    for ts in range(NTT):
                        sl = (bi * NTT + ts) % 3
                        ps = qk_banks[sl]
                        pe.wait(dve, ps_free[sl])
                        if xt2_waited[ts] is False:
                            pe.wait(act, xt2_tts[ts])
                            xt2_waited[ts] = True
                        for dc in range(NDC):
                            nc.tensor.matmul(
                                ps[:, :],
                                xt[:, dc * TPC + ts * 128: dc * TPC + (ts + 1) * 128],
                                wcol[:, dc * 512:(dc + 1) * 512],
                                start=(dc == 0), stop=False)
                        if not b_waited:
                            pe.wait(dve, sel_done)
                            b_waited = True
                        last = nc.tensor.matmul(
                            ps[:, :],
                            low[0:16, li * TPC + ts * 128: li * TPC + (ts + 1) * 128],
                            bsel[li][:, ow * 512:(ow + 1) * 512],
                            start=False, stop=True)
                        pe.put(last)
                        p_mm = pe.sig()
                        if ncol >= 0:
                            act.wait(pe, p_mm)
                            act.wait(dve, nrm_done)
                            act.put(nc.scalar.activation(sqt, ps[:, :], AF.Square))
                            a_sq = act.sig()
                            dve.wait(act, a_sq)
                            red = nrm[:, 40:44]
                            njcol = nrm[:, ncol + ow * 4: ncol + ow * 4 + 4]
                            dve.put(nc.vector.reduce_sum(
                                red, sqt.rearrange("p (h d) -> p h d", d=DH),
                                axis=AX.X))
                            nc.vector.drain()
                            dve.put(nc.vector.tensor_tensor(njcol, njcol, red,
                                                            ALU.max))
                            nc.vector.drain()
                            nrm_done = dve.sig()
                        stg = qkst[sl]
                        dve.wait(pe, p_mm)
                        s_q[sl].waited_by(dve)
                        if rope:
                            cs = cos_sb[:, ts * 256:(ts + 1) * 256]
                            sn = sin_sb[:, ts * 256:(ts + 1) * 256]
                            psv = ps.rearrange("p (r two) -> p r two", two=2)
                            stv = stg.rearrange("p (r two) -> p r two", two=2)
                            x1a, x2a = psv[:, :, 0], psv[:, :, 1]
                            o1a, o2a = stv[:, :, 0], stv[:, :, 1]
                            rt = ro_t[sl]
                            dve.put(nc.vector.tensor_tensor(rt[0], x1a, cs, ALU.mult))
                            dve.put(nc.vector.tensor_tensor(rt[1], x2a, sn, ALU.mult))
                            dve.put(nc.vector.tensor_tensor(rt[2], x1a, sn, ALU.mult))
                            dve.put(nc.vector.tensor_tensor(rt[3], x2a, cs, ALU.mult))
                            nc.vector.drain()
                            dve.put(nc.vector.tensor_sub(o1a, rt[0], rt[1]))
                            dve.put(nc.vector.tensor_add(o2a, rt[2], rt[3]))
                        else:
                            dve.put(nc.vector.tensor_copy(stg, ps[:, :]))
                        ps_free[sl] = d_st = dve.sig()
                        sp.wait(dve, d_st)
                        s_q[sl].dma(sp, out_d[ow * 4:(ow + 1) * 4,
                                              ts * 128:(ts + 1) * 128,
                                              :].transpose([1, 0, 2]),
                                    stg.rearrange("p (h d) -> p h d", d=DH))
                    wcol_free[wb] = pe.n

        dve.put(nc.vector.tensor_copy(nrmf[:, 0:48], nrm))
        d_nf = dve.sig()
        pe.wait(dve, d_nf)
        pe.wait(act, act.n)
        tpf = banks[0]
        pe.put(nc.tensor.transpose(tpf[:, 0:128], nrmf, identf))
        p_tn = pe.sig()
        dve.wait(pe, p_tn)
        dve.put(nc.vector.reduce_max(nrmf[0:32, 64:65], tpf[0:32, 0:128], axis=AX.X))
        d_mx = dve.sig()
        sp.wait(dve, d_mx)
        s_so.dma(sp, qn_o, nrmf[0:16, 64:65])
        s_so.dma(sp, kn_o, nrmf[16:32, 64:65])

        s_so.waited_by(sp)
        s_q[0].waited_by(sp)
        s_q[1].waited_by(sp)
        s_so.waited_by(gp)
        s_q[0].waited_by(gp)
        s_q[1].waited_by(gp)
    return nc


def build_l2():
    nc = bass.Bass("TRN2", target_bir_lowering=False)
    q_d = nc.dram_tensor("qh", [HPC, DH, S], BF16, kind="ExternalInput").ap()
    k_d = nc.dram_tensor("kh", [HPC, DH, S], BF16, kind="ExternalInput").ap()
    v_d = nc.dram_tensor("vh", [128, HPC, S // 128, DH], BF16,
                         kind="ExternalInput").ap()
    qn_d = nc.dram_tensor("qn", [1, HPC, NCORE], F32, kind="ExternalInput").ap()
    kn_d = nc.dram_tensor("kn", [1, HPC, NCORE], F32, kind="ExternalInput").ap()
    tri_d = nc.dram_tensor("tri", [128, 128], F32, kind="ExternalInput").ap()
    on_d = nc.dram_tensor("onesf", [1, 128], F32, kind="ExternalInput").ap()
    ob_d = nc.dram_tensor("onesb", [128, 1], BF16, kind="ExternalInput").ap()
    ao_o = nc.dram_tensor("ao", [HPC, DH, S], BF16, kind="ExternalOutput").ap()

    NQT = S // 512

    with ExitStack() as ctx:
        SB = lambda nm, sh, dt: ctx.enter_context(nc.sbuf_tensor("sb_" + nm, sh, dt)).ap()
        qs = SB("qs", [128, HPC * S], BF16)
        ks = SB("ks", [128, HPC * S], BF16)
        vs = SB("vs", [128, HPC * (S // 128) * DH], BF16)
        tri = SB("tri", [128, 128], F32)
        onesf = SB("onesf", [1, 128], F32)
        onesb = SB("onesb", [128, 1], BF16)
        nstat = SB("nstat", [1, 2 * NCORE + 8], F32)
        kn_sb = SB("knsb", [1, 2 * NCORE], F32)
        bias_sb = SB("bias", [128, 2], F32)
        pt = [SB(f"pt{i}", [128, 512], BF16) for i in range(4)]
        rsa = [SB(f"rsa{i}", [128, 512], F32) for i in range(2)]
        rsc = SB("rsc", [128, 512], BF16)
        rcp = [SB(f"rcp{i}", [1, 512], F32) for i in range(HPC)]
        rbs = SB("rbs", [128, 512], F32)
        aost = SB("aost", [128, 512], BF16)
        banks = [ctx.enter_context(nc.psum_tensor(f"bank{i}", [128, 512], F32)).ap()
                 for i in range(8)]
        st_ps = [banks[0], banks[1], banks[7]]
        pv_ps = banks[2:4]
        rs_ps = [banks[4][0:1, :], banks[5][0:1, :]]
        rb_ps = banks[6]

        E = mk_engs(nc, ctx)
        pe, act, dve, sp, gp = E

        s_ld = Stream(nc, ctx, "ld")
        s_ao = Stream(nc, ctx, "ao")
        s_h = [Stream(nc, ctx, f"h{i}") for i in range(2)]
        s_h2 = [Stream(nc, ctx, f"hb{i}") for i in range(2)]
        s_h0a = Stream(nc, ctx, "h0a")
        FC = 1024
        for hl in range(2):
            s_h0a.dma(sp, ks[:, hl * S: hl * S + FC], k_d[hl][:, 0:FC])
            s_h0a.dma(act, qs[:, hl * S: hl * S + FC], q_d[hl][:, 0:FC])
            s_h0a.dma(sp, vs[:, hl * (S // 128) * DH:
                             hl * (S // 128) * DH + (FC // 128) * DH],
                      v_d[:, hl, 0:FC // 128].rearrange("p k d -> p (k d)"))
        HS = S // 2
        for hl in range(2):
            s_h[hl].dma(sp, ks[:, hl * S + FC: hl * S + HS], k_d[hl][:, FC:HS])
            s_h[hl].dma(act, qs[:, hl * S + FC: hl * S + HS], q_d[hl][:, FC:HS])
            s_h[hl].dma(sp, vs[:, hl * (S // 128) * DH + (FC // 128) * DH:
                               hl * (S // 128) * DH + (HS // 128) * DH],
                        v_d[:, hl, FC // 128:HS // 128].rearrange("p k d -> p (k d)"))
        for hl in range(2):
            s_h2[hl].dma(sp, ks[:, hl * S + HS:(hl + 1) * S], k_d[hl][:, HS:S])
            s_h2[hl].dma(act, qs[:, hl * S + HS:(hl + 1) * S], q_d[hl][:, HS:S])
            s_h2[hl].dma(sp, vs[:, hl * (S // 128) * DH + (HS // 128) * DH:
                                (hl + 1) * (S // 128) * DH],
                         v_d[:, hl, HS // 128:].rearrange("p k d -> p (k d)"))
        s_ld.dma(sp, tri, tri_d)
        s_ld.dma(sp, onesf, on_d)
        s_ld.dma(sp, onesb, ob_d)
        s_ld.dma(sp, nstat[0:1, 0:2 * NCORE], qn_d.rearrange("a h c -> a (h c)"))
        s_ld.dma(sp, kn_sb, kn_d.rearrange("a h c -> a (h c)"))
        s_ld.waited_by(dve)
        s_ld.waited_by(pe)
        s_ld.waited_by(act)
        h_waited = [False, False]
        h2_waited = [False, False]
        h0a_waited = [False]
        mq = nstat[0:1, 2 * NCORE:2 * NCORE + 2]
        mk = nstat[0:1, 2 * NCORE + 2:2 * NCORE + 4]
        pr = nstat[0:1, 2 * NCORE + 4:2 * NCORE + 6]
        dve.put(nc.vector.reduce_max(
            mq, nstat[0:1, 0:2 * NCORE].rearrange("a (h c) -> a h c", c=NCORE),
            axis=AX.X))
        nc.vector.drain()
        dve.put(nc.vector.reduce_max(
            mk, kn_sb.rearrange("a (h c) -> a h c", c=NCORE), axis=AX.X))
        nc.vector.drain()
        dve.put(nc.vector.tensor_tensor(pr, mq, mk, ALU.mult))
        d_pr = dve.sig()
        act.wait(dve, d_pr)
        act.put(nc.scalar.activation(pr, pr, AF.Sqrt,
                                     scale=(1.02 * INVSQ) * (1.02 * INVSQ)))
        a_pr = act.sig()
        dve.wait(act, a_pr)
        dve.put(nc.vector.tensor_scalar_mul(pr, pr, -1.0))
        d_ng = dve.sig()
        pe.wait(dve, d_ng)
        pe.put(nc.tensor.matmul(rb_ps[:, 0:2], onesf[0:1, :], pr, start=True,
                                stop=True))
        p_b = pe.sig()
        dve.wait(pe, p_b)
        dve.put(nc.vector.tensor_copy(bias_sb, rb_ps[:, 0:2]))
        d_bias = dve.sig()
        pe.wait(dve, d_bias)
        act.wait(dve, d_bias)

        iters = []
        for qt in range(NQT):
            for hl in range(HPC):
                for kc in range(qt + 1):
                    diag = (kc == qt)
                    for ksub in range(4):
                        iters.append(dict(
                            qt=qt, hl=hl, kt=kc * 4 + ksub,
                            qoff=ksub * 128 if diag else 0,
                            diag=diag,
                            first=(kc == 0 and ksub == 0),
                            first3=(kc == 0 and ksub == 3),
                            lastmm=(diag and ksub == 3)))

        pt_free = [0, 0, 0, 0]
        pt_dve_free = [0, 0, 0, 0]
        rsc_free = [0]
        st_free = [0, 0, 0]
        pv_free = [0, 0]
        rs_free = [0, 0]
        rcp_free = [0, 0]
        rb_free = 0
        exp_of = {}

        def emit_st(i):
            it = iters[i]
            sl = i % 3
            psl = i % 4
            stp = st_ps[sl]
            qb = it["hl"] * S + it["qt"] * 512
            if not h0a_waited[0]:
                s_h0a.waited_by(pe)
                s_h0a.waited_by(act)
                h0a_waited[0] = True
            if not h_waited[it["hl"]] and (it["qt"] * 512 + 512 > FC
                                           or (it["kt"] + 1) * 128 > FC):
                s_h[it["hl"]].waited_by(pe)
                s_h[it["hl"]].waited_by(act)
                h_waited[it["hl"]] = True
            if not h2_waited[it["hl"]] and (it["qt"] * 512 + 512 > HS
                                            or (it["kt"] + 1) * 128 > HS):
                s_h2[it["hl"]].waited_by(pe)
                s_h2[it["hl"]].waited_by(act)
                h2_waited[it["hl"]] = True
            pe.wait(act, st_free[sl])
            if it["first"]:
                pe.wait(dve, pv_free[it["hl"]])
                pe.wait(dve, rs_free[it["hl"]])
            pe.put(nc.tensor.matmul(
                stp[:, it["qoff"]:512],
                ks[:, it["hl"] * S + it["kt"] * 128:
                   it["hl"] * S + (it["kt"] + 1) * 128],
                qs[:, qb + it["qoff"]: qb + 512], start=True, stop=True))
            p_st = pe.sig()
            if it["diag"]:
                dve.wait(pe, p_st)
                dsl = stp[:, it["qoff"]:it["qoff"] + 128]
                dve.put(nc.vector.tensor_tensor(dsl, dsl, tri, ALU.add))
                d_msk = dve.sig()
                act.wait(dve, d_msk)
            act.wait(pe, p_st)
            act.wait(pe, pt_free[psl])
            act.wait(dve, pt_dve_free[psl])
            act.put(nc.scalar.activation(
                pt[psl][:, it["qoff"]:512], stp[:, it["qoff"]:512], AF.Exp,
                bias=bias_sb[:, it["hl"]:it["hl"] + 1], scale=INVSQ))
            st_free[sl] = exp_of[i] = act.sig()

        def emit_pv(i):
            it = iters[i]
            psl = i % 4
            hl = it["hl"]
            pe.wait(act, exp_of[i])
            nc.tensor.matmul(
                pv_ps[hl][:, it["qoff"]:512],
                vs[:, (hl * (S // 128) + it["kt"]) * DH:
                   (hl * (S // 128) + it["kt"] + 1) * DH],
                pt[psl][:, it["qoff"]:512],
                start=it["first"], stop=it["lastmm"])
            last = nc.tensor.matmul(
                rs_ps[hl][:, it["qoff"]:512], onesb[:, 0:1],
                pt[psl][:, it["qoff"]:512],
                start=it["first"], stop=it["lastmm"])
            pe.put(last)
            pt_free[psl] = pe.sig()

        def emit_fin_a(i):
            it = iters[i]
            hl = it["hl"]
            p_pv = pe.n
            dve.wait(pe, p_pv)
            dve.wait(pe, rcp_free[hl])
            dve.put(nc.vector.reciprocal(rcp[hl], rs_ps[hl]))
            return dve.sig()

        def emit_fin_b(i, d_rcp):
            nonlocal rb_free
            it = iters[i]
            hl = it["hl"]
            qt = it["qt"]
            pe.wait(dve, d_rcp)
            pe.wait(dve, rb_free)
            pe.put(nc.tensor.matmul(rb_ps[:, :], onesf[0:1, :], rcp[hl],
                                    start=True, stop=True))
            rcp_free[hl] = p_rb = pe.sig()
            dve.wait(pe, p_rb)
            dve.put(nc.vector.tensor_copy(rbs, rb_ps[:, :]))
            rb_free = dve.sig()
            nc.vector.drain()
            s_ao.waited_by(dve)
            dve.put(nc.vector.tensor_tensor(aost, pv_ps[hl][:, :], rbs,
                                            ALU.mult))
            pv_free[hl] = rs_free[hl] = d_ao = dve.sig()
            sp.wait(dve, d_ao)
            s_ao.dma(sp, ao_o[hl][:, qt * 512:(qt + 1) * 512], aost)

        pending = []
        emit_st(0)
        emit_st(1)
        for i in range(len(iters)):
            while pending and pending[0][0] <= i:
                _, bi, d_rcp = pending.pop(0)
                emit_fin_b(bi, d_rcp)
            if i + 2 < len(iters):
                emit_st(i + 2)
            emit_pv(i)
            if iters[i]["lastmm"]:
                d_rcp = emit_fin_a(i)
                pending.append((i + 3, i, d_rcp))
        while pending:
            _, bi, d_rcp = pending.pop(0)
            emit_fin_b(bi, d_rcp)

        s_ao.waited_by(sp)
        s_ao.waited_by(gp)
    return nc


def build_l3():
    nc = bass.Bass("TRN2", target_bir_lowering=False)
    x1_in = nc.dram_tensor("x1", [NTT, 128, D], F32, kind="ExternalInput").ap()
    ao_in = nc.dram_tensor("aof", [NDC, 128, TPC], BF16, kind="ExternalInput").ap()
    wo_d = nc.dram_tensor("wo", [NDC, 128, D], F32, kind="ExternalInput").ap()
    ax_d = nc.dram_tensor("ao_a", [NDC, 128, NA, RK], F32, kind="ExternalInput").ap()
    bx_d = nc.dram_tensor("ao_b", [NA, RK, D], F32, kind="ExternalInput").ap()
    wg_d = nc.dram_tensor("wg", [NHC, 128, D], F32, kind="ExternalInput").ap()
    wu_d = nc.dram_tensor("wu", [NHC, 128, D], F32, kind="ExternalInput").ap()
    wd_d = nc.dram_tensor("wd", [NHC, 128, D], F32, kind="ExternalInput").ap()
    st_d = nc.dram_tensor("starts", [1, 4], I32, kind="ExternalInput").ap()
    io_d = nc.dram_tensor("iota", [1, 4], F32, kind="ExternalInput").ap()
    id_d = nc.dram_tensor("ident", [128, 128], BF16, kind="ExternalInput").ap()
    y_o = nc.dram_tensor("y", [NTT, 128, D], F32, kind="ExternalOutput").ap()

    with ExitStack() as ctx:
        SB = lambda nm, sh, dt: ctx.enter_context(nc.sbuf_tensor("sb_" + nm, sh, dt)).ap()
        resid = SB("resid", [128, NTT * D], F32)
        scr = SB("scr", [128, D], BF16)
        lnout = [SB(f"lnout{b}", [128, D], BF16) for b in range(2)]
        stats = SB("stats", [128, 8 * NTT], F32)
        eps_t = SB("eps", [128, 1], F32)
        ident = SB("ident", [128, 128], BF16)
        starts_sb = SB("starts", [1, 16], F32)
        iota_sb = SB("iotas", [1, 4], F32)
        ohb = SB("ohb", [128, 8], F32)
        ones1 = SB("ones1", [1, 128], F32)
        asel = SB("asel", [128, NDC * 3 * RK], BF16)
        bsel = SB("bsel", [16, D], BF16)
        low = SB("low", [16, TPC], BF16)
        banks = [ctx.enter_context(nc.psum_tensor(f"bank{i}", [128, 512], F32)).ap()
                 for i in range(8)]
        E = mk_engs(nc, ctx)
        pe, act, dve, sp, gp = E

        MS = lambda nm: Stream(nc, ctx, nm)
        s_ld, s_ss, s_sa, s_bt, s_so = MS("ld"), MS("ss"), MS("sa"), MS("bt"), MS("so")
        s_wa = MS("wa")
        s_aos = MS("aos")
        ST = {k: MS(k) for k in
              ["wg0", "wg1", "wu0", "wu1"] + [f"wd{i}" for i in range(NG)]}
        c2 = ctx.enter_context(ExitStack())
        SB2 = lambda nm, sh, dt: c2.enter_context(nc.sbuf_tensor("sb_" + nm, sh, dt)).ap()
        aos = SB2("aos", [128, NDC * TPC], BF16)
        s_aos.dma(sp, aos.rearrange("p (c t) -> p c t", t=TPC),
                  ao_in.transpose([1, 0, 2]))
        for tt in range(NTT):
            s_ld.dma(sp, resid[:, tt * D:(tt + 1) * D], x1_in[tt])
        s_ld.dma(sp, ident, id_d)
        s_ld.dma(sp, iota_sb, io_d)
        s_ss.dma(gp, starts_sb[0:1, 0:4], st_d)
        dve.put(nc.vector.memset(eps_t, 1e-5))
        dve.put(nc.vector.memset(ones1, 1.0))
        dve.sig()
        s_ld.waited_by(act)
        s_ld.waited_by(dve)

        if True:

            wall = [SB2(f"wall{i}", [128, D], BF16) for i in range(NDC)]
            for dc in range(NDC):
                s_wa.dma(gp, wall[dc], wo_d[dc], max_dma_last_dim=8192)
            a_sb = SB2("asb", [128, NDC * NA * RK], BF16)
            b_all = SB2("ball", [16, NA * D], BF16)
            s_sa.dma(gp, a_sb.rearrange("p (f) -> p f"), ax_d.transpose([1, 0, 2, 3]))
            s_sa.dma(gp, b_all.rearrange("p (a d) -> p a d", d=D),
                     bx_d.transpose([1, 0, 2]))
            emit_onehot(nc, E, starts_sb, iota_sb, ohb, ones1, banks[7], s_ld,
                        s_ss)
            s_sa.waited_by(dve)
            slA = asel.rearrange("p (c t r) -> p c t r", t=3, r=RK)[:, :, 0, :]
            emit_ab_select(nc, E, ohb, [(a_sb, slA)], [(b_all, bsel)])
            gp.e.wait_ge(dve.sem, dve.n)

            s_aos.waited_by(pe)
            pe.wait(dve, dve.n)
            lo_ps = banks[2][0:16, :]
            for tc in range(2):
                pe.wait(act, act.n)
                last = None
                for dc in range(NDC):
                    lh = asel.rearrange("p (c t r) -> p c t r", t=3, r=RK)[:, dc, 0, :]
                    last = nc.tensor.matmul(
                        lo_ps, lh,
                        aos[:, dc * TPC + tc * 512: dc * TPC + tc * 512 + 512],
                        start=(dc == 0), stop=(dc == NDC - 1))
                pe.put(last)
                p_lo = pe.sig()
                act.wait(pe, p_lo)
                act.put(nc.scalar.copy(low[0:16, tc * 512: tc * 512 + 512], lo_ps))
                act.sig()
            low_ready = act.n
            pe.wait(act, low_ready)

            o_done = [0] * 4
            resid_ts_counts = [0] * NTT
            wa_waited = 0
            low_waited = False
            for ts in range(NTT):
                for oc in range(4):
                    pe.wait(dve, o_done[oc])
                    for dc in range(NDC):
                        if wa_waited < (dc + 1) * 16:
                            wa_waited = (dc + 1) * 16
                            s_wa.waited_by(pe, wa_waited)
                        nc.tensor.matmul(
                            banks[oc][:, :],
                            aos[:, dc * TPC + ts * 128: dc * TPC + (ts + 1) * 128],
                            wall[dc][:, oc * 512:(oc + 1) * 512],
                            start=(dc == 0), stop=False)
                    if not low_waited:
                        pe.wait(act, low_ready)
                        low_waited = True
                    last = nc.tensor.matmul(
                        banks[oc][:, :], low[0:16, ts * 128:(ts + 1) * 128],
                        bsel[:, oc * 512:(oc + 1) * 512], start=False, stop=True)
                    pe.put(last)
                    p_o = pe.sig()
                    dve.wait(pe, p_o)
                    rsl = resid[:, ts * D + oc * 512: ts * D + oc * 512 + 512]
                    dve.put(nc.vector.tensor_tensor(rsl, rsl, banks[oc][:, :],
                                                    ALU.add))
                    o_done[oc] = dve.sig()
                resid_ts_counts[ts] = dve.n
            oproj_done = dve.n
            gp.e.wait_ge(dve.sem, oproj_done)
            sp.e.wait_ge(dve.sem, oproj_done)
        ctx.pop_all().close() if False else None
        c2.close()

        with ExitStack() as c3:
            SB3 = lambda nm, sh, dt: c3.enter_context(nc.sbuf_tensor("sb_" + nm, sh, dt)).ap()
            xt = SB3("xt", [128, NDC * TPC], BF16)
            xt_tts = emit_layernorm(nc, E, resid, xt, scr, stats, eps_t, ident,
                                    banks[0:2], lnout, resid_ts_counts)
            gated = SB3("gated", [128, NG * TPC], BF16)
            wg_sb = [SB3(f"wg{b}", [128, D], BF16) for b in range(2)]
            wu_sb = [SB3(f"wu{b}", [128, D], BF16) for b in range(2)]
            wd_sb = [SB3(f"wd{b}", [128, D], BF16) for b in range(NG)]
            def _store_y(ts, dcount):
                sp.wait(dve, dcount)
                s_so.dma(sp, y_o[ts], resid[:, ts * D:(ts + 1) * D])
            ffn_done = emit_ffn(nc, E, ST, xt, resid, gated, wg_sb, wu_sb, wd_sb,
                                [banks[4], banks[5]], [banks[6], banks[7]],
                                banks[0:4], {"wg": wg_d, "wu": wu_d, "wd": wd_d},
                                xt_tts, store_cb=_store_y)
            gp.e.wait_ge(dve.sem, ffn_done)
            s_so.waited_by(sp)
            s_so.waited_by(gp)
    return nc


def _rope_tables():
    pos = np.arange(S, dtype=np.float64)
    j = np.arange(64, dtype=np.float64)
    inv = 10000.0 ** (-2.0 * j / DH)
    ang = pos[:, None] * inv[None, :]
    cos4 = np.tile(np.cos(ang).astype(np.float32), (1, 4))
    sin4 = np.tile(np.sin(ang).astype(np.float32), (1, 4))
    return cos4, sin4


def _tri_mask():
    m = np.zeros((128, 128), np.float32)
    kk, qq = np.meshgrid(np.arange(128), np.arange(128), indexing="ij")
    m[qq < kk] = -1e9
    return m


def _stage_a(A):
    return np.ascontiguousarray(A.transpose(2, 0, 1).reshape(NDC, 128, NA, RK))


def _stage_b(B):
    return np.ascontiguousarray(B.transpose(0, 2, 1))


def _stage_wT(W):
    return np.ascontiguousarray(W.T.reshape(NDC, 128, W.shape[0]))


def _stage_w_colblocks(W):
    return np.ascontiguousarray(W.T.reshape(NDC, 128, 4, 512).transpose(0, 2, 1, 3))


def _stage_wgu(W):
    return np.ascontiguousarray(
        W.reshape(NHC, 128, NDC, 128).transpose(0, 3, 2, 1).reshape(NHC, 128, D))


def _stage_wd(W):
    return np.ascontiguousarray(W.T.reshape(NHC, 128, D))


_IDENT_BF = np.eye(128, dtype=BF)
_IDENT_F32 = np.eye(128, dtype=np.float32)
_IOTA = np.arange(4, dtype=np.float32).reshape(1, 4)
_CACHE = {}


def _get_ncs():
    if "l1" not in _CACHE:
        _CACHE["l1"] = build_l1()
        _CACHE["l2"] = build_l2()
        _CACHE["l3"] = build_l3()
    return _CACHE["l1"], _CACHE["l2"], _CACHE["l3"]


def kernel(L0_attn_x, L0_attn_lora_start_indices, L0_attn_Wq, L0_attn_Aq,
           L0_attn_Bq, L0_attn_Wk, L0_attn_Ak, L0_attn_Bk, L0_attn_Wv,
           L0_attn_Av, L0_attn_Bv, L0_attn_Wo, L0_attn_Ao, L0_attn_Bo,
           L0_ffn_w_gate, L0_ffn_w_up, L0_ffn_w_down):
    nc1, nc2, nc3 = _get_ncs()
    f32 = lambda a: np.ascontiguousarray(np.asarray(a, dtype=np.float32))
    x = f32(L0_attn_x).reshape(S, D)
    starts = np.ascontiguousarray(np.asarray(L0_attn_lora_start_indices,
                                             dtype=np.int32)).reshape(1, 4)

    wg = _stage_wgu(f32(L0_ffn_w_gate))
    wu = _stage_wgu(f32(L0_ffn_w_up))
    wd = _stage_wd(f32(L0_ffn_w_down))
    wq = _stage_w_colblocks(f32(L0_attn_Wq))
    wk = _stage_w_colblocks(f32(L0_attn_Wk))
    wv = _stage_w_colblocks(f32(L0_attn_Wv))
    wo = _stage_wT(f32(L0_attn_Wo))
    aq, ak, av, ao_a = (_stage_a(f32(a)) for a in
                        (L0_attn_Aq, L0_attn_Ak, L0_attn_Av, L0_attn_Ao))
    bq, bk, bv, bo_b = (_stage_b(f32(b)) for b in
                        (L0_attn_Bq, L0_attn_Bk, L0_attn_Bv, L0_attn_Bo))
    cos4, sin4 = _rope_tables()

    in1 = []
    for c in range(NCORE):
        tok = slice(c * TPC, (c + 1) * TPC)
        in1.append(dict(
            x=np.ascontiguousarray(x[tok].reshape(NTT, 128, D)),
            wg=wg, wu=wu, wd=wd, wq=wq, wk=wk, wv=wv,
            aq=aq, ak=ak, av=av, bq=bq, bk=bk, bv=bv,
            cosr=np.ascontiguousarray(cos4[tok].reshape(NTT, 128, 256)),
            sinr=np.ascontiguousarray(sin4[tok].reshape(NTT, 128, 256)),
            starts=starts, iota=_IOTA, ident=_IDENT_BF, identf=_IDENT_F32))
    r1 = run_bass_kernel_spmd(nc1, in1, core_ids=list(range(NCORE)))
    x1 = np.stack([r1.results[c]["x1"].reshape(TPC, D) for c in range(NCORE)])
    qo = np.stack([r1.results[c]["qo"] for c in range(NCORE)])
    ko = np.stack([r1.results[c]["ko"] for c in range(NCORE)])
    vo = np.stack([r1.results[c]["vo"] for c in range(NCORE)])
    qn = np.stack([r1.results[c]["qn"].reshape(H) for c in range(NCORE)])
    kn = np.stack([r1.results[c]["kn"].reshape(H) for c in range(NCORE)])

    q_full = np.ascontiguousarray(qo.transpose(1, 0, 2, 3).reshape(H, S, DH))
    k_full = np.ascontiguousarray(ko.transpose(1, 0, 2, 3).reshape(H, S, DH))
    v_full = np.ascontiguousarray(vo.transpose(1, 0, 2, 3).reshape(H, S, DH))
    tri = _tri_mask()
    onesf = np.ones((1, 128), np.float32)
    onesb = np.ones((128, 1), BF)
    in2 = []
    for c in range(NCORE):
        hs = slice(HPC * c, HPC * (c + 1))
        in2.append(dict(
            qh=np.ascontiguousarray(q_full[hs].transpose(0, 2, 1)),
            kh=np.ascontiguousarray(k_full[hs].transpose(0, 2, 1)),
            vh=np.ascontiguousarray(v_full[hs].reshape(HPC, S // 128, 128, DH)
                                    .transpose(2, 0, 1, 3)),
            qn=np.ascontiguousarray(qn.T[hs].reshape(1, HPC, NCORE)),
            kn=np.ascontiguousarray(kn.T[hs].reshape(1, HPC, NCORE)),
            tri=tri, onesf=onesf, onesb=onesb))
    r2 = run_bass_kernel_spmd(nc2, in2, core_ids=list(range(NCORE)))
    ao = np.stack([r2.results[c]["ao"] for c in range(NCORE)])
    ao_full = ao.reshape(H, DH, S)

    in3 = []
    for c in range(NCORE):
        tok = slice(c * TPC, (c + 1) * TPC)
        in3.append(dict(
            x1=np.ascontiguousarray(x1[c].reshape(NTT, 128, D)),
            aof=np.ascontiguousarray(ao_full[:, :, tok]),
            wo=wo, ao_a=ao_a, ao_b=bo_b, wg=wg, wu=wu, wd=wd,
            starts=starts, iota=_IOTA, ident=_IDENT_BF))
    r3 = run_bass_kernel_spmd(nc3, in3, core_ids=list(range(NCORE)))
    y = np.concatenate([r3.results[c]["y"].reshape(TPC, D) for c in range(NCORE)])
    return y.reshape(1, S, D)


# version 16
# speedup vs baseline: 1.0119x; 1.0003x over previous
import math
import numpy as np
import ml_dtypes
from contextlib import ExitStack

import concourse.bass as bass
import concourse.mybir as mybir
from concourse.bass_utils import run_bass_kernel_spmd

F32 = mybir.dt.float32
BF16 = mybir.dt.bfloat16
I32 = mybir.dt.int32
AF = mybir.ActivationFunctionType
ALU = mybir.AluOpType
AX = mybir.AxisListType

NCORE = 8
D = 2048
S = 8192
H = 16
DH = 128
HID = 8192
TPC = S // NCORE
NTT = TPC // 128
NDC = D // 128
NHC = HID // 128
NG = 8
HPC = H // NCORE
RK = 16
NA = 4
INVSQ = 1.0 / math.sqrt(DH)
BF = ml_dtypes.bfloat16


class Eng:

    def __init__(self, nc, ctx, name, eng):
        self.nc = nc
        self.e = eng
        self.sem = ctx.enter_context(nc.semaphore(f"sem_{name}"))
        self.n = 0
        self.last = None

    def put(self, instr):
        self.last = instr
        return instr

    def sig(self, k=1):
        self.last.then_inc(self.sem, k)
        self.n += k
        return self.n

    def wait(self, other, count):
        if count > 0:
            self.e.wait_ge(other.sem, count)


class Stream:

    def __init__(self, nc, ctx, name):
        self.sem = ctx.enter_context(nc.semaphore(f"st_{name}"))
        self.n = 0

    def dma(self, qeng, out, in_, **kw):
        qeng.e.dma_start(out, in_, **kw).then_inc(self.sem, 16)
        self.n += 16
        return self.n

    def waited_by(self, eng, count=None):
        eng.e.wait_ge(self.sem, self.n if count is None else count)


def mk_engs(nc, ctx):
    return (Eng(nc, ctx, "pe", nc.tensor), Eng(nc, ctx, "act", nc.scalar),
            Eng(nc, ctx, "dve", nc.vector), Eng(nc, ctx, "sp", nc.sync),
            Eng(nc, ctx, "gp", nc.gpsimd))


def emit_layernorm(nc, E, resid, xt, scr, stats, eps_t, ident, banks, lnout,
                   resid_dve_ready):
    if not isinstance(resid_dve_ready, (list, tuple)):
        resid_dve_ready = [resid_dve_ready] * NTT
    """LayerNorm of the NTT resident token tiles into transposed bf16 xt.

    resid [128, NTT*D] f32 -> xt [128, NDC*TPC] bf16 (X^T, feature-major).
    banks: two f32 [128,512] psum banks (bitcast bf16) for PE transposes.
    Returns act count at which xt is fully written.
    """
    pe, act, dve, sp, gp = E
    sums = stats[:, 0:NTT]
    sumsq = stats[:, NTT:2 * NTT]
    s2 = stats[:, 2 * NTT:3 * NTT]
    vpe = stats[:, 3 * NTT:4 * NTT]
    sq = stats[:, 4 * NTT:5 * NTT]
    rstd = stats[:, 5 * NTT:6 * NTT]
    nmr = stats[:, 6 * NTT:7 * NTT]

    for tt in range(NTT):
        act.wait(dve, resid_dve_ready[tt])
        act.put(nc.scalar.activation(scr[:, :], resid[:, tt * D:(tt + 1) * D],
                                     AF.Square, accum_out=sumsq[:, tt:tt + 1]))
        nc.scalar.drain()
    for tt in range(NTT):
        dve.wait(dve, resid_dve_ready[tt])
        dve.put(nc.vector.reduce_sum(sums[:, tt:tt + 1],
                                     resid[:, tt * D:(tt + 1) * D], axis=AX.X))
        nc.vector.drain()
    d_sums = dve.sig()
    act.wait(dve, d_sums)
    act.put(nc.scalar.activation(s2, sums, AF.Square, scale=1.0 / D))
    a_stats = act.sig()
    dve.wait(act, a_stats)
    dve.put(nc.vector.scalar_tensor_tensor(vpe, sumsq, 1.0 / D, s2, ALU.mult,
                                           ALU.subtract))
    d_vpe = dve.sig()
    act.wait(dve, d_vpe)
    act.put(nc.scalar.activation(sq, vpe, AF.Sqrt, bias=eps_t[:, 0:1]))
    a_sq = act.sig()
    dve.wait(act, a_sq)
    dve.put(nc.vector.reciprocal(rstd, sq))
    nc.vector.drain()
    dve.put(nc.vector.scalar_tensor_tensor(nmr, sums, -1.0 / D, rstd, ALU.mult,
                                           ALU.mult))
    nc.vector.drain()
    dve.sig()

    slot_evac = [0, 0]
    xt_tt_counts = [0] * NTT
    for tt in range(NTT):
        lo = lnout[tt % 2]
        dve.wait(pe, pe.n)
        dve.put(nc.vector.tensor_scalar(lo[:, :], resid[:, tt * D:(tt + 1) * D],
                                        rstd[:, tt:tt + 1], nmr[:, tt:tt + 1],
                                        ALU.mult, ALU.add))
        d_ln = dve.sig()
        for dc in range(NDC):
            sl = dc % 2
            ps = banks[sl].bitcast(BF16)[:, 0:128]
            pe.wait(act, slot_evac[sl])
            pe.wait(dve, d_ln)
            pe.put(nc.tensor.transpose(ps, lo[:, dc * 128:(dc + 1) * 128], ident))
            p_tp = pe.sig()
            act.wait(pe, p_tp)
            act.put(nc.scalar.copy(
                xt[:, dc * TPC + tt * 128: dc * TPC + (tt + 1) * 128], ps))
            slot_evac[sl] = act.sig()
        xt_tt_counts[tt] = act.n
    return xt_tt_counts


def emit_ffn(nc, E, ST, xt, resid, gated, wg_sb, wu_sb, wd_sb, gps, ups, dps,
             wst, xt_act_ready, store_cb=None):
    pe, act, dve, sp, gp = E
    NGRP = NHC // NG
    gate_done = [0, 0]
    down_done = [0] * 4
    wbuf_free = [0, 0]
    wd_free = [0] * NG
    swg = [ST["wg0"], ST["wg1"]]
    swu = [ST["wu0"], ST["wu1"]]
    swd = [ST[f"wd{i}"] for i in range(NG)]
    if isinstance(xt_act_ready, (list, tuple)):
        xt_tc = [xt_act_ready[3], xt_act_ready[7]]
    else:
        xt_tc = [xt_act_ready, xt_act_ready]
    waited_tc = [False, False]
    for g in range(NGRP):
        for hc in range(NG):
            hcg = g * NG + hc
            b = hc % 2
            gp.e.wait_ge(pe.sem, wbuf_free[b])
            swg[b].dma(gp, wg_sb[b][:, :], wst["wg"][hcg], max_dma_last_dim=8192)
            swu[b].dma(gp, wu_sb[b][:, :], wst["wu"][hcg], max_dma_last_dim=8192)
            gp.e.wait_ge(pe.sem, wd_free[hc])
            swd[hc].dma(gp, wd_sb[hc][:, :], wst["wd"][hcg],
                        max_dma_last_dim=8192)
            for tc in range(2):
                sl = tc
                if tc == 0:
                    swg[b].waited_by(pe)
                    swu[b].waited_by(pe)
                if not waited_tc[tc]:
                    pe.wait(act, xt_tc[tc])
                    waited_tc[tc] = True
                pe.wait(dve, gate_done[sl])
                for dc in range(NDC):
                    nc.tensor.matmul(
                        gps[sl][:, :], wg_sb[b][:, dc * 128:(dc + 1) * 128],
                        xt[:, dc * TPC + tc * 512: dc * TPC + tc * 512 + 512],
                        start=(dc == 0), stop=(dc == NDC - 1))
                last = None
                for dc in range(NDC):
                    last = nc.tensor.matmul(
                        ups[sl][:, :], wu_sb[b][:, dc * 128:(dc + 1) * 128],
                        xt[:, dc * TPC + tc * 512: dc * TPC + tc * 512 + 512],
                        start=(dc == 0), stop=(dc == NDC - 1))
                pe.put(last)
                p_mm = pe.sig()
                act.wait(pe, p_mm)
                gsl = gated[:, hc * TPC + tc * 512: hc * TPC + tc * 512 + 512]
                act.put(nc.scalar.activation(gsl, gps[sl][:, :], AF.Silu))
                a_silu = act.sig()
                dve.wait(act, a_silu)
                dve.put(nc.vector.scalar_tensor_tensor(gsl, gsl, 0.5, ups[sl][:, :],
                                                       ALU.mult, ALU.mult))
                gate_done[sl] = dve.sig()
            wbuf_free[b] = pe.n
        d_gate = dve.n
        pe.wait(dve, d_gate)
        if not waited_tc[1]:
            pe.wait(act, xt_tc[1])
            waited_tc[1] = True
        for hc in range(NG):
            swd[hc].waited_by(pe)
        for ts in range(NTT):
            for oc in range(4):
                pe.wait(dve, down_done[oc])
                last = None
                for hc in range(NG):
                    last = nc.tensor.matmul(
                        dps[oc][:, :],
                        gated[:, hc * TPC + ts * 128: hc * TPC + (ts + 1) * 128],
                        wd_sb[hc][:, oc * 512:(oc + 1) * 512],
                        start=(hc == 0), stop=(hc == NG - 1))
                pe.put(last)
                p_dn = pe.sig()
                dve.wait(pe, p_dn)
                rsl = resid[:, ts * D + oc * 512: ts * D + oc * 512 + 512]
                dve.put(nc.vector.tensor_tensor(rsl, rsl, dps[oc][:, :], ALU.add))
                down_done[oc] = dve.sig()
            if store_cb is not None and g == NGRP - 1:
                store_cb(ts, dve.n)
        for hc in range(NG):
            wd_free[hc] = pe.n
    return dve.n


def emit_onehot(nc, E, starts_sb, iota_sb, ohb, ones1, oh_ps, s_ld, s_ss):
    pe, act, dve, sp, gp = E
    s_ld.waited_by(dve)
    s_ss.waited_by(dve)
    cnt = starts_sb[0:1, 4:5]
    le = starts_sb[0:1, 12:16]
    ohr = starts_sb[0:1, 8:12]
    dve.put(nc.vector.tensor_single_scalar(le, starts_sb[0:1, 0:4], 0.0, ALU.is_le))
    nc.vector.drain()
    dve.put(nc.vector.reduce_sum(cnt, le, axis=AX.X))
    nc.vector.drain()
    dve.put(nc.vector.tensor_scalar_add(cnt, cnt, -1.0))
    nc.vector.drain()
    dve.put(nc.vector.tensor_relu(cnt, cnt))
    nc.vector.drain()
    dve.put(nc.vector.tensor_scalar(ohr, iota_sb[0:1, 0:4], cnt[0:1, 0:1], None,
                                    ALU.subtract, ALU.bypass))
    nc.vector.drain()
    dve.put(nc.vector.tensor_tensor(ohr, ohr, ohr, ALU.mult))
    nc.vector.drain()
    dve.put(nc.vector.tensor_scalar(ohr, ohr, -1.0, 1.0, ALU.mult, ALU.add))
    nc.vector.drain()
    dve.put(nc.vector.tensor_relu(ohr, ohr))
    d_oh = dve.sig()
    pe.wait(dve, d_oh)
    pe.put(nc.tensor.matmul(oh_ps[:, 0:4], ones1[0:1, :], ohr, start=True,
                            stop=True))
    p_oh = pe.sig()
    dve.wait(pe, p_oh)
    dve.put(nc.vector.tensor_copy(ohb[:, 0:4], oh_ps[:, 0:4]))
    nc.vector.drain()
    return dve.sig()


def emit_ab_select(nc, E, ohb, a_pairs, b_pairs):
    pe, act, dve, sp, gp = E
    for a_sb, asl in a_pairs:
        for a in range(NA):
            src = a_sb.rearrange("p (c a r) -> p c a r", a=NA, r=RK)[:, :, a, :]
            if a == 0:
                dve.put(nc.vector.tensor_scalar_mul(asl, src, ohb[:, 0:1]))
            else:
                dve.put(nc.vector.scalar_tensor_tensor(asl, src, ohb[:, a:a + 1],
                                                       asl, ALU.mult, ALU.add))
            nc.vector.drain()
    a_done = dve.sig()
    for b_all, bsel in b_pairs:
        for a in range(NA):
            src = b_all[:, a * D:(a + 1) * D]
            if a == 0:
                dve.put(nc.vector.tensor_scalar_mul(bsel[:, :], src,
                                                    ohb[0:16, 0:1]))
            else:
                dve.put(nc.vector.scalar_tensor_tensor(bsel[:, :], src,
                                                       ohb[0:16, a:a + 1],
                                                       bsel[:, :], ALU.mult,
                                                       ALU.add))
            nc.vector.drain()
    return a_done, dve.sig()


def build_l1():
    nc = bass.Bass("TRN2", target_bir_lowering=False)
    x_in = nc.dram_tensor("x", [NTT, 128, D], F32, kind="ExternalInput").ap()
    wg_d = nc.dram_tensor("wg", [NHC, 128, D], F32, kind="ExternalInput").ap()
    wu_d = nc.dram_tensor("wu", [NHC, 128, D], F32, kind="ExternalInput").ap()
    wd_d = nc.dram_tensor("wd", [NHC, 128, D], F32, kind="ExternalInput").ap()
    wq_d = nc.dram_tensor("wq", [NDC, 4, 128, 512], F32, kind="ExternalInput").ap()
    wk_d = nc.dram_tensor("wk", [NDC, 4, 128, 512], F32, kind="ExternalInput").ap()
    wv_d = nc.dram_tensor("wv", [NDC, 4, 128, 512], F32, kind="ExternalInput").ap()
    aq_d = nc.dram_tensor("aq", [NDC, 128, NA, RK], F32, kind="ExternalInput").ap()
    ak_d = nc.dram_tensor("ak", [NDC, 128, NA, RK], F32, kind="ExternalInput").ap()
    av_d = nc.dram_tensor("av", [NDC, 128, NA, RK], F32, kind="ExternalInput").ap()
    bq_d = nc.dram_tensor("bq", [NA, RK, D], F32, kind="ExternalInput").ap()
    bk_d = nc.dram_tensor("bk", [NA, RK, D], F32, kind="ExternalInput").ap()
    bv_d = nc.dram_tensor("bv", [NA, RK, D], F32, kind="ExternalInput").ap()
    cos_d = nc.dram_tensor("cosr", [NTT, 128, 256], F32, kind="ExternalInput").ap()
    sin_d = nc.dram_tensor("sinr", [NTT, 128, 256], F32, kind="ExternalInput").ap()
    st_d = nc.dram_tensor("starts", [1, 4], I32, kind="ExternalInput").ap()
    io_d = nc.dram_tensor("iota", [1, 4], F32, kind="ExternalInput").ap()
    id_d = nc.dram_tensor("ident", [128, 128], BF16, kind="ExternalInput").ap()
    idf_d = nc.dram_tensor("identf", [128, 128], F32, kind="ExternalInput").ap()

    x1_o = nc.dram_tensor("x1", [NTT, 128, D], F32, kind="ExternalOutput").ap()
    q_o = nc.dram_tensor("qo", [H, TPC, DH], BF16, kind="ExternalOutput").ap()
    k_o = nc.dram_tensor("ko", [H, TPC, DH], BF16, kind="ExternalOutput").ap()
    v_o = nc.dram_tensor("vo", [H, TPC, DH], BF16, kind="ExternalOutput").ap()
    qn_o = nc.dram_tensor("qn", [H, 1], F32, kind="ExternalOutput").ap()
    kn_o = nc.dram_tensor("kn", [H, 1], F32, kind="ExternalOutput").ap()

    with ExitStack() as ctx:
        SB = lambda nm, sh, dt: ctx.enter_context(nc.sbuf_tensor("sb_" + nm, sh, dt)).ap()
        resid = SB("resid", [128, NTT * D], F32)
        xt = SB("xt", [128, NDC * TPC], BF16)
        scr = SB("scr", [128, D], BF16)
        lnout = [SB(f"lnout{b}", [128, D], BF16) for b in range(2)]
        stats = SB("stats", [128, 8 * NTT], F32)
        eps_t = SB("eps", [128, 1], F32)
        ident = SB("ident", [128, 128], BF16)
        identf = SB("identf", [128, 128], F32)
        starts_sb = SB("starts", [1, 16], F32)
        iota_sb = SB("iotas", [1, 4], F32)
        ohb = SB("ohb", [128, 8], F32)
        ones1 = SB("ones1", [1, 128], F32)
        asel = SB("asel", [128, NDC * 3 * RK], BF16)
        bsel = [SB(f"bsel{i}", [16, D], BF16) for i in range(3)]
        low = SB("low", [16, 3 * TPC], BF16)
        nrm = SB("nrm", [128, 48], F32)
        nrmf = SB("nrmf", [128, 128], F32)
        banks = [ctx.enter_context(nc.psum_tensor(f"bank{i}", [128, 512], F32)).ap()
                 for i in range(8)]

        E = mk_engs(nc, ctx)
        pe, act, dve, sp, gp = E

        MS = lambda nm: Stream(nc, ctx, nm)
        s_ld, s_ss, s_sa, s_bt, s_so = MS("ld"), MS("ss"), MS("sa"), MS("bt"), MS("so")
        s_q = [MS("q0"), MS("q1")]
        s_wc = MS("wc")
        ST = {k: MS(k) for k in
              ["wg0", "wg1", "wu0", "wu1"] + [f"wd{i}" for i in range(NG)]}
        for tt in range(NTT):
            s_ld.dma(sp, resid[:, tt * D:(tt + 1) * D], x_in[tt])
        s_ld.dma(sp, ident, id_d)
        s_ld.dma(sp, identf, idf_d)
        s_ld.dma(sp, iota_sb, io_d)
        s_ss.dma(gp, starts_sb[0:1, 0:4], st_d)
        dve.put(nc.vector.memset(eps_t, 1e-5))
        dve.put(nc.vector.memset(ones1, 1.0))
        dve.put(nc.vector.memset(nrm, 0.0))
        dve.put(nc.vector.memset(nrmf, 0.0))
        dve.sig()
        s_ld.waited_by(act)
        s_ld.waited_by(dve)

        xt_tts = emit_layernorm(nc, E, resid, xt, scr, stats, eps_t, ident,
                                banks[0:2], lnout, 0)
        xt_ready = xt_tts[-1]

        with ExitStack() as c3:
            gated = c3.enter_context(nc.sbuf_tensor("gated", [128, NG * TPC],
                                                    BF16)).ap()
            wg_sb = [c3.enter_context(nc.sbuf_tensor(f"wg{b}", [128, D], BF16)).ap()
                     for b in range(2)]
            wu_sb = [c3.enter_context(nc.sbuf_tensor(f"wu{b}", [128, D], BF16)).ap()
                     for b in range(2)]
            wd_sb = [c3.enter_context(nc.sbuf_tensor(f"wd{b}", [128, D], BF16)).ap()
                     for b in range(NG)]
            resid_ts_counts = [0] * NTT

            def _store_x1(ts, dcount):
                resid_ts_counts[ts] = dcount
                sp.wait(dve, dcount)
                s_so.dma(sp, x1_o[ts], resid[:, ts * D:(ts + 1) * D])
            ffn_done = emit_ffn(nc, E, ST, xt, resid, gated, wg_sb, wu_sb, wd_sb,
                                [banks[4], banks[5]], [banks[6], banks[7]],
                                banks[0:4], {"wg": wg_d, "wu": wu_d, "wd": wd_d},
                                xt_tts, store_cb=_store_x1)
            gp.e.wait_ge(dve.sem, ffn_done)

        xt2_tts = emit_layernorm(nc, E, resid, xt, scr, stats, eps_t, ident,
                                 banks[0:2], lnout, resid_ts_counts)
        xt2_ready = xt2_tts[-1]

        with ExitStack() as c2:
            a_sb = [c2.enter_context(nc.sbuf_tensor(f"asb{i}", [128, NDC * NA * RK],
                                                    BF16)).ap() for i in range(3)]
            b_all = [c2.enter_context(nc.sbuf_tensor(f"ball{i}", [16, NA * D],
                                                     BF16)).ap() for i in range(3)]
            for i, ad in enumerate([aq_d, ak_d, av_d]):
                s_sa.dma(gp, a_sb[i].rearrange("p (f) -> p f"),
                         ad.transpose([1, 0, 2, 3]))
            for i, bd in enumerate([bq_d, bk_d, bv_d]):
                s_sa.dma(gp, b_all[i].rearrange("p (a d) -> p a d", d=D),
                         bd.transpose([1, 0, 2]))
            emit_onehot(nc, E, starts_sb, iota_sb, ohb, ones1, banks[7], s_ld,
                        s_ss)
            s_sa.waited_by(dve)
            a_pairs = []
            for i in range(3):
                sl = asel.rearrange("p (c t r) -> p c t r", t=3, r=RK)[:, :, i, :]
                a_pairs.append((a_sb[i], sl))
            b_pairs = [(b_all[i], bsel[i]) for i in range(3)]
            a_done, sel_done = emit_ab_select(nc, E, ohb, a_pairs, b_pairs)
            gp.e.wait_ge(dve.sem, sel_done)

        pe.wait(dve, a_done)
        b_waited = False
        lo_ps = banks[2][0:16, :]
        for i in range(3):
            for tc in range(2):
                pe.wait(act, act.n)
                pe.wait(act, xt2_tts[4 * tc + 3])
                last = None
                for dc in range(NDC):
                    lh = asel.rearrange("p (c t r) -> p c t r", t=3, r=RK)[:, dc, i, :]
                    last = nc.tensor.matmul(
                        lo_ps, lh,
                        xt[:, dc * TPC + tc * 512: dc * TPC + tc * 512 + 512],
                        start=(dc == 0), stop=(dc == NDC - 1))
                pe.put(last)
                p_lo = pe.sig()
                act.wait(pe, p_lo)
                act.put(nc.scalar.copy(
                    low[0:16, i * TPC + tc * 512: i * TPC + tc * 512 + 512], lo_ps))
                act.sig()
        low_ready = act.n
        pe.wait(act, low_ready)

        with ExitStack() as c4:
            SB4 = lambda nm, sh, dt: c4.enter_context(nc.sbuf_tensor("sb_" + nm, sh, dt)).ap()
            wcol2 = [SB4(f"wcol{b}", [128, NDC * 512], BF16) for b in range(2)]
            cos_sb = SB4("cos", [128, NTT * 256], F32)
            sin_sb = SB4("sin", [128, NTT * 256], F32)
            ro_t = [[SB4(f"rot{s}_{i}", [128, 256], F32) for i in range(4)]
                    for s in range(3)]
            qkst = [SB4(f"qkst{b}", [128, 512], BF16) for b in range(3)]
            sqt = SB4("sqt", [128, 512], F32)
            sp.wait(dve, sel_done)
            for tt in range(NTT):
                s_ld.dma(sp, cos_sb[:, tt * 256:(tt + 1) * 256], cos_d[tt])
                s_ld.dma(sp, sin_sb[:, tt * 256:(tt + 1) * 256], sin_d[tt])
            s_ld.waited_by(dve)
            qkv_plan = [(wq_d, q_o, True, 0, 0), (wk_d, k_o, True, 1, 16),
                        (wv_d, v_o, False, 2, -1)]
            wcol_free = [0, 0]
            xt2_waited = [False] * NTT
            s_wc2 = [s_wc, Stream(nc, ctx, "wc2")]
            qk_banks = [banks[4], banks[5], banks[3]]
            ps_free = [0, 0, 0]
            st_free = [0, 0, 0]
            s_q = s_q + [Stream(nc, ctx, "q2")]
            nrm_done = 0
            blocks = [(w_d, out_d, rope, li, ncol, ow)
                      for w_d, out_d, rope, li, ncol in qkv_plan
                      for ow in range(4)]
            for bi, (w_d, out_d, rope, li, ncol, ow) in enumerate(blocks):
                    wb = bi % 2
                    if bi == 0:
                        gp.e.wait_ge(pe.sem, wcol_free[wb])
                        for dc in range(NDC):
                            s_wc2[wb].dma(gp, wcol2[wb][:, dc * 512:(dc + 1) * 512],
                                          w_d[dc, ow], max_dma_last_dim=8192)
                    if bi + 1 < len(blocks):
                        nw_d, _, _, _, _, now = blocks[bi + 1][0], None, None, None, None, blocks[bi + 1][5]
                        nwb = (bi + 1) % 2
                        gp.e.wait_ge(pe.sem, wcol_free[nwb])
                        for dc in range(NDC):
                            s_wc2[nwb].dma(gp, wcol2[nwb][:, dc * 512:(dc + 1) * 512],
                                           blocks[bi + 1][0][dc, now],
                                           max_dma_last_dim=8192)
                    wcol = wcol2[wb]
                    s_wc2[wb].waited_by(pe)
                    for ts in range(NTT):
                        sl = (bi * NTT + ts) % 3
                        ps = qk_banks[sl]
                        pe.wait(dve, ps_free[sl])
                        if xt2_waited[ts] is False:
                            pe.wait(act, xt2_tts[ts])
                            xt2_waited[ts] = True
                        for dc in range(NDC):
                            nc.tensor.matmul(
                                ps[:, :],
                                xt[:, dc * TPC + ts * 128: dc * TPC + (ts + 1) * 128],
                                wcol[:, dc * 512:(dc + 1) * 512],
                                start=(dc == 0), stop=False)
                        if not b_waited:
                            pe.wait(dve, sel_done)
                            b_waited = True
                        last = nc.tensor.matmul(
                            ps[:, :],
                            low[0:16, li * TPC + ts * 128: li * TPC + (ts + 1) * 128],
                            bsel[li][:, ow * 512:(ow + 1) * 512],
                            start=False, stop=True)
                        pe.put(last)
                        p_mm = pe.sig()
                        if ncol >= 0:
                            act.wait(pe, p_mm)
                            act.wait(dve, nrm_done)
                            act.put(nc.scalar.activation(sqt, ps[:, :], AF.Square))
                            a_sq = act.sig()
                            dve.wait(act, a_sq)
                            red = nrm[:, 40:44]
                            njcol = nrm[:, ncol + ow * 4: ncol + ow * 4 + 4]
                            dve.put(nc.vector.reduce_sum(
                                red, sqt.rearrange("p (h d) -> p h d", d=DH),
                                axis=AX.X))
                            nc.vector.drain()
                            dve.put(nc.vector.tensor_tensor(njcol, njcol, red,
                                                            ALU.max))
                            nc.vector.drain()
                            nrm_done = dve.sig()
                        stg = qkst[sl]
                        dve.wait(pe, p_mm)
                        s_q[sl].waited_by(dve)
                        if rope:
                            cs = cos_sb[:, ts * 256:(ts + 1) * 256]
                            sn = sin_sb[:, ts * 256:(ts + 1) * 256]
                            psv = ps.rearrange("p (r two) -> p r two", two=2)
                            stv = stg.rearrange("p (r two) -> p r two", two=2)
                            x1a, x2a = psv[:, :, 0], psv[:, :, 1]
                            o1a, o2a = stv[:, :, 0], stv[:, :, 1]
                            rt = ro_t[sl]
                            dve.put(nc.vector.tensor_tensor(rt[0], x1a, cs, ALU.mult))
                            dve.put(nc.vector.tensor_tensor(rt[1], x2a, sn, ALU.mult))
                            dve.put(nc.vector.tensor_tensor(rt[2], x1a, sn, ALU.mult))
                            dve.put(nc.vector.tensor_tensor(rt[3], x2a, cs, ALU.mult))
                            nc.vector.drain()
                            dve.put(nc.vector.tensor_sub(o1a, rt[0], rt[1]))
                            dve.put(nc.vector.tensor_add(o2a, rt[2], rt[3]))
                        else:
                            dve.put(nc.vector.tensor_copy(stg, ps[:, :]))
                        ps_free[sl] = d_st = dve.sig()
                        sp.wait(dve, d_st)
                        s_q[sl].dma(sp, out_d[ow * 4:(ow + 1) * 4,
                                              ts * 128:(ts + 1) * 128,
                                              :].transpose([1, 0, 2]),
                                    stg.rearrange("p (h d) -> p h d", d=DH))
                    wcol_free[wb] = pe.n

        dve.put(nc.vector.tensor_copy(nrmf[:, 0:48], nrm))
        d_nf = dve.sig()
        pe.wait(dve, d_nf)
        pe.wait(act, act.n)
        tpf = banks[0]
        pe.put(nc.tensor.transpose(tpf[:, 0:128], nrmf, identf))
        p_tn = pe.sig()
        dve.wait(pe, p_tn)
        dve.put(nc.vector.reduce_max(nrmf[0:32, 64:65], tpf[0:32, 0:128], axis=AX.X))
        d_mx = dve.sig()
        sp.wait(dve, d_mx)
        s_so.dma(sp, qn_o, nrmf[0:16, 64:65])
        s_so.dma(sp, kn_o, nrmf[16:32, 64:65])

        s_so.waited_by(sp)
        s_q[0].waited_by(sp)
        s_q[1].waited_by(sp)
        s_so.waited_by(gp)
        s_q[0].waited_by(gp)
        s_q[1].waited_by(gp)
    return nc


def build_l2():
    nc = bass.Bass("TRN2", target_bir_lowering=False)
    q_d = nc.dram_tensor("qh", [HPC, DH, S], BF16, kind="ExternalInput").ap()
    k_d = nc.dram_tensor("kh", [HPC, DH, S], BF16, kind="ExternalInput").ap()
    v_d = nc.dram_tensor("vh", [128, HPC, S // 128, DH], BF16,
                         kind="ExternalInput").ap()
    qn_d = nc.dram_tensor("qn", [1, HPC, NCORE], F32, kind="ExternalInput").ap()
    kn_d = nc.dram_tensor("kn", [1, HPC, NCORE], F32, kind="ExternalInput").ap()
    tri_d = nc.dram_tensor("tri", [128, 128], F32, kind="ExternalInput").ap()
    on_d = nc.dram_tensor("onesf", [1, 128], F32, kind="ExternalInput").ap()
    ob_d = nc.dram_tensor("onesb", [128, 1], BF16, kind="ExternalInput").ap()
    ao_o = nc.dram_tensor("ao", [HPC, DH, S], BF16, kind="ExternalOutput").ap()

    NQT = S // 512

    with ExitStack() as ctx:
        SB = lambda nm, sh, dt: ctx.enter_context(nc.sbuf_tensor("sb_" + nm, sh, dt)).ap()
        qs = SB("qs", [128, HPC * S], BF16)
        ks = SB("ks", [128, HPC * S], BF16)
        vs = SB("vs", [128, HPC * (S // 128) * DH], BF16)
        tri = SB("tri", [128, 128], F32)
        onesf = SB("onesf", [1, 128], F32)
        onesb = SB("onesb", [128, 1], BF16)
        nstat = SB("nstat", [1, 2 * NCORE + 8], F32)
        kn_sb = SB("knsb", [1, 2 * NCORE], F32)
        bias_sb = SB("bias", [128, 2], F32)
        pt = [SB(f"pt{i}", [128, 512], BF16) for i in range(4)]
        rsa = [SB(f"rsa{i}", [128, 512], F32) for i in range(2)]
        rsc = SB("rsc", [128, 512], BF16)
        rcp = [SB(f"rcp{i}", [1, 512], F32) for i in range(HPC)]
        rbs = SB("rbs", [128, 512], F32)
        aost = SB("aost", [128, 512], BF16)
        banks = [ctx.enter_context(nc.psum_tensor(f"bank{i}", [128, 512], F32)).ap()
                 for i in range(8)]
        st_ps = [banks[0], banks[1], banks[7]]
        pv_ps = banks[2:4]
        rs_ps = [banks[4][0:1, :], banks[5][0:1, :]]
        rb_ps = banks[6]

        E = mk_engs(nc, ctx)
        pe, act, dve, sp, gp = E

        s_ld = Stream(nc, ctx, "ld")
        s_ao = Stream(nc, ctx, "ao")
        s_h = [Stream(nc, ctx, f"h{i}") for i in range(2)]
        s_h2 = [Stream(nc, ctx, f"hb{i}") for i in range(2)]
        s_h0a = Stream(nc, ctx, "h0a")
        FC = 1024
        for hl in range(2):
            s_h0a.dma(sp, ks[:, hl * S: hl * S + FC], k_d[hl][:, 0:FC])
            s_h0a.dma(act, qs[:, hl * S: hl * S + FC], q_d[hl][:, 0:FC])
            s_h0a.dma(sp, vs[:, hl * (S // 128) * DH:
                             hl * (S // 128) * DH + (FC // 128) * DH],
                      v_d[:, hl, 0:FC // 128].rearrange("p k d -> p (k d)"))
        HS = S // 2
        for hl in range(2):
            s_h[hl].dma(sp, ks[:, hl * S + FC: hl * S + HS], k_d[hl][:, FC:HS])
            s_h[hl].dma(act, qs[:, hl * S + FC: hl * S + HS], q_d[hl][:, FC:HS])
            s_h[hl].dma(sp, vs[:, hl * (S // 128) * DH + (FC // 128) * DH:
                               hl * (S // 128) * DH + (HS // 128) * DH],
                        v_d[:, hl, FC // 128:HS // 128].rearrange("p k d -> p (k d)"))
        for hl in range(2):
            s_h2[hl].dma(sp, ks[:, hl * S + HS:(hl + 1) * S], k_d[hl][:, HS:S])
            s_h2[hl].dma(act, qs[:, hl * S + HS:(hl + 1) * S], q_d[hl][:, HS:S])
            s_h2[hl].dma(sp, vs[:, hl * (S // 128) * DH + (HS // 128) * DH:
                                (hl + 1) * (S // 128) * DH],
                         v_d[:, hl, HS // 128:].rearrange("p k d -> p (k d)"))
        s_ld.dma(sp, tri, tri_d)
        s_ld.dma(sp, onesf, on_d)
        s_ld.dma(sp, onesb, ob_d)
        s_ld.dma(sp, nstat[0:1, 0:2 * NCORE], qn_d.rearrange("a h c -> a (h c)"))
        s_ld.dma(sp, kn_sb, kn_d.rearrange("a h c -> a (h c)"))
        s_ld.waited_by(dve)
        s_ld.waited_by(pe)
        s_ld.waited_by(act)
        h_waited = [False, False]
        h2_waited = [False, False]
        h0a_waited = [False]
        mq = nstat[0:1, 2 * NCORE:2 * NCORE + 2]
        mk = nstat[0:1, 2 * NCORE + 2:2 * NCORE + 4]
        pr = nstat[0:1, 2 * NCORE + 4:2 * NCORE + 6]
        dve.put(nc.vector.reduce_max(
            mq, nstat[0:1, 0:2 * NCORE].rearrange("a (h c) -> a h c", c=NCORE),
            axis=AX.X))
        nc.vector.drain()
        dve.put(nc.vector.reduce_max(
            mk, kn_sb.rearrange("a (h c) -> a h c", c=NCORE), axis=AX.X))
        nc.vector.drain()
        dve.put(nc.vector.tensor_tensor(pr, mq, mk, ALU.mult))
        d_pr = dve.sig()
        act.wait(dve, d_pr)
        act.put(nc.scalar.activation(pr, pr, AF.Sqrt,
                                     scale=(1.02 * INVSQ) * (1.02 * INVSQ)))
        a_pr = act.sig()
        dve.wait(act, a_pr)
        dve.put(nc.vector.tensor_scalar_mul(pr, pr, -1.0))
        d_ng = dve.sig()
        pe.wait(dve, d_ng)
        pe.put(nc.tensor.matmul(rb_ps[:, 0:2], onesf[0:1, :], pr, start=True,
                                stop=True))
        p_b = pe.sig()
        dve.wait(pe, p_b)
        dve.put(nc.vector.tensor_copy(bias_sb, rb_ps[:, 0:2]))
        d_bias = dve.sig()
        pe.wait(dve, d_bias)
        act.wait(dve, d_bias)

        iters = []
        for qt in range(NQT):
            for hl in range(HPC):
                for kc in range(qt + 1):
                    diag = (kc == qt)
                    for ksub in range(4):
                        iters.append(dict(
                            qt=qt, hl=hl, kt=kc * 4 + ksub,
                            qoff=ksub * 128 if diag else 0,
                            diag=diag,
                            first=(kc == 0 and ksub == 0),
                            first3=(kc == 0 and ksub == 3),
                            lastmm=(diag and ksub == 3)))

        pt_free = [0, 0, 0, 0]
        pt_dve_free = [0, 0, 0, 0]
        rsc_free = [0]
        st_free = [0, 0, 0]
        pv_free = [0, 0]
        rs_free = [0, 0]
        rcp_free = [0, 0]
        rb_free = 0
        exp_of = {}

        def emit_st(i):
            it = iters[i]
            sl = i % 3
            psl = i % 4
            stp = st_ps[sl]
            qb = it["hl"] * S + it["qt"] * 512
            if not h0a_waited[0]:
                s_h0a.waited_by(pe)
                s_h0a.waited_by(act)
                h0a_waited[0] = True
            if not h_waited[it["hl"]] and (it["qt"] * 512 + 512 > FC
                                           or (it["kt"] + 1) * 128 > FC):
                s_h[it["hl"]].waited_by(pe)
                s_h[it["hl"]].waited_by(act)
                h_waited[it["hl"]] = True
            if not h2_waited[it["hl"]] and (it["qt"] * 512 + 512 > HS
                                            or (it["kt"] + 1) * 128 > HS):
                s_h2[it["hl"]].waited_by(pe)
                s_h2[it["hl"]].waited_by(act)
                h2_waited[it["hl"]] = True
            pe.wait(act, st_free[sl])
            if it["first"]:
                pe.wait(dve, pv_free[it["hl"]])
                pe.wait(dve, rs_free[it["hl"]])
            pe.put(nc.tensor.matmul(
                stp[:, it["qoff"]:512],
                ks[:, it["hl"] * S + it["kt"] * 128:
                   it["hl"] * S + (it["kt"] + 1) * 128],
                qs[:, qb + it["qoff"]: qb + 512], start=True, stop=True))
            p_st = pe.sig()
            if it["diag"]:
                dve.wait(pe, p_st)
                dsl = stp[:, it["qoff"]:it["qoff"] + 128]
                dve.put(nc.vector.tensor_tensor(dsl, dsl, tri, ALU.add))
                d_msk = dve.sig()
                act.wait(dve, d_msk)
            act.wait(pe, p_st)
            act.wait(pe, pt_free[psl])
            act.wait(dve, pt_dve_free[psl])
            act.put(nc.scalar.activation(
                pt[psl][:, it["qoff"]:512], stp[:, it["qoff"]:512], AF.Exp,
                bias=bias_sb[:, it["hl"]:it["hl"] + 1], scale=INVSQ))
            st_free[sl] = exp_of[i] = act.sig()

        def emit_pv(i):
            it = iters[i]
            psl = i % 4
            hl = it["hl"]
            pe.wait(act, exp_of[i])
            nc.tensor.matmul(
                pv_ps[hl][:, it["qoff"]:512],
                vs[:, (hl * (S // 128) + it["kt"]) * DH:
                   (hl * (S // 128) + it["kt"] + 1) * DH],
                pt[psl][:, it["qoff"]:512],
                start=it["first"], stop=it["lastmm"])
            last = nc.tensor.matmul(
                rs_ps[hl][:, it["qoff"]:512], onesb[:, 0:1],
                pt[psl][:, it["qoff"]:512],
                start=it["first"], stop=it["lastmm"])
            pe.put(last)
            pt_free[psl] = pe.sig()

        def emit_fin_a(i):
            it = iters[i]
            hl = it["hl"]
            p_pv = pe.n
            dve.wait(pe, p_pv)
            dve.wait(pe, rcp_free[hl])
            dve.put(nc.vector.reciprocal(rcp[hl], rs_ps[hl]))
            return dve.sig()

        def emit_fin_b(i, d_rcp):
            nonlocal rb_free
            it = iters[i]
            hl = it["hl"]
            qt = it["qt"]
            pe.wait(dve, d_rcp)
            pe.wait(dve, rb_free)
            pe.put(nc.tensor.matmul(rb_ps[:, :], onesf[0:1, :], rcp[hl],
                                    start=True, stop=True))
            rcp_free[hl] = p_rb = pe.sig()
            dve.wait(pe, p_rb)
            dve.put(nc.vector.tensor_copy(rbs, rb_ps[:, :]))
            rb_free = dve.sig()
            nc.vector.drain()
            s_ao.waited_by(dve)
            dve.put(nc.vector.tensor_tensor(aost, pv_ps[hl][:, :], rbs,
                                            ALU.mult))
            pv_free[hl] = rs_free[hl] = d_ao = dve.sig()
            sp.wait(dve, d_ao)
            s_ao.dma(sp, ao_o[hl][:, qt * 512:(qt + 1) * 512], aost)

        pending = []
        emit_st(0)
        emit_st(1)
        for i in range(len(iters)):
            while pending and pending[0][0] <= i:
                _, bi, d_rcp = pending.pop(0)
                emit_fin_b(bi, d_rcp)
            if i + 2 < len(iters):
                emit_st(i + 2)
            emit_pv(i)
            if iters[i]["lastmm"]:
                d_rcp = emit_fin_a(i)
                pending.append((i + 3, i, d_rcp))
        while pending:
            _, bi, d_rcp = pending.pop(0)
            emit_fin_b(bi, d_rcp)

        s_ao.waited_by(sp)
        s_ao.waited_by(gp)
    return nc


def build_l3():
    nc = bass.Bass("TRN2", target_bir_lowering=False)
    x1_in = nc.dram_tensor("x1", [NTT, 128, D], F32, kind="ExternalInput").ap()
    ao_in = nc.dram_tensor("aof", [NDC, 128, TPC], BF16, kind="ExternalInput").ap()
    wo_d = nc.dram_tensor("wo", [NDC, 128, D], F32, kind="ExternalInput").ap()
    ax_d = nc.dram_tensor("ao_a", [NDC, 128, NA, RK], F32, kind="ExternalInput").ap()
    bx_d = nc.dram_tensor("ao_b", [NA, RK, D], F32, kind="ExternalInput").ap()
    wg_d = nc.dram_tensor("wg", [NHC, 128, D], F32, kind="ExternalInput").ap()
    wu_d = nc.dram_tensor("wu", [NHC, 128, D], F32, kind="ExternalInput").ap()
    wd_d = nc.dram_tensor("wd", [NHC, 128, D], F32, kind="ExternalInput").ap()
    st_d = nc.dram_tensor("starts", [1, 4], I32, kind="ExternalInput").ap()
    io_d = nc.dram_tensor("iota", [1, 4], F32, kind="ExternalInput").ap()
    id_d = nc.dram_tensor("ident", [128, 128], BF16, kind="ExternalInput").ap()
    y_o = nc.dram_tensor("y", [NTT, 128, D], F32, kind="ExternalOutput").ap()

    with ExitStack() as ctx:
        SB = lambda nm, sh, dt: ctx.enter_context(nc.sbuf_tensor("sb_" + nm, sh, dt)).ap()
        resid = SB("resid", [128, NTT * D], F32)
        scr = SB("scr", [128, D], BF16)
        lnout = [SB(f"lnout{b}", [128, D], BF16) for b in range(2)]
        stats = SB("stats", [128, 8 * NTT], F32)
        eps_t = SB("eps", [128, 1], F32)
        ident = SB("ident", [128, 128], BF16)
        starts_sb = SB("starts", [1, 16], F32)
        iota_sb = SB("iotas", [1, 4], F32)
        ohb = SB("ohb", [128, 8], F32)
        ones1 = SB("ones1", [1, 128], F32)
        asel = SB("asel", [128, NDC * 3 * RK], BF16)
        bsel = SB("bsel", [16, D], BF16)
        low = SB("low", [16, TPC], BF16)
        banks = [ctx.enter_context(nc.psum_tensor(f"bank{i}", [128, 512], F32)).ap()
                 for i in range(8)]
        E = mk_engs(nc, ctx)
        pe, act, dve, sp, gp = E

        MS = lambda nm: Stream(nc, ctx, nm)
        s_ld, s_ss, s_sa, s_bt, s_so = MS("ld"), MS("ss"), MS("sa"), MS("bt"), MS("so")
        s_wa = MS("wa")
        s_aos = MS("aos")
        ST = {k: MS(k) for k in
              ["wg0", "wg1", "wu0", "wu1"] + [f"wd{i}" for i in range(NG)]}
        c2 = ctx.enter_context(ExitStack())
        SB2 = lambda nm, sh, dt: c2.enter_context(nc.sbuf_tensor("sb_" + nm, sh, dt)).ap()
        aos = SB2("aos", [128, NDC * TPC], BF16)
        s_aos.dma(sp, aos.rearrange("p (c t) -> p c t", t=TPC),
                  ao_in.transpose([1, 0, 2]))
        for tt in range(NTT):
            s_ld.dma(sp, resid[:, tt * D:(tt + 1) * D], x1_in[tt])
        s_ld.dma(sp, ident, id_d)
        s_ld.dma(sp, iota_sb, io_d)
        s_ss.dma(gp, starts_sb[0:1, 0:4], st_d)
        dve.put(nc.vector.memset(eps_t, 1e-5))
        dve.put(nc.vector.memset(ones1, 1.0))
        dve.sig()
        s_ld.waited_by(act)
        s_ld.waited_by(dve)

        if True:

            wall = [SB2(f"wall{i}", [128, D], BF16) for i in range(NDC)]
            s_was = [Stream(nc, ctx, f"wa{i}") for i in range(NDC)]
            for dc in range(NDC):
                s_was[dc].dma(gp, wall[dc], wo_d[dc], max_dma_last_dim=8192)
            a_sb = SB2("asb", [128, NDC * NA * RK], BF16)
            b_all = SB2("ball", [16, NA * D], BF16)
            s_sa.dma(gp, a_sb.rearrange("p (f) -> p f"), ax_d.transpose([1, 0, 2, 3]))
            s_sa.dma(gp, b_all.rearrange("p (a d) -> p a d", d=D),
                     bx_d.transpose([1, 0, 2]))
            emit_onehot(nc, E, starts_sb, iota_sb, ohb, ones1, banks[7], s_ld,
                        s_ss)
            s_sa.waited_by(dve)
            slA = asel.rearrange("p (c t r) -> p c t r", t=3, r=RK)[:, :, 0, :]
            emit_ab_select(nc, E, ohb, [(a_sb, slA)], [(b_all, bsel)])
            gp.e.wait_ge(dve.sem, dve.n)

            s_aos.waited_by(pe)
            pe.wait(dve, dve.n)
            lo_ps = banks[2][0:16, :]
            for tc in range(2):
                pe.wait(act, act.n)
                last = None
                for dc in range(NDC):
                    lh = asel.rearrange("p (c t r) -> p c t r", t=3, r=RK)[:, dc, 0, :]
                    last = nc.tensor.matmul(
                        lo_ps, lh,
                        aos[:, dc * TPC + tc * 512: dc * TPC + tc * 512 + 512],
                        start=(dc == 0), stop=(dc == NDC - 1))
                pe.put(last)
                p_lo = pe.sig()
                act.wait(pe, p_lo)
                act.put(nc.scalar.copy(low[0:16, tc * 512: tc * 512 + 512], lo_ps))
                act.sig()
            low_ready = act.n
            pe.wait(act, low_ready)

            o_done = [0] * 4
            resid_ts_counts = [0] * NTT
            wa_waited = 0
            low_waited = False
            for ts in range(NTT):
                for oc in range(4):
                    pe.wait(dve, o_done[oc])
                    for dc in range(NDC):
                        if wa_waited <= dc:
                            wa_waited = dc + 1
                            s_was[dc].waited_by(pe)
                        nc.tensor.matmul(
                            banks[oc][:, :],
                            aos[:, dc * TPC + ts * 128: dc * TPC + (ts + 1) * 128],
                            wall[dc][:, oc * 512:(oc + 1) * 512],
                            start=(dc == 0), stop=False)
                    if not low_waited:
                        pe.wait(act, low_ready)
                        low_waited = True
                    last = nc.tensor.matmul(
                        banks[oc][:, :], low[0:16, ts * 128:(ts + 1) * 128],
                        bsel[:, oc * 512:(oc + 1) * 512], start=False, stop=True)
                    pe.put(last)
                    p_o = pe.sig()
                    dve.wait(pe, p_o)
                    rsl = resid[:, ts * D + oc * 512: ts * D + oc * 512 + 512]
                    dve.put(nc.vector.tensor_tensor(rsl, rsl, banks[oc][:, :],
                                                    ALU.add))
                    o_done[oc] = dve.sig()
                resid_ts_counts[ts] = dve.n
            oproj_done = dve.n
            gp.e.wait_ge(dve.sem, oproj_done)
            sp.e.wait_ge(dve.sem, oproj_done)
        ctx.pop_all().close() if False else None
        c2.close()

        with ExitStack() as c3:
            SB3 = lambda nm, sh, dt: c3.enter_context(nc.sbuf_tensor("sb_" + nm, sh, dt)).ap()
            xt = SB3("xt", [128, NDC * TPC], BF16)
            xt_tts = emit_layernorm(nc, E, resid, xt, scr, stats, eps_t, ident,
                                    banks[0:2], lnout, resid_ts_counts)
            gated = SB3("gated", [128, NG * TPC], BF16)
            wg_sb = [SB3(f"wg{b}", [128, D], BF16) for b in range(2)]
            wu_sb = [SB3(f"wu{b}", [128, D], BF16) for b in range(2)]
            wd_sb = [SB3(f"wd{b}", [128, D], BF16) for b in range(NG)]
            def _store_y(ts, dcount):
                sp.wait(dve, dcount)
                s_so.dma(sp, y_o[ts], resid[:, ts * D:(ts + 1) * D])
            ffn_done = emit_ffn(nc, E, ST, xt, resid, gated, wg_sb, wu_sb, wd_sb,
                                [banks[4], banks[5]], [banks[6], banks[7]],
                                banks[0:4], {"wg": wg_d, "wu": wu_d, "wd": wd_d},
                                xt_tts, store_cb=_store_y)
            gp.e.wait_ge(dve.sem, ffn_done)
            s_so.waited_by(sp)
            s_so.waited_by(gp)
    return nc


def _rope_tables():
    pos = np.arange(S, dtype=np.float64)
    j = np.arange(64, dtype=np.float64)
    inv = 10000.0 ** (-2.0 * j / DH)
    ang = pos[:, None] * inv[None, :]
    cos4 = np.tile(np.cos(ang).astype(np.float32), (1, 4))
    sin4 = np.tile(np.sin(ang).astype(np.float32), (1, 4))
    return cos4, sin4


def _tri_mask():
    m = np.zeros((128, 128), np.float32)
    kk, qq = np.meshgrid(np.arange(128), np.arange(128), indexing="ij")
    m[qq < kk] = -1e9
    return m


def _stage_a(A):
    return np.ascontiguousarray(A.transpose(2, 0, 1).reshape(NDC, 128, NA, RK))


def _stage_b(B):
    return np.ascontiguousarray(B.transpose(0, 2, 1))


def _stage_wT(W):
    return np.ascontiguousarray(W.T.reshape(NDC, 128, W.shape[0]))


def _stage_w_colblocks(W):
    return np.ascontiguousarray(W.T.reshape(NDC, 128, 4, 512).transpose(0, 2, 1, 3))


def _stage_wgu(W):
    return np.ascontiguousarray(
        W.reshape(NHC, 128, NDC, 128).transpose(0, 3, 2, 1).reshape(NHC, 128, D))


def _stage_wd(W):
    return np.ascontiguousarray(W.T.reshape(NHC, 128, D))


_IDENT_BF = np.eye(128, dtype=BF)
_IDENT_F32 = np.eye(128, dtype=np.float32)
_IOTA = np.arange(4, dtype=np.float32).reshape(1, 4)
_CACHE = {}


def _get_ncs():
    if "l1" not in _CACHE:
        _CACHE["l1"] = build_l1()
        _CACHE["l2"] = build_l2()
        _CACHE["l3"] = build_l3()
    return _CACHE["l1"], _CACHE["l2"], _CACHE["l3"]


def kernel(L0_attn_x, L0_attn_lora_start_indices, L0_attn_Wq, L0_attn_Aq,
           L0_attn_Bq, L0_attn_Wk, L0_attn_Ak, L0_attn_Bk, L0_attn_Wv,
           L0_attn_Av, L0_attn_Bv, L0_attn_Wo, L0_attn_Ao, L0_attn_Bo,
           L0_ffn_w_gate, L0_ffn_w_up, L0_ffn_w_down):
    nc1, nc2, nc3 = _get_ncs()
    f32 = lambda a: np.ascontiguousarray(np.asarray(a, dtype=np.float32))
    x = f32(L0_attn_x).reshape(S, D)
    starts = np.ascontiguousarray(np.asarray(L0_attn_lora_start_indices,
                                             dtype=np.int32)).reshape(1, 4)

    wg = _stage_wgu(f32(L0_ffn_w_gate))
    wu = _stage_wgu(f32(L0_ffn_w_up))
    wd = _stage_wd(f32(L0_ffn_w_down))
    wq = _stage_w_colblocks(f32(L0_attn_Wq))
    wk = _stage_w_colblocks(f32(L0_attn_Wk))
    wv = _stage_w_colblocks(f32(L0_attn_Wv))
    wo = _stage_wT(f32(L0_attn_Wo))
    aq, ak, av, ao_a = (_stage_a(f32(a)) for a in
                        (L0_attn_Aq, L0_attn_Ak, L0_attn_Av, L0_attn_Ao))
    bq, bk, bv, bo_b = (_stage_b(f32(b)) for b in
                        (L0_attn_Bq, L0_attn_Bk, L0_attn_Bv, L0_attn_Bo))
    cos4, sin4 = _rope_tables()

    in1 = []
    for c in range(NCORE):
        tok = slice(c * TPC, (c + 1) * TPC)
        in1.append(dict(
            x=np.ascontiguousarray(x[tok].reshape(NTT, 128, D)),
            wg=wg, wu=wu, wd=wd, wq=wq, wk=wk, wv=wv,
            aq=aq, ak=ak, av=av, bq=bq, bk=bk, bv=bv,
            cosr=np.ascontiguousarray(cos4[tok].reshape(NTT, 128, 256)),
            sinr=np.ascontiguousarray(sin4[tok].reshape(NTT, 128, 256)),
            starts=starts, iota=_IOTA, ident=_IDENT_BF, identf=_IDENT_F32))
    r1 = run_bass_kernel_spmd(nc1, in1, core_ids=list(range(NCORE)))
    x1 = np.stack([r1.results[c]["x1"].reshape(TPC, D) for c in range(NCORE)])
    qo = np.stack([r1.results[c]["qo"] for c in range(NCORE)])
    ko = np.stack([r1.results[c]["ko"] for c in range(NCORE)])
    vo = np.stack([r1.results[c]["vo"] for c in range(NCORE)])
    qn = np.stack([r1.results[c]["qn"].reshape(H) for c in range(NCORE)])
    kn = np.stack([r1.results[c]["kn"].reshape(H) for c in range(NCORE)])

    q_full = np.ascontiguousarray(qo.transpose(1, 0, 2, 3).reshape(H, S, DH))
    k_full = np.ascontiguousarray(ko.transpose(1, 0, 2, 3).reshape(H, S, DH))
    v_full = np.ascontiguousarray(vo.transpose(1, 0, 2, 3).reshape(H, S, DH))
    tri = _tri_mask()
    onesf = np.ones((1, 128), np.float32)
    onesb = np.ones((128, 1), BF)
    in2 = []
    for c in range(NCORE):
        hs = slice(HPC * c, HPC * (c + 1))
        in2.append(dict(
            qh=np.ascontiguousarray(q_full[hs].transpose(0, 2, 1)),
            kh=np.ascontiguousarray(k_full[hs].transpose(0, 2, 1)),
            vh=np.ascontiguousarray(v_full[hs].reshape(HPC, S // 128, 128, DH)
                                    .transpose(2, 0, 1, 3)),
            qn=np.ascontiguousarray(qn.T[hs].reshape(1, HPC, NCORE)),
            kn=np.ascontiguousarray(kn.T[hs].reshape(1, HPC, NCORE)),
            tri=tri, onesf=onesf, onesb=onesb))
    r2 = run_bass_kernel_spmd(nc2, in2, core_ids=list(range(NCORE)))
    ao = np.stack([r2.results[c]["ao"] for c in range(NCORE)])
    ao_full = ao.reshape(H, DH, S)

    in3 = []
    for c in range(NCORE):
        tok = slice(c * TPC, (c + 1) * TPC)
        in3.append(dict(
            x1=np.ascontiguousarray(x1[c].reshape(NTT, 128, D)),
            aof=np.ascontiguousarray(ao_full[:, :, tok]),
            wo=wo, ao_a=ao_a, ao_b=bo_b, wg=wg, wu=wu, wd=wd,
            starts=starts, iota=_IOTA, ident=_IDENT_BF))
    r3 = run_bass_kernel_spmd(nc3, in3, core_ids=list(range(NCORE)))
    y = np.concatenate([r3.results[c]["y"].reshape(TPC, D) for c in range(NCORE)])
    return y.reshape(1, S, D)


# version 17
# speedup vs baseline: 1.0157x; 1.0038x over previous
import math
import numpy as np
import ml_dtypes
from contextlib import ExitStack

import concourse.bass as bass
import concourse.mybir as mybir
from concourse.bass_utils import run_bass_kernel_spmd

F32 = mybir.dt.float32
BF16 = mybir.dt.bfloat16
I32 = mybir.dt.int32
AF = mybir.ActivationFunctionType
ALU = mybir.AluOpType
AX = mybir.AxisListType

NCORE = 8
D = 2048
S = 8192
H = 16
DH = 128
HID = 8192
TPC = S // NCORE
NTT = TPC // 128
NDC = D // 128
NHC = HID // 128
NG = 8
HPC = H // NCORE
RK = 16
NA = 4
INVSQ = 1.0 / math.sqrt(DH)
BF = ml_dtypes.bfloat16


class Eng:

    def __init__(self, nc, ctx, name, eng):
        self.nc = nc
        self.e = eng
        self.sem = ctx.enter_context(nc.semaphore(f"sem_{name}"))
        self.n = 0
        self.last = None

    def put(self, instr):
        self.last = instr
        return instr

    def sig(self, k=1):
        self.last.then_inc(self.sem, k)
        self.n += k
        return self.n

    def wait(self, other, count):
        if count > 0:
            self.e.wait_ge(other.sem, count)


class Stream:

    def __init__(self, nc, ctx, name):
        self.sem = ctx.enter_context(nc.semaphore(f"st_{name}"))
        self.n = 0

    def dma(self, qeng, out, in_, **kw):
        qeng.e.dma_start(out, in_, **kw).then_inc(self.sem, 16)
        self.n += 16
        return self.n

    def waited_by(self, eng, count=None):
        eng.e.wait_ge(self.sem, self.n if count is None else count)


def mk_engs(nc, ctx):
    return (Eng(nc, ctx, "pe", nc.tensor), Eng(nc, ctx, "act", nc.scalar),
            Eng(nc, ctx, "dve", nc.vector), Eng(nc, ctx, "sp", nc.sync),
            Eng(nc, ctx, "gp", nc.gpsimd))


def emit_layernorm(nc, E, resid, xt, scr, stats, eps_t, ident, banks, lnout,
                   resid_dve_ready):
    if not isinstance(resid_dve_ready, (list, tuple)):
        resid_dve_ready = [resid_dve_ready] * NTT
    """LayerNorm of the NTT resident token tiles into transposed bf16 xt.

    resid [128, NTT*D] f32 -> xt [128, NDC*TPC] bf16 (X^T, feature-major).
    banks: two f32 [128,512] psum banks (bitcast bf16) for PE transposes.
    Returns act count at which xt is fully written.
    """
    pe, act, dve, sp, gp = E
    sums = stats[:, 0:NTT]
    sumsq = stats[:, NTT:2 * NTT]
    s2 = stats[:, 2 * NTT:3 * NTT]
    vpe = stats[:, 3 * NTT:4 * NTT]
    sq = stats[:, 4 * NTT:5 * NTT]
    rstd = stats[:, 5 * NTT:6 * NTT]
    nmr = stats[:, 6 * NTT:7 * NTT]

    for tt in range(NTT):
        act.wait(dve, resid_dve_ready[tt])
        act.put(nc.scalar.activation(scr[:, :], resid[:, tt * D:(tt + 1) * D],
                                     AF.Square, accum_out=sumsq[:, tt:tt + 1]))
        nc.scalar.drain()
    for tt in range(NTT):
        dve.wait(dve, resid_dve_ready[tt])
        dve.put(nc.vector.reduce_sum(sums[:, tt:tt + 1],
                                     resid[:, tt * D:(tt + 1) * D], axis=AX.X))
        nc.vector.drain()
    d_sums = dve.sig()
    act.wait(dve, d_sums)
    act.put(nc.scalar.activation(s2, sums, AF.Square, scale=1.0 / D))
    a_stats = act.sig()
    dve.wait(act, a_stats)
    dve.put(nc.vector.scalar_tensor_tensor(vpe, sumsq, 1.0 / D, s2, ALU.mult,
                                           ALU.subtract))
    d_vpe = dve.sig()
    act.wait(dve, d_vpe)
    act.put(nc.scalar.activation(sq, vpe, AF.Sqrt, bias=eps_t[:, 0:1]))
    a_sq = act.sig()
    dve.wait(act, a_sq)
    dve.put(nc.vector.reciprocal(rstd, sq))
    nc.vector.drain()
    dve.put(nc.vector.scalar_tensor_tensor(nmr, sums, -1.0 / D, rstd, ALU.mult,
                                           ALU.mult))
    nc.vector.drain()
    dve.sig()

    slot_evac = [0, 0]
    xt_tt_counts = [0] * NTT
    for tt in range(NTT):
        lo = lnout[tt % 2]
        dve.wait(pe, pe.n)
        dve.put(nc.vector.tensor_scalar(lo[:, :], resid[:, tt * D:(tt + 1) * D],
                                        rstd[:, tt:tt + 1], nmr[:, tt:tt + 1],
                                        ALU.mult, ALU.add))
        d_ln = dve.sig()
        for dc in range(NDC):
            sl = dc % 2
            ps = banks[sl].bitcast(BF16)[:, 0:128]
            pe.wait(act, slot_evac[sl])
            pe.wait(dve, d_ln)
            pe.put(nc.tensor.transpose(ps, lo[:, dc * 128:(dc + 1) * 128], ident))
            p_tp = pe.sig()
            act.wait(pe, p_tp)
            act.put(nc.scalar.copy(
                xt[:, dc * TPC + tt * 128: dc * TPC + (tt + 1) * 128], ps))
            slot_evac[sl] = act.sig()
        xt_tt_counts[tt] = act.n
    return xt_tt_counts


def emit_ffn(nc, E, ST, xt, resid, gated, wg_sb, wu_sb, wd_sb, gps, ups, dps,
             wst, xt_act_ready, store_cb=None):
    pe, act, dve, sp, gp = E
    NGRP = NHC // NG
    gate_done = [0, 0]
    down_done = [0] * 4
    wbuf_free = [0, 0]
    wd_free = [0] * NG
    swg = [ST["wg0"], ST["wg1"]]
    swu = [ST["wu0"], ST["wu1"]]
    swd = [ST[f"wd{i}"] for i in range(NG)]
    if isinstance(xt_act_ready, (list, tuple)):
        xt_tc = [xt_act_ready[3], xt_act_ready[7]]
    else:
        xt_tc = [xt_act_ready, xt_act_ready]
    waited_tc = [False, False]
    for g in range(NGRP):
        for hc in range(NG):
            hcg = g * NG + hc
            b = hc % 2
            gp.e.wait_ge(pe.sem, wbuf_free[b])
            swg[b].dma(gp, wg_sb[b][:, :], wst["wg"][hcg], max_dma_last_dim=8192)
            swu[b].dma(gp, wu_sb[b][:, :], wst["wu"][hcg], max_dma_last_dim=8192)
            gp.e.wait_ge(pe.sem, wd_free[hc])
            swd[hc].dma(gp, wd_sb[hc][:, :], wst["wd"][hcg],
                        max_dma_last_dim=8192)
            for tc in range(2):
                sl = tc
                if tc == 0:
                    swg[b].waited_by(pe)
                    swu[b].waited_by(pe)
                if not waited_tc[tc]:
                    pe.wait(act, xt_tc[tc])
                    waited_tc[tc] = True
                pe.wait(dve, gate_done[sl])
                for dc in range(NDC):
                    nc.tensor.matmul(
                        gps[sl][:, :], wg_sb[b][:, dc * 128:(dc + 1) * 128],
                        xt[:, dc * TPC + tc * 512: dc * TPC + tc * 512 + 512],
                        start=(dc == 0), stop=(dc == NDC - 1))
                last = None
                for dc in range(NDC):
                    last = nc.tensor.matmul(
                        ups[sl][:, :], wu_sb[b][:, dc * 128:(dc + 1) * 128],
                        xt[:, dc * TPC + tc * 512: dc * TPC + tc * 512 + 512],
                        start=(dc == 0), stop=(dc == NDC - 1))
                pe.put(last)
                p_mm = pe.sig()
                act.wait(pe, p_mm)
                gsl = gated[:, hc * TPC + tc * 512: hc * TPC + tc * 512 + 512]
                act.put(nc.scalar.activation(gsl, gps[sl][:, :], AF.Silu))
                a_silu = act.sig()
                dve.wait(act, a_silu)
                dve.put(nc.vector.scalar_tensor_tensor(gsl, gsl, 0.5, ups[sl][:, :],
                                                       ALU.mult, ALU.mult))
                gate_done[sl] = dve.sig()
            wbuf_free[b] = pe.n
        d_gate = dve.n
        pe.wait(dve, d_gate)
        if not waited_tc[1]:
            pe.wait(act, xt_tc[1])
            waited_tc[1] = True
        for hc in range(NG):
            swd[hc].waited_by(pe)
        for ts in range(NTT):
            for oc in range(4):
                pe.wait(dve, down_done[oc])
                last = None
                for hc in range(NG):
                    last = nc.tensor.matmul(
                        dps[oc][:, :],
                        gated[:, hc * TPC + ts * 128: hc * TPC + (ts + 1) * 128],
                        wd_sb[hc][:, oc * 512:(oc + 1) * 512],
                        start=(hc == 0), stop=(hc == NG - 1))
                pe.put(last)
                p_dn = pe.sig()
                dve.wait(pe, p_dn)
                rsl = resid[:, ts * D + oc * 512: ts * D + oc * 512 + 512]
                dve.put(nc.vector.tensor_tensor(rsl, rsl, dps[oc][:, :], ALU.add))
                down_done[oc] = dve.sig()
            if store_cb is not None and g == NGRP - 1:
                store_cb(ts, dve.n)
        for hc in range(NG):
            wd_free[hc] = pe.n
    return dve.n


def emit_onehot(nc, E, starts_sb, iota_sb, ohb, ones1, oh_ps, s_ld, s_ss):
    pe, act, dve, sp, gp = E
    s_ld.waited_by(dve)
    s_ss.waited_by(dve)
    cnt = starts_sb[0:1, 4:5]
    le = starts_sb[0:1, 12:16]
    ohr = starts_sb[0:1, 8:12]
    dve.put(nc.vector.tensor_single_scalar(le, starts_sb[0:1, 0:4], 0.0, ALU.is_le))
    nc.vector.drain()
    dve.put(nc.vector.reduce_sum(cnt, le, axis=AX.X))
    nc.vector.drain()
    dve.put(nc.vector.tensor_scalar_add(cnt, cnt, -1.0))
    nc.vector.drain()
    dve.put(nc.vector.tensor_relu(cnt, cnt))
    nc.vector.drain()
    dve.put(nc.vector.tensor_scalar(ohr, iota_sb[0:1, 0:4], cnt[0:1, 0:1], None,
                                    ALU.subtract, ALU.bypass))
    nc.vector.drain()
    dve.put(nc.vector.tensor_tensor(ohr, ohr, ohr, ALU.mult))
    nc.vector.drain()
    dve.put(nc.vector.tensor_scalar(ohr, ohr, -1.0, 1.0, ALU.mult, ALU.add))
    nc.vector.drain()
    dve.put(nc.vector.tensor_relu(ohr, ohr))
    d_oh = dve.sig()
    pe.wait(dve, d_oh)
    pe.put(nc.tensor.matmul(oh_ps[:, 0:4], ones1[0:1, :], ohr, start=True,
                            stop=True))
    p_oh = pe.sig()
    dve.wait(pe, p_oh)
    dve.put(nc.vector.tensor_copy(ohb[:, 0:4], oh_ps[:, 0:4]))
    nc.vector.drain()
    return dve.sig()


def emit_ab_select(nc, E, ohb, a_pairs, b_pairs):
    pe, act, dve, sp, gp = E
    for a_sb, asl in a_pairs:
        for a in range(NA):
            src = a_sb.rearrange("p (c a r) -> p c a r", a=NA, r=RK)[:, :, a, :]
            if a == 0:
                dve.put(nc.vector.tensor_scalar_mul(asl, src, ohb[:, 0:1]))
            else:
                dve.put(nc.vector.scalar_tensor_tensor(asl, src, ohb[:, a:a + 1],
                                                       asl, ALU.mult, ALU.add))
            nc.vector.drain()
    a_done = dve.sig()
    for b_all, bsel in b_pairs:
        for a in range(NA):
            src = b_all[:, a * D:(a + 1) * D]
            if a == 0:
                dve.put(nc.vector.tensor_scalar_mul(bsel[:, :], src,
                                                    ohb[0:16, 0:1]))
            else:
                dve.put(nc.vector.scalar_tensor_tensor(bsel[:, :], src,
                                                       ohb[0:16, a:a + 1],
                                                       bsel[:, :], ALU.mult,
                                                       ALU.add))
            nc.vector.drain()
    return a_done, dve.sig()


def build_l1():
    nc = bass.Bass("TRN2", target_bir_lowering=False)
    x_in = nc.dram_tensor("x", [NTT, 128, D], F32, kind="ExternalInput").ap()
    wg_d = nc.dram_tensor("wg", [NHC, 128, D], F32, kind="ExternalInput").ap()
    wu_d = nc.dram_tensor("wu", [NHC, 128, D], F32, kind="ExternalInput").ap()
    wd_d = nc.dram_tensor("wd", [NHC, 128, D], F32, kind="ExternalInput").ap()
    wq_d = nc.dram_tensor("wq", [NDC, 4, 128, 512], F32, kind="ExternalInput").ap()
    wk_d = nc.dram_tensor("wk", [NDC, 4, 128, 512], F32, kind="ExternalInput").ap()
    wv_d = nc.dram_tensor("wv", [NDC, 4, 128, 512], F32, kind="ExternalInput").ap()
    aq_d = nc.dram_tensor("aq", [NDC, 128, NA, RK], F32, kind="ExternalInput").ap()
    ak_d = nc.dram_tensor("ak", [NDC, 128, NA, RK], F32, kind="ExternalInput").ap()
    av_d = nc.dram_tensor("av", [NDC, 128, NA, RK], F32, kind="ExternalInput").ap()
    bq_d = nc.dram_tensor("bq", [NA, RK, D], F32, kind="ExternalInput").ap()
    bk_d = nc.dram_tensor("bk", [NA, RK, D], F32, kind="ExternalInput").ap()
    bv_d = nc.dram_tensor("bv", [NA, RK, D], F32, kind="ExternalInput").ap()
    cos_d = nc.dram_tensor("cosr", [NTT, 128, 256], F32, kind="ExternalInput").ap()
    sin_d = nc.dram_tensor("sinr", [NTT, 128, 256], F32, kind="ExternalInput").ap()
    st_d = nc.dram_tensor("starts", [1, 4], I32, kind="ExternalInput").ap()
    io_d = nc.dram_tensor("iota", [1, 4], F32, kind="ExternalInput").ap()
    id_d = nc.dram_tensor("ident", [128, 128], BF16, kind="ExternalInput").ap()
    idf_d = nc.dram_tensor("identf", [128, 128], F32, kind="ExternalInput").ap()

    x1_o = nc.dram_tensor("x1", [NTT, 128, D], F32, kind="ExternalOutput").ap()
    q_o = nc.dram_tensor("qo", [H, TPC, DH], BF16, kind="ExternalOutput").ap()
    k_o = nc.dram_tensor("ko", [H, TPC, DH], BF16, kind="ExternalOutput").ap()
    v_o = nc.dram_tensor("vo", [H, TPC, DH], BF16, kind="ExternalOutput").ap()
    qn_o = nc.dram_tensor("qn", [H, 1], F32, kind="ExternalOutput").ap()
    kn_o = nc.dram_tensor("kn", [H, 1], F32, kind="ExternalOutput").ap()

    with ExitStack() as ctx:
        SB = lambda nm, sh, dt: ctx.enter_context(nc.sbuf_tensor("sb_" + nm, sh, dt)).ap()
        resid = SB("resid", [128, NTT * D], F32)
        xt = SB("xt", [128, NDC * TPC], BF16)
        scr = SB("scr", [128, D], BF16)
        lnout = [SB(f"lnout{b}", [128, D], BF16) for b in range(2)]
        stats = SB("stats", [128, 8 * NTT], F32)
        eps_t = SB("eps", [128, 1], F32)
        ident = SB("ident", [128, 128], BF16)
        identf = SB("identf", [128, 128], F32)
        starts_sb = SB("starts", [1, 16], F32)
        iota_sb = SB("iotas", [1, 4], F32)
        ohb = SB("ohb", [128, 8], F32)
        ones1 = SB("ones1", [1, 128], F32)
        asel = SB("asel", [128, NDC * 3 * RK], BF16)
        bsel = [SB(f"bsel{i}", [16, D], BF16) for i in range(3)]
        low = SB("low", [16, 3 * TPC], BF16)
        nrm = SB("nrm", [128, 48], F32)
        nrmf = SB("nrmf", [128, 128], F32)
        banks = [ctx.enter_context(nc.psum_tensor(f"bank{i}", [128, 512], F32)).ap()
                 for i in range(8)]

        E = mk_engs(nc, ctx)
        pe, act, dve, sp, gp = E

        MS = lambda nm: Stream(nc, ctx, nm)
        s_ld, s_ss, s_sa, s_bt, s_so = MS("ld"), MS("ss"), MS("sa"), MS("bt"), MS("so")
        s_q = [MS("q0"), MS("q1")]
        s_wc = MS("wc")
        ST = {k: MS(k) for k in
              ["wg0", "wg1", "wu0", "wu1"] + [f"wd{i}" for i in range(NG)]}
        for tt in range(NTT):
            s_ld.dma(sp, resid[:, tt * D:(tt + 1) * D], x_in[tt])
        s_ld.dma(sp, ident, id_d)
        s_ld.dma(sp, identf, idf_d)
        s_ld.dma(sp, iota_sb, io_d)
        s_ss.dma(gp, starts_sb[0:1, 0:4], st_d)
        dve.put(nc.vector.memset(eps_t, 1e-5))
        dve.put(nc.vector.memset(ones1, 1.0))
        dve.put(nc.vector.memset(nrm, 0.0))
        dve.put(nc.vector.memset(nrmf, 0.0))
        dve.sig()
        s_ld.waited_by(act)
        s_ld.waited_by(dve)

        xt_tts = emit_layernorm(nc, E, resid, xt, scr, stats, eps_t, ident,
                                banks[0:2], lnout, 0)
        xt_ready = xt_tts[-1]

        with ExitStack() as c3:
            gated = c3.enter_context(nc.sbuf_tensor("gated", [128, NG * TPC],
                                                    BF16)).ap()
            wg_sb = [c3.enter_context(nc.sbuf_tensor(f"wg{b}", [128, D], BF16)).ap()
                     for b in range(2)]
            wu_sb = [c3.enter_context(nc.sbuf_tensor(f"wu{b}", [128, D], BF16)).ap()
                     for b in range(2)]
            wd_sb = [c3.enter_context(nc.sbuf_tensor(f"wd{b}", [128, D], BF16)).ap()
                     for b in range(NG)]
            resid_ts_counts = [0] * NTT

            def _store_x1(ts, dcount):
                resid_ts_counts[ts] = dcount
                sp.wait(dve, dcount)
                s_so.dma(sp, x1_o[ts], resid[:, ts * D:(ts + 1) * D])
            ffn_done = emit_ffn(nc, E, ST, xt, resid, gated, wg_sb, wu_sb, wd_sb,
                                [banks[4], banks[5]], [banks[6], banks[7]],
                                banks[0:4], {"wg": wg_d, "wu": wu_d, "wd": wd_d},
                                xt_tts, store_cb=_store_x1)
            gp.e.wait_ge(dve.sem, ffn_done)

        xt2_tts = emit_layernorm(nc, E, resid, xt, scr, stats, eps_t, ident,
                                 banks[0:2], lnout, resid_ts_counts)
        xt2_ready = xt2_tts[-1]

        with ExitStack() as c2:
            a_sb = [c2.enter_context(nc.sbuf_tensor(f"asb{i}", [128, NDC * NA * RK],
                                                    BF16)).ap() for i in range(3)]
            b_all = [c2.enter_context(nc.sbuf_tensor(f"ball{i}", [16, NA * D],
                                                     BF16)).ap() for i in range(3)]
            for i, ad in enumerate([aq_d, ak_d, av_d]):
                s_sa.dma(gp, a_sb[i].rearrange("p (f) -> p f"),
                         ad.transpose([1, 0, 2, 3]))
            for i, bd in enumerate([bq_d, bk_d, bv_d]):
                s_sa.dma(gp, b_all[i].rearrange("p (a d) -> p a d", d=D),
                         bd.transpose([1, 0, 2]))
            emit_onehot(nc, E, starts_sb, iota_sb, ohb, ones1, banks[7], s_ld,
                        s_ss)
            s_sa.waited_by(dve)
            a_pairs = []
            for i in range(3):
                sl = asel.rearrange("p (c t r) -> p c t r", t=3, r=RK)[:, :, i, :]
                a_pairs.append((a_sb[i], sl))
            b_pairs = [(b_all[i], bsel[i]) for i in range(3)]
            a_done, sel_done = emit_ab_select(nc, E, ohb, a_pairs, b_pairs)
            gp.e.wait_ge(dve.sem, sel_done)

        pe.wait(dve, a_done)
        b_waited = False
        lo_ps = banks[2][0:16, :]
        for i in range(3):
            for tc in range(2):
                pe.wait(act, act.n)
                pe.wait(act, xt2_tts[4 * tc + 3])
                last = None
                for dc in range(NDC):
                    lh = asel.rearrange("p (c t r) -> p c t r", t=3, r=RK)[:, dc, i, :]
                    last = nc.tensor.matmul(
                        lo_ps, lh,
                        xt[:, dc * TPC + tc * 512: dc * TPC + tc * 512 + 512],
                        start=(dc == 0), stop=(dc == NDC - 1))
                pe.put(last)
                p_lo = pe.sig()
                act.wait(pe, p_lo)
                act.put(nc.scalar.copy(
                    low[0:16, i * TPC + tc * 512: i * TPC + tc * 512 + 512], lo_ps))
                act.sig()
        low_ready = act.n
        pe.wait(act, low_ready)

        with ExitStack() as c4:
            SB4 = lambda nm, sh, dt: c4.enter_context(nc.sbuf_tensor("sb_" + nm, sh, dt)).ap()
            wcol2 = [SB4(f"wcol{b}", [128, NDC * 512], BF16) for b in range(2)]
            cos_sb = SB4("cos", [128, NTT * 256], F32)
            sin_sb = SB4("sin", [128, NTT * 256], F32)
            ro_t = [[SB4(f"rot{s}_{i}", [128, 256], F32) for i in range(4)]
                    for s in range(3)]
            qkst = [SB4(f"qkst{b}", [128, 512], BF16) for b in range(3)]
            sqt = SB4("sqt", [128, 512], F32)
            sp.wait(dve, sel_done)
            for tt in range(NTT):
                s_ld.dma(sp, cos_sb[:, tt * 256:(tt + 1) * 256], cos_d[tt])
                s_ld.dma(sp, sin_sb[:, tt * 256:(tt + 1) * 256], sin_d[tt])
            s_ld.waited_by(dve)
            qkv_plan = [(wq_d, q_o, True, 0, 0), (wk_d, k_o, True, 1, 16),
                        (wv_d, v_o, False, 2, -1)]
            wcol_free = [0, 0]
            xt2_waited = [False] * NTT
            s_wc2 = [s_wc, Stream(nc, ctx, "wc2")]
            qk_banks = [banks[4], banks[5], banks[3]]
            ps_free = [0, 0, 0]
            st_free = [0, 0, 0]
            s_q = s_q + [Stream(nc, ctx, "q2")]
            nrm_done = 0
            blocks = [(w_d, out_d, rope, li, ncol, ow)
                      for w_d, out_d, rope, li, ncol in qkv_plan
                      for ow in range(4)]
            for bi, (w_d, out_d, rope, li, ncol, ow) in enumerate(blocks):
                    wb = bi % 2
                    if bi == 0:
                        gp.e.wait_ge(pe.sem, wcol_free[wb])
                        for dc in range(NDC):
                            s_wc2[wb].dma(gp, wcol2[wb][:, dc * 512:(dc + 1) * 512],
                                          w_d[dc, ow], max_dma_last_dim=8192)
                    if bi + 1 < len(blocks):
                        nw_d, _, _, _, _, now = blocks[bi + 1][0], None, None, None, None, blocks[bi + 1][5]
                        nwb = (bi + 1) % 2
                        gp.e.wait_ge(pe.sem, wcol_free[nwb])
                        for dc in range(NDC):
                            s_wc2[nwb].dma(gp, wcol2[nwb][:, dc * 512:(dc + 1) * 512],
                                           blocks[bi + 1][0][dc, now],
                                           max_dma_last_dim=8192)
                    wcol = wcol2[wb]
                    s_wc2[wb].waited_by(pe)
                    for ts in range(NTT):
                        sl = (bi * NTT + ts) % 3
                        ps = qk_banks[sl]
                        pe.wait(dve, ps_free[sl])
                        if xt2_waited[ts] is False:
                            pe.wait(act, xt2_tts[ts])
                            xt2_waited[ts] = True
                        for dc in range(NDC):
                            nc.tensor.matmul(
                                ps[:, :],
                                xt[:, dc * TPC + ts * 128: dc * TPC + (ts + 1) * 128],
                                wcol[:, dc * 512:(dc + 1) * 512],
                                start=(dc == 0), stop=False)
                        if not b_waited:
                            pe.wait(dve, sel_done)
                            b_waited = True
                        last = nc.tensor.matmul(
                            ps[:, :],
                            low[0:16, li * TPC + ts * 128: li * TPC + (ts + 1) * 128],
                            bsel[li][:, ow * 512:(ow + 1) * 512],
                            start=False, stop=True)
                        pe.put(last)
                        p_mm = pe.sig()
                        if ncol >= 0:
                            act.wait(pe, p_mm)
                            act.wait(dve, nrm_done)
                            act.put(nc.scalar.activation(sqt, ps[:, :], AF.Square))
                            a_sq = act.sig()
                            dve.wait(act, a_sq)
                            red = nrm[:, 40:44]
                            njcol = nrm[:, ncol + ow * 4: ncol + ow * 4 + 4]
                            dve.put(nc.vector.reduce_sum(
                                red, sqt.rearrange("p (h d) -> p h d", d=DH),
                                axis=AX.X))
                            nc.vector.drain()
                            dve.put(nc.vector.tensor_tensor(njcol, njcol, red,
                                                            ALU.max))
                            nc.vector.drain()
                            nrm_done = dve.sig()
                        stg = qkst[sl]
                        dve.wait(pe, p_mm)
                        s_q[sl].waited_by(dve)
                        if rope:
                            cs = cos_sb[:, ts * 256:(ts + 1) * 256]
                            sn = sin_sb[:, ts * 256:(ts + 1) * 256]
                            psv = ps.rearrange("p (r two) -> p r two", two=2)
                            stv = stg.rearrange("p (r two) -> p r two", two=2)
                            x1a, x2a = psv[:, :, 0], psv[:, :, 1]
                            o1a, o2a = stv[:, :, 0], stv[:, :, 1]
                            rt = ro_t[sl]
                            dve.put(nc.vector.tensor_tensor(rt[0], x1a, cs, ALU.mult))
                            dve.put(nc.vector.tensor_tensor(rt[1], x2a, sn, ALU.mult))
                            dve.put(nc.vector.tensor_tensor(rt[2], x1a, sn, ALU.mult))
                            dve.put(nc.vector.tensor_tensor(rt[3], x2a, cs, ALU.mult))
                            nc.vector.drain()
                            dve.put(nc.vector.tensor_sub(o1a, rt[0], rt[1]))
                            dve.put(nc.vector.tensor_add(o2a, rt[2], rt[3]))
                        else:
                            dve.put(nc.vector.tensor_copy(stg, ps[:, :]))
                        ps_free[sl] = d_st = dve.sig()
                        sp.wait(dve, d_st)
                        s_q[sl].dma(sp, out_d[ow * 4:(ow + 1) * 4,
                                              ts * 128:(ts + 1) * 128,
                                              :].transpose([1, 0, 2]),
                                    stg.rearrange("p (h d) -> p h d", d=DH))
                    wcol_free[wb] = pe.n

        dve.put(nc.vector.tensor_copy(nrmf[:, 0:48], nrm))
        d_nf = dve.sig()
        pe.wait(dve, d_nf)
        pe.wait(act, act.n)
        tpf = banks[0]
        pe.put(nc.tensor.transpose(tpf[:, 0:128], nrmf, identf))
        p_tn = pe.sig()
        dve.wait(pe, p_tn)
        dve.put(nc.vector.reduce_max(nrmf[0:32, 64:65], tpf[0:32, 0:128], axis=AX.X))
        d_mx = dve.sig()
        sp.wait(dve, d_mx)
        s_so.dma(sp, qn_o, nrmf[0:16, 64:65])
        s_so.dma(sp, kn_o, nrmf[16:32, 64:65])

        s_so.waited_by(sp)
        s_q[0].waited_by(sp)
        s_q[1].waited_by(sp)
        s_so.waited_by(gp)
        s_q[0].waited_by(gp)
        s_q[1].waited_by(gp)
    return nc


def build_l2():
    nc = bass.Bass("TRN2", target_bir_lowering=False)
    q_d = nc.dram_tensor("qh", [HPC, DH, S], BF16, kind="ExternalInput").ap()
    k_d = nc.dram_tensor("kh", [HPC, DH, S], BF16, kind="ExternalInput").ap()
    v_d = nc.dram_tensor("vh", [128, HPC, S // 128, DH], BF16,
                         kind="ExternalInput").ap()
    qn_d = nc.dram_tensor("qn", [1, HPC, NCORE], F32, kind="ExternalInput").ap()
    kn_d = nc.dram_tensor("kn", [1, HPC, NCORE], F32, kind="ExternalInput").ap()
    tri_d = nc.dram_tensor("tri", [128, 128], F32, kind="ExternalInput").ap()
    on_d = nc.dram_tensor("onesf", [1, 128], F32, kind="ExternalInput").ap()
    ob_d = nc.dram_tensor("onesb", [128, 1], BF16, kind="ExternalInput").ap()
    ao_o = nc.dram_tensor("ao", [HPC, DH, S], BF16, kind="ExternalOutput").ap()

    NQT = S // 512

    with ExitStack() as ctx:
        SB = lambda nm, sh, dt: ctx.enter_context(nc.sbuf_tensor("sb_" + nm, sh, dt)).ap()
        qs = SB("qs", [128, HPC * S], BF16)
        ks = SB("ks", [128, HPC * S], BF16)
        vs = SB("vs", [128, HPC * (S // 128) * DH], BF16)
        tri = SB("tri", [128, 128], F32)
        onesf = SB("onesf", [1, 128], F32)
        onesb = SB("onesb", [128, 1], BF16)
        nstat = SB("nstat", [1, 2 * NCORE + 8], F32)
        kn_sb = SB("knsb", [1, 2 * NCORE], F32)
        bias_sb = SB("bias", [128, 2], F32)
        pt = [SB(f"pt{i}", [128, 512], BF16) for i in range(4)]
        rsa = [SB(f"rsa{i}", [128, 512], F32) for i in range(2)]
        rsc = SB("rsc", [128, 512], BF16)
        rcp = [SB(f"rcp{i}", [1, 512], F32) for i in range(HPC)]
        rbs = SB("rbs", [128, 512], F32)
        aost = [SB(f"aost{i}", [128, 512], BF16) for i in range(2)]
        banks = [ctx.enter_context(nc.psum_tensor(f"bank{i}", [128, 512], F32)).ap()
                 for i in range(8)]
        st_ps = [banks[0], banks[1], banks[7]]
        pv_ps = banks[2:4]
        rs_ps = [banks[4][0:1, :], banks[5][0:1, :]]
        rb_ps = banks[6]

        E = mk_engs(nc, ctx)
        pe, act, dve, sp, gp = E

        s_ld = Stream(nc, ctx, "ld")
        s_aos2 = [Stream(nc, ctx, "ao0"), Stream(nc, ctx, "ao1")]
        ao_slot = [0]
        s_h = [Stream(nc, ctx, f"h{i}") for i in range(2)]
        s_h2 = [Stream(nc, ctx, f"hb{i}") for i in range(2)]
        s_h0a = Stream(nc, ctx, "h0a")
        FC = 1024
        for hl in range(2):
            s_h0a.dma(sp, ks[:, hl * S: hl * S + FC], k_d[hl][:, 0:FC])
            s_h0a.dma(act, qs[:, hl * S: hl * S + FC], q_d[hl][:, 0:FC])
            s_h0a.dma(sp, vs[:, hl * (S // 128) * DH:
                             hl * (S // 128) * DH + (FC // 128) * DH],
                      v_d[:, hl, 0:FC // 128].rearrange("p k d -> p (k d)"))
        HS = S // 2
        for hl in range(2):
            s_h[hl].dma(sp, ks[:, hl * S + FC: hl * S + HS], k_d[hl][:, FC:HS])
            s_h[hl].dma(act, qs[:, hl * S + FC: hl * S + HS], q_d[hl][:, FC:HS])
            s_h[hl].dma(sp, vs[:, hl * (S // 128) * DH + (FC // 128) * DH:
                               hl * (S // 128) * DH + (HS // 128) * DH],
                        v_d[:, hl, FC // 128:HS // 128].rearrange("p k d -> p (k d)"))
        for hl in range(2):
            s_h2[hl].dma(sp, ks[:, hl * S + HS:(hl + 1) * S], k_d[hl][:, HS:S])
            s_h2[hl].dma(act, qs[:, hl * S + HS:(hl + 1) * S], q_d[hl][:, HS:S])
            s_h2[hl].dma(sp, vs[:, hl * (S // 128) * DH + (HS // 128) * DH:
                                (hl + 1) * (S // 128) * DH],
                         v_d[:, hl, HS // 128:].rearrange("p k d -> p (k d)"))
        s_ld.dma(sp, tri, tri_d)
        s_ld.dma(sp, onesf, on_d)
        s_ld.dma(sp, onesb, ob_d)
        s_ld.dma(sp, nstat[0:1, 0:2 * NCORE], qn_d.rearrange("a h c -> a (h c)"))
        s_ld.dma(sp, kn_sb, kn_d.rearrange("a h c -> a (h c)"))
        s_ld.waited_by(dve)
        s_ld.waited_by(pe)
        s_ld.waited_by(act)
        h_waited = [False, False]
        h2_waited = [False, False]
        h0a_waited = [False]
        mq = nstat[0:1, 2 * NCORE:2 * NCORE + 2]
        mk = nstat[0:1, 2 * NCORE + 2:2 * NCORE + 4]
        pr = nstat[0:1, 2 * NCORE + 4:2 * NCORE + 6]
        dve.put(nc.vector.reduce_max(
            mq, nstat[0:1, 0:2 * NCORE].rearrange("a (h c) -> a h c", c=NCORE),
            axis=AX.X))
        nc.vector.drain()
        dve.put(nc.vector.reduce_max(
            mk, kn_sb.rearrange("a (h c) -> a h c", c=NCORE), axis=AX.X))
        nc.vector.drain()
        dve.put(nc.vector.tensor_tensor(pr, mq, mk, ALU.mult))
        d_pr = dve.sig()
        act.wait(dve, d_pr)
        act.put(nc.scalar.activation(pr, pr, AF.Sqrt,
                                     scale=(1.02 * INVSQ) * (1.02 * INVSQ)))
        a_pr = act.sig()
        dve.wait(act, a_pr)
        dve.put(nc.vector.tensor_scalar_mul(pr, pr, -1.0))
        d_ng = dve.sig()
        pe.wait(dve, d_ng)
        pe.put(nc.tensor.matmul(rb_ps[:, 0:2], onesf[0:1, :], pr, start=True,
                                stop=True))
        p_b = pe.sig()
        dve.wait(pe, p_b)
        dve.put(nc.vector.tensor_copy(bias_sb, rb_ps[:, 0:2]))
        d_bias = dve.sig()
        pe.wait(dve, d_bias)
        act.wait(dve, d_bias)

        iters = []
        for qt in range(NQT):
            for hl in range(HPC):
                for kc in range(qt + 1):
                    diag = (kc == qt)
                    for ksub in range(4):
                        iters.append(dict(
                            qt=qt, hl=hl, kt=kc * 4 + ksub,
                            qoff=ksub * 128 if diag else 0,
                            diag=diag,
                            first=(kc == 0 and ksub == 0),
                            first3=(kc == 0 and ksub == 3),
                            lastmm=(diag and ksub == 3)))

        pt_free = [0, 0, 0, 0]
        pt_dve_free = [0, 0, 0, 0]
        rsc_free = [0]
        st_free = [0, 0, 0]
        pv_free = [0, 0]
        rs_free = [0, 0]
        rcp_free = [0, 0]
        rb_free = 0
        exp_of = {}

        def emit_st(i):
            it = iters[i]
            sl = i % 3
            psl = i % 4
            stp = st_ps[sl]
            qb = it["hl"] * S + it["qt"] * 512
            if not h0a_waited[0]:
                s_h0a.waited_by(pe)
                s_h0a.waited_by(act)
                h0a_waited[0] = True
            if not h_waited[it["hl"]] and (it["qt"] * 512 + 512 > FC
                                           or (it["kt"] + 1) * 128 > FC):
                s_h[it["hl"]].waited_by(pe)
                s_h[it["hl"]].waited_by(act)
                h_waited[it["hl"]] = True
            if not h2_waited[it["hl"]] and (it["qt"] * 512 + 512 > HS
                                            or (it["kt"] + 1) * 128 > HS):
                s_h2[it["hl"]].waited_by(pe)
                s_h2[it["hl"]].waited_by(act)
                h2_waited[it["hl"]] = True
            pe.wait(act, st_free[sl])
            if it["first"]:
                pe.wait(dve, pv_free[it["hl"]])
                pe.wait(dve, rs_free[it["hl"]])
            pe.put(nc.tensor.matmul(
                stp[:, it["qoff"]:512],
                ks[:, it["hl"] * S + it["kt"] * 128:
                   it["hl"] * S + (it["kt"] + 1) * 128],
                qs[:, qb + it["qoff"]: qb + 512], start=True, stop=True))
            p_st = pe.sig()
            if it["diag"]:
                dve.wait(pe, p_st)
                dsl = stp[:, it["qoff"]:it["qoff"] + 128]
                dve.put(nc.vector.tensor_tensor(dsl, dsl, tri, ALU.add))
                d_msk = dve.sig()
                act.wait(dve, d_msk)
            act.wait(pe, p_st)
            act.wait(pe, pt_free[psl])
            act.wait(dve, pt_dve_free[psl])
            act.put(nc.scalar.activation(
                pt[psl][:, it["qoff"]:512], stp[:, it["qoff"]:512], AF.Exp,
                bias=bias_sb[:, it["hl"]:it["hl"] + 1], scale=INVSQ))
            st_free[sl] = exp_of[i] = act.sig()

        def emit_pv(i):
            it = iters[i]
            psl = i % 4
            hl = it["hl"]
            pe.wait(act, exp_of[i])
            nc.tensor.matmul(
                pv_ps[hl][:, it["qoff"]:512],
                vs[:, (hl * (S // 128) + it["kt"]) * DH:
                   (hl * (S // 128) + it["kt"] + 1) * DH],
                pt[psl][:, it["qoff"]:512],
                start=it["first"], stop=it["lastmm"])
            last = nc.tensor.matmul(
                rs_ps[hl][:, it["qoff"]:512], onesb[:, 0:1],
                pt[psl][:, it["qoff"]:512],
                start=it["first"], stop=it["lastmm"])
            pe.put(last)
            pt_free[psl] = pe.sig()

        def emit_fin_a(i):
            it = iters[i]
            hl = it["hl"]
            p_pv = pe.n
            dve.wait(pe, p_pv)
            dve.wait(pe, rcp_free[hl])
            dve.put(nc.vector.reciprocal(rcp[hl], rs_ps[hl]))
            return dve.sig()

        def emit_fin_b(i, d_rcp):
            nonlocal rb_free
            it = iters[i]
            hl = it["hl"]
            qt = it["qt"]
            pe.wait(dve, d_rcp)
            pe.wait(dve, rb_free)
            pe.put(nc.tensor.matmul(rb_ps[:, :], onesf[0:1, :], rcp[hl],
                                    start=True, stop=True))
            rcp_free[hl] = p_rb = pe.sig()
            dve.wait(pe, p_rb)
            dve.put(nc.vector.tensor_copy(rbs, rb_ps[:, :]))
            rb_free = dve.sig()
            nc.vector.drain()
            asl = ao_slot[0]
            ao_slot[0] ^= 1
            s_aos2[asl].waited_by(dve)
            dve.put(nc.vector.tensor_tensor(aost[asl], pv_ps[hl][:, :], rbs,
                                            ALU.mult))
            pv_free[hl] = rs_free[hl] = d_ao = dve.sig()
            sp.wait(dve, d_ao)
            s_aos2[asl].dma(sp, ao_o[hl][:, qt * 512:(qt + 1) * 512], aost[asl])

        pending = []
        emit_st(0)
        emit_st(1)
        for i in range(len(iters)):
            while pending and pending[0][0] <= i:
                _, bi, d_rcp = pending.pop(0)
                emit_fin_b(bi, d_rcp)
            if i + 2 < len(iters):
                emit_st(i + 2)
            emit_pv(i)
            if iters[i]["lastmm"]:
                d_rcp = emit_fin_a(i)
                pending.append((i + 3, i, d_rcp))
        while pending:
            _, bi, d_rcp = pending.pop(0)
            emit_fin_b(bi, d_rcp)

        s_aos2[0].waited_by(sp)
        s_aos2[1].waited_by(sp)
        s_aos2[0].waited_by(gp)
        s_aos2[1].waited_by(gp)
    return nc


def build_l3():
    nc = bass.Bass("TRN2", target_bir_lowering=False)
    x1_in = nc.dram_tensor("x1", [NTT, 128, D], F32, kind="ExternalInput").ap()
    ao_in = nc.dram_tensor("aof", [NDC, 128, TPC], BF16, kind="ExternalInput").ap()
    wo_d = nc.dram_tensor("wo", [NDC, 128, D], F32, kind="ExternalInput").ap()
    ax_d = nc.dram_tensor("ao_a", [NDC, 128, NA, RK], F32, kind="ExternalInput").ap()
    bx_d = nc.dram_tensor("ao_b", [NA, RK, D], F32, kind="ExternalInput").ap()
    wg_d = nc.dram_tensor("wg", [NHC, 128, D], F32, kind="ExternalInput").ap()
    wu_d = nc.dram_tensor("wu", [NHC, 128, D], F32, kind="ExternalInput").ap()
    wd_d = nc.dram_tensor("wd", [NHC, 128, D], F32, kind="ExternalInput").ap()
    st_d = nc.dram_tensor("starts", [1, 4], I32, kind="ExternalInput").ap()
    io_d = nc.dram_tensor("iota", [1, 4], F32, kind="ExternalInput").ap()
    id_d = nc.dram_tensor("ident", [128, 128], BF16, kind="ExternalInput").ap()
    y_o = nc.dram_tensor("y", [NTT, 128, D], F32, kind="ExternalOutput").ap()

    with ExitStack() as ctx:
        SB = lambda nm, sh, dt: ctx.enter_context(nc.sbuf_tensor("sb_" + nm, sh, dt)).ap()
        resid = SB("resid", [128, NTT * D], F32)
        scr = SB("scr", [128, D], BF16)
        lnout = [SB(f"lnout{b}", [128, D], BF16) for b in range(2)]
        stats = SB("stats", [128, 8 * NTT], F32)
        eps_t = SB("eps", [128, 1], F32)
        ident = SB("ident", [128, 128], BF16)
        starts_sb = SB("starts", [1, 16], F32)
        iota_sb = SB("iotas", [1, 4], F32)
        ohb = SB("ohb", [128, 8], F32)
        ones1 = SB("ones1", [1, 128], F32)
        asel = SB("asel", [128, NDC * 3 * RK], BF16)
        bsel = SB("bsel", [16, D], BF16)
        low = SB("low", [16, TPC], BF16)
        banks = [ctx.enter_context(nc.psum_tensor(f"bank{i}", [128, 512], F32)).ap()
                 for i in range(8)]
        E = mk_engs(nc, ctx)
        pe, act, dve, sp, gp = E

        MS = lambda nm: Stream(nc, ctx, nm)
        s_ld, s_ss, s_sa, s_bt, s_so = MS("ld"), MS("ss"), MS("sa"), MS("bt"), MS("so")
        s_wa = MS("wa")
        s_aos = MS("aos")
        ST = {k: MS(k) for k in
              ["wg0", "wg1", "wu0", "wu1"] + [f"wd{i}" for i in range(NG)]}
        c2 = ctx.enter_context(ExitStack())
        SB2 = lambda nm, sh, dt: c2.enter_context(nc.sbuf_tensor("sb_" + nm, sh, dt)).ap()
        aos = SB2("aos", [128, NDC * TPC], BF16)
        s_aos.dma(sp, aos.rearrange("p (c t) -> p c t", t=TPC),
                  ao_in.transpose([1, 0, 2]))
        for tt in range(NTT):
            s_ld.dma(sp, resid[:, tt * D:(tt + 1) * D], x1_in[tt])
        s_ld.dma(sp, ident, id_d)
        s_ld.dma(sp, iota_sb, io_d)
        s_ss.dma(gp, starts_sb[0:1, 0:4], st_d)
        dve.put(nc.vector.memset(eps_t, 1e-5))
        dve.put(nc.vector.memset(ones1, 1.0))
        dve.sig()
        s_ld.waited_by(act)
        s_ld.waited_by(dve)

        if True:

            wall = [SB2(f"wall{i}", [128, D], BF16) for i in range(NDC)]
            s_was = [Stream(nc, ctx, f"wa{i}") for i in range(NDC)]
            for dc in range(NDC):
                s_was[dc].dma(gp, wall[dc], wo_d[dc], max_dma_last_dim=8192)
            a_sb = SB2("asb", [128, NDC * NA * RK], BF16)
            b_all = SB2("ball", [16, NA * D], BF16)
            s_sa.dma(gp, a_sb.rearrange("p (f) -> p f"), ax_d.transpose([1, 0, 2, 3]))
            s_sa.dma(gp, b_all.rearrange("p (a d) -> p a d", d=D),
                     bx_d.transpose([1, 0, 2]))
            emit_onehot(nc, E, starts_sb, iota_sb, ohb, ones1, banks[7], s_ld,
                        s_ss)
            s_sa.waited_by(dve)
            slA = asel.rearrange("p (c t r) -> p c t r", t=3, r=RK)[:, :, 0, :]
            emit_ab_select(nc, E, ohb, [(a_sb, slA)], [(b_all, bsel)])
            gp.e.wait_ge(dve.sem, dve.n)

            s_aos.waited_by(pe)
            pe.wait(dve, dve.n)
            lo_ps = banks[2][0:16, :]
            for tc in range(2):
                pe.wait(act, act.n)
                last = None
                for dc in range(NDC):
                    lh = asel.rearrange("p (c t r) -> p c t r", t=3, r=RK)[:, dc, 0, :]
                    last = nc.tensor.matmul(
                        lo_ps, lh,
                        aos[:, dc * TPC + tc * 512: dc * TPC + tc * 512 + 512],
                        start=(dc == 0), stop=(dc == NDC - 1))
                pe.put(last)
                p_lo = pe.sig()
                act.wait(pe, p_lo)
                act.put(nc.scalar.copy(low[0:16, tc * 512: tc * 512 + 512], lo_ps))
                act.sig()
            low_ready = act.n
            pe.wait(act, low_ready)

            o_done = [0] * 4
            resid_ts_counts = [0] * NTT
            wa_waited = 0
            low_waited = False
            for ts in range(NTT):
                for oc in range(4):
                    pe.wait(dve, o_done[oc])
                    for dc in range(NDC):
                        if wa_waited <= dc:
                            wa_waited = dc + 1
                            s_was[dc].waited_by(pe)
                        nc.tensor.matmul(
                            banks[oc][:, :],
                            aos[:, dc * TPC + ts * 128: dc * TPC + (ts + 1) * 128],
                            wall[dc][:, oc * 512:(oc + 1) * 512],
                            start=(dc == 0), stop=False)
                    if not low_waited:
                        pe.wait(act, low_ready)
                        low_waited = True
                    last = nc.tensor.matmul(
                        banks[oc][:, :], low[0:16, ts * 128:(ts + 1) * 128],
                        bsel[:, oc * 512:(oc + 1) * 512], start=False, stop=True)
                    pe.put(last)
                    p_o = pe.sig()
                    dve.wait(pe, p_o)
                    rsl = resid[:, ts * D + oc * 512: ts * D + oc * 512 + 512]
                    dve.put(nc.vector.tensor_tensor(rsl, rsl, banks[oc][:, :],
                                                    ALU.add))
                    o_done[oc] = dve.sig()
                resid_ts_counts[ts] = dve.n
            oproj_done = dve.n
            gp.e.wait_ge(dve.sem, oproj_done)
            sp.e.wait_ge(dve.sem, oproj_done)
        ctx.pop_all().close() if False else None
        c2.close()

        with ExitStack() as c3:
            SB3 = lambda nm, sh, dt: c3.enter_context(nc.sbuf_tensor("sb_" + nm, sh, dt)).ap()
            xt = SB3("xt", [128, NDC * TPC], BF16)
            xt_tts = emit_layernorm(nc, E, resid, xt, scr, stats, eps_t, ident,
                                    banks[0:2], lnout, resid_ts_counts)
            gated = SB3("gated", [128, NG * TPC], BF16)
            wg_sb = [SB3(f"wg{b}", [128, D], BF16) for b in range(2)]
            wu_sb = [SB3(f"wu{b}", [128, D], BF16) for b in range(2)]
            wd_sb = [SB3(f"wd{b}", [128, D], BF16) for b in range(NG)]
            def _store_y(ts, dcount):
                sp.wait(dve, dcount)
                s_so.dma(sp, y_o[ts], resid[:, ts * D:(ts + 1) * D])
            ffn_done = emit_ffn(nc, E, ST, xt, resid, gated, wg_sb, wu_sb, wd_sb,
                                [banks[4], banks[5]], [banks[6], banks[7]],
                                banks[0:4], {"wg": wg_d, "wu": wu_d, "wd": wd_d},
                                xt_tts, store_cb=_store_y)
            gp.e.wait_ge(dve.sem, ffn_done)
            s_so.waited_by(sp)
            s_so.waited_by(gp)
    return nc


def _rope_tables():
    pos = np.arange(S, dtype=np.float64)
    j = np.arange(64, dtype=np.float64)
    inv = 10000.0 ** (-2.0 * j / DH)
    ang = pos[:, None] * inv[None, :]
    cos4 = np.tile(np.cos(ang).astype(np.float32), (1, 4))
    sin4 = np.tile(np.sin(ang).astype(np.float32), (1, 4))
    return cos4, sin4


def _tri_mask():
    m = np.zeros((128, 128), np.float32)
    kk, qq = np.meshgrid(np.arange(128), np.arange(128), indexing="ij")
    m[qq < kk] = -1e9
    return m


def _stage_a(A):
    return np.ascontiguousarray(A.transpose(2, 0, 1).reshape(NDC, 128, NA, RK))


def _stage_b(B):
    return np.ascontiguousarray(B.transpose(0, 2, 1))


def _stage_wT(W):
    return np.ascontiguousarray(W.T.reshape(NDC, 128, W.shape[0]))


def _stage_w_colblocks(W):
    return np.ascontiguousarray(W.T.reshape(NDC, 128, 4, 512).transpose(0, 2, 1, 3))


def _stage_wgu(W):
    return np.ascontiguousarray(
        W.reshape(NHC, 128, NDC, 128).transpose(0, 3, 2, 1).reshape(NHC, 128, D))


def _stage_wd(W):
    return np.ascontiguousarray(W.T.reshape(NHC, 128, D))


_IDENT_BF = np.eye(128, dtype=BF)
_IDENT_F32 = np.eye(128, dtype=np.float32)
_IOTA = np.arange(4, dtype=np.float32).reshape(1, 4)
_CACHE = {}


def _get_ncs():
    if "l1" not in _CACHE:
        _CACHE["l1"] = build_l1()
        _CACHE["l2"] = build_l2()
        _CACHE["l3"] = build_l3()
    return _CACHE["l1"], _CACHE["l2"], _CACHE["l3"]


def kernel(L0_attn_x, L0_attn_lora_start_indices, L0_attn_Wq, L0_attn_Aq,
           L0_attn_Bq, L0_attn_Wk, L0_attn_Ak, L0_attn_Bk, L0_attn_Wv,
           L0_attn_Av, L0_attn_Bv, L0_attn_Wo, L0_attn_Ao, L0_attn_Bo,
           L0_ffn_w_gate, L0_ffn_w_up, L0_ffn_w_down):
    nc1, nc2, nc3 = _get_ncs()
    f32 = lambda a: np.ascontiguousarray(np.asarray(a, dtype=np.float32))
    x = f32(L0_attn_x).reshape(S, D)
    starts = np.ascontiguousarray(np.asarray(L0_attn_lora_start_indices,
                                             dtype=np.int32)).reshape(1, 4)

    wg = _stage_wgu(f32(L0_ffn_w_gate))
    wu = _stage_wgu(f32(L0_ffn_w_up))
    wd = _stage_wd(f32(L0_ffn_w_down))
    wq = _stage_w_colblocks(f32(L0_attn_Wq))
    wk = _stage_w_colblocks(f32(L0_attn_Wk))
    wv = _stage_w_colblocks(f32(L0_attn_Wv))
    wo = _stage_wT(f32(L0_attn_Wo))
    aq, ak, av, ao_a = (_stage_a(f32(a)) for a in
                        (L0_attn_Aq, L0_attn_Ak, L0_attn_Av, L0_attn_Ao))
    bq, bk, bv, bo_b = (_stage_b(f32(b)) for b in
                        (L0_attn_Bq, L0_attn_Bk, L0_attn_Bv, L0_attn_Bo))
    cos4, sin4 = _rope_tables()

    in1 = []
    for c in range(NCORE):
        tok = slice(c * TPC, (c + 1) * TPC)
        in1.append(dict(
            x=np.ascontiguousarray(x[tok].reshape(NTT, 128, D)),
            wg=wg, wu=wu, wd=wd, wq=wq, wk=wk, wv=wv,
            aq=aq, ak=ak, av=av, bq=bq, bk=bk, bv=bv,
            cosr=np.ascontiguousarray(cos4[tok].reshape(NTT, 128, 256)),
            sinr=np.ascontiguousarray(sin4[tok].reshape(NTT, 128, 256)),
            starts=starts, iota=_IOTA, ident=_IDENT_BF, identf=_IDENT_F32))
    r1 = run_bass_kernel_spmd(nc1, in1, core_ids=list(range(NCORE)))
    x1 = np.stack([r1.results[c]["x1"].reshape(TPC, D) for c in range(NCORE)])
    qo = np.stack([r1.results[c]["qo"] for c in range(NCORE)])
    ko = np.stack([r1.results[c]["ko"] for c in range(NCORE)])
    vo = np.stack([r1.results[c]["vo"] for c in range(NCORE)])
    qn = np.stack([r1.results[c]["qn"].reshape(H) for c in range(NCORE)])
    kn = np.stack([r1.results[c]["kn"].reshape(H) for c in range(NCORE)])

    q_full = np.ascontiguousarray(qo.transpose(1, 0, 2, 3).reshape(H, S, DH))
    k_full = np.ascontiguousarray(ko.transpose(1, 0, 2, 3).reshape(H, S, DH))
    v_full = np.ascontiguousarray(vo.transpose(1, 0, 2, 3).reshape(H, S, DH))
    tri = _tri_mask()
    onesf = np.ones((1, 128), np.float32)
    onesb = np.ones((128, 1), BF)
    in2 = []
    for c in range(NCORE):
        hs = slice(HPC * c, HPC * (c + 1))
        in2.append(dict(
            qh=np.ascontiguousarray(q_full[hs].transpose(0, 2, 1)),
            kh=np.ascontiguousarray(k_full[hs].transpose(0, 2, 1)),
            vh=np.ascontiguousarray(v_full[hs].reshape(HPC, S // 128, 128, DH)
                                    .transpose(2, 0, 1, 3)),
            qn=np.ascontiguousarray(qn.T[hs].reshape(1, HPC, NCORE)),
            kn=np.ascontiguousarray(kn.T[hs].reshape(1, HPC, NCORE)),
            tri=tri, onesf=onesf, onesb=onesb))
    r2 = run_bass_kernel_spmd(nc2, in2, core_ids=list(range(NCORE)))
    ao = np.stack([r2.results[c]["ao"] for c in range(NCORE)])
    ao_full = ao.reshape(H, DH, S)

    in3 = []
    for c in range(NCORE):
        tok = slice(c * TPC, (c + 1) * TPC)
        in3.append(dict(
            x1=np.ascontiguousarray(x1[c].reshape(NTT, 128, D)),
            aof=np.ascontiguousarray(ao_full[:, :, tok]),
            wo=wo, ao_a=ao_a, ao_b=bo_b, wg=wg, wu=wu, wd=wd,
            starts=starts, iota=_IOTA, ident=_IDENT_BF))
    r3 = run_bass_kernel_spmd(nc3, in3, core_ids=list(range(NCORE)))
    y = np.concatenate([r3.results[c]["y"].reshape(TPC, D) for c in range(NCORE)])
    return y.reshape(1, S, D)
